# version 05
# baseline (speedup 1.0000x reference)
import contextlib
import numpy as np
import ml_dtypes

import concourse.bass as bass
import concourse.mybir as mybir
from concourse.bass_utils import run_bass_kernel_spmd

F32 = mybir.dt.float32
BF16 = mybir.dt.bfloat16
AF = mybir.ActivationFunctionType
ALU = mybir.AluOpType
AX = mybir.AxisListType
BF = ml_dtypes.bfloat16

NCORE = 8
S = 8192
D = 2048
TOK = S // NCORE
LN_EPS = 1e-5



class InstProxy:

    def __init__(self, inst, prog, eng_name):
        self._i = inst
        self._p = prog
        self._e = eng_name

    def then_inc(self, sem, n=1):
        ep = self._p._eng_epoch[self._e]
        self._p.counts[(ep, sem.name)] = \
            self._p.counts.get((ep, sem.name), 0) + n
        self._i.then_inc(sem, n)
        return self

    def __getattr__(self, k):
        return getattr(self._i, k)


class Eng:

    def __init__(self, prog, name, handle):
        self.p = prog
        self.name = name
        self.e = handle
        self._sem = prog.sems["self_" + name]

    def wait_ge(self, sem, v):
        base = 0
        if sem.name.startswith("pool"):
            base = self.p.bases.get(
                (self.p._eng_epoch[self.name], sem.name), 0)
        return self.e.wait_ge(sem, base + v)

    def __getattr__(self, k):
        a = getattr(self.e, k)
        if not callable(a):
            return a
        p, nm = self.p, self.name

        def wrapped(*args, **kw):
            r = a(*args, **kw)
            if type(r).__name__ == "BassInstruction":
                return InstProxy(r, p, nm)
            return r
        return wrapped

    def chain(self, inst):
        self.p.selfcnt[self.name] += 1
        c = self.p.selfcnt[self.name]
        inst.then_inc(self._sem, 1)
        self.e.wait_ge(self._sem, c)
        return inst

    def chain_inc(self, inst, sem, n=1):
        self.chain(inst)
        self.nop().then_inc(sem, n)
        return inst


class Ring:

    def __init__(self, p, n=4, per_slot=None):
        self.sems = [p.sem() for _ in range(n)]
        self.R = n
        self.per_slot = per_slot
        self.npush = 0
        self.nslotpush = {}

    def push(self, inst):
        i = self.npush % self.R
        self.npush += 1
        inst.then_inc(self.sems[i], 16)
        return inst

    def wait(self, e, j):
        i = j % self.R
        e.wait_ge(self.sems[i], 16 * (j // self.R + 1))

    def wait_group(self, e, hi=None):
        if hi is None:
            hi = self.npush - 1
        n = hi + 1
        for i in range(self.R):
            cnt_i = (n - i + self.R - 1) // self.R
            if cnt_i > 0:
                e.wait_ge(self.sems[i], 16 * cnt_i)

    def push_slot(self, inst, slot):
        assert self.per_slot is not None
        i = slot % self.R
        self.nslotpush[slot] = self.nslotpush.get(slot, 0) + 1
        assert self.nslotpush[slot] <= self.per_slot
        inst.then_inc(self.sems[i], 16)
        return inst

    def wait_slot(self, e, slot):
        i = slot % self.R
        e.wait_ge(self.sems[i], 16 * self.per_slot * (slot // self.R + 1))


class Prog:
    ENG = ("sync", "gpsimd", "scalar", "vector", "tensor")

    def __init__(self, name="prog", bases=None):
        self.nc = bass.Bass("TRN2", target_bir_lowering=False, debug=False)
        self.ctx = contextlib.ExitStack()
        self.work = {e: [] for e in self.ENG}
        self.sems = {}
        self.selfcnt = {e: 0 for e in self.ENG}
        self.nbar = 0
        self._uniq = 0
        self._astack = []
        self._pool = []
        self._cursor = 0
        self.counts = {}
        self.bases = bases if bases is not None else {}
        self._eng_epoch = {e: 0 for e in self.ENG}
        for e in self.ENG:
            self.sem("self_" + e)

    def uniq(self, prefix):
        self._uniq += 1
        return f"{prefix}_{self._uniq}"

    def sem(self, name=None, persistent=False):
        if name is None and not persistent:
            if self._cursor < len(self._pool):
                s = self._pool[self._cursor]
            else:
                s = self.ctx.enter_context(
                    self.nc.semaphore(self.uniq("pool")))
                self._pool.append(s)
            self._cursor += 1
            return s
        if name is None:
            name = self.uniq("sem")
        if name not in self.sems:
            self.sems[name] = self.ctx.enter_context(self.nc.semaphore(name))
        return self.sems[name]

    def ring(self, n=4, per_slot=None):
        return Ring(self, n, per_slot)

    def on(self, eng):
        def deco(f):
            self.work[eng].append(f)
            return f
        return deco

    def sb(self, name, shape, dt, freeable=False):
        cm = self.nc.sbuf_tensor(self.uniq(name), shape, dt)
        h = cm.__enter__()
        self._astack.append(cm)
        return h

    def ps(self, name, shape, dt, freeable=False):
        cm = self.nc.psum_tensor(self.uniq(name), shape, dt)
        h = cm.__enter__()
        self._astack.append(cm)
        return h

    def scope(self):
        return len(self._astack)

    def pop_scope(self, mark):
        while len(self._astack) > mark:
            self._astack.pop().__exit__(None, None, None)

    def din(self, name, shape, dt):
        return self.nc.dram_tensor(name, list(shape), dt, kind="ExternalInput").ap()

    def dout(self, name, shape, dt):
        return self.nc.dram_tensor(name, list(shape), dt, kind="ExternalOutput").ap()

    def dtmp(self, name, shape, dt):
        return self.nc.dram_tensor(self.uniq(name), list(shape), dt,
                                   kind="Internal").ap()

    def barrier(self, reset=False):
        self.nbar += 1
        base = 5 * (self.nbar - 1)
        bar = self.sem("bar", persistent=True)
        for e in self.ENG:
            @self.on(e)
            def _(eng, base=base, bar=bar, e_name=e, reset=reset):
                eng.nop().then_inc(bar, 1)
                eng.wait_ge(bar, base + 5)
                if reset:
                    self._eng_epoch[e_name] += 1
        if reset:
            self._cursor = 0

    def finalize(self):
        nc = self.nc
        with nc.Block() as block:
            for name in self.ENG:
                def runner(e, name=name):
                    w = Eng(self, name, e)
                    for f in self.work[name]:
                        f(w)
                getattr(block, name)(runner)
        self.pop_scope(0)
        self.ctx.close()
        return nc



def host_constants():
    dk = 128
    inv_freq = 1.0 / (10000.0 ** (np.arange(0, dk, 2, dtype=np.float64) / dk))
    pos = np.arange(S, dtype=np.float64)
    fr = pos[:, None] * inv_freq[None, :]
    ang = np.concatenate([fr, fr], axis=1)
    cos = np.cos(ang).T.astype(np.float32)
    sin = np.sin(ang).T.astype(np.float32)
    alpha = np.float32(1.0 / np.sqrt(128.0))
    perm = np.zeros((128, 128), dtype=np.float32)
    for m in range(64):
        perm[m + 64, m] = -1.0
    for m in range(64, 128):
        perm[m - 64, m] = 1.0
    return {
        "cosq": np.ascontiguousarray(cos * alpha),
        "sinq": np.ascontiguousarray(sin * alpha),
        "cosk": np.ascontiguousarray(cos),
        "sink": np.ascontiguousarray(sin),
        "permsign": perm,
        "ident": np.eye(128, dtype=BF),
        "ident_f32": np.eye(128, dtype=np.float32),
        "ones_row_bf": np.ones((1, 128), dtype=BF),
        "ones_col_bf": np.ones((128, 1), dtype=BF),
        "ones_row_f32": np.ones((1, 128), dtype=np.float32),
        "ones_col_f32": np.ones((128, 1), dtype=np.float32),
    }


CONST_SHAPES = {
    "ident": ([128, 128], BF16), "ident_f32": ([128, 128], F32),
    "ones_row_bf": ([1, 128], BF16),
    "ones_col_bf": ([128, 1], BF16), "ones_row_f32": ([1, 128], F32),
    "ones_col_f32": ([128, 1], F32), "permsign": ([128, 128], F32),
}


def load_consts(p, which):
    ring = Ring(p, 1)
    ring.sems = [p.sem(persistent=True)]
    tiles = {}
    dins = {}
    for w in which:
        shp, dt = CONST_SHAPES[w]
        dins[w] = p.din(w, shp, dt)
        tiles[w] = p.sb("c_" + w, shp, dt)

    @p.on("sync")
    def _(e):
        for w in which:
            ring.push(e.dma_start(tiles[w][:], dins[w]))
    return tiles, ring



def phase_ln(p, x_in, xn_out, T=TOK):
    nt = T // 128
    rld = p.ring(4)
    rst = p.ring(4)
    s2d = p.sem()
    vep = p.sem()
    rsd = p.sem()
    nbd = p.sem()
    nrm = p.sem()

    xt = [p.sb("ln_x", [128, D], F32) for _ in range(2)]
    xnt = [p.sb("ln_o", [128, D], BF16) for _ in range(2)]
    scr_v = p.sb("ln_scrv", [128, D], F32)
    scr_s = p.sb("ln_scrs", [128, D], F32)
    s1 = [p.sb("ln_s1", [128, 1], F32) for _ in range(2)]
    s2 = [p.sb("ln_s2", [128, 1], F32) for _ in range(2)]
    mean = p.sb("ln_mean", [128, 1], F32)
    ex2 = p.sb("ln_ex2", [128, 1], F32)
    varep = p.sb("ln_var", [128, 1], F32)
    rstd = p.sb("ln_rstd", [128, 1], F32)
    nbias = p.sb("ln_nb", [128, 1], F32)

    @p.on("sync")
    def _(e):
        for i in range(nt):
            if i >= 2:
                e.wait_ge(nrm, i - 1)
            rld.push(e.dma_start(xt[i % 2][:], x_in[i * 128:(i + 1) * 128, :]))

    @p.on("vector")
    def _(e):
        for i in range(nt):
            b = i % 2
            rld.wait(e, i)
            e.chain(e.tensor_scalar(scr_v[:], xt[b][:], 0.0, None,
                                    ALU.add, ALU.add, accum_out=s1[b][:]))
            e.wait_ge(s2d, i + 1)
            e.chain(e.tensor_scalar(mean[:], s1[b][:], 1.0 / D, None,
                                    ALU.mult, ALU.bypass))
            e.chain(e.tensor_scalar(ex2[:], s2[b][:], 1.0 / D, None,
                                    ALU.mult, ALU.bypass))
            e.chain(e.scalar_tensor_tensor(varep[:], mean[:], -1.0, mean[:],
                                           ALU.mult, ALU.mult))
            e.chain(e.tensor_add(varep[:], varep[:], ex2[:]))
            e.chain_inc(e.tensor_scalar(varep[:], varep[:], LN_EPS, None,
                                        ALU.add, ALU.bypass), vep)
            e.wait_ge(rsd, i + 1)
            e.chain_inc(e.scalar_tensor_tensor(nbias[:], rstd[:], -1.0,
                                               mean[:], ALU.mult, ALU.mult), nbd)

    @p.on("scalar")
    def _(e):
        for i in range(nt):
            b = i % 2
            rld.wait(e, i)
            e.activation(scr_s[:], xt[b][:], AF.Square,
                         accum_out=s2[b][:]).then_inc(s2d, 1)
            e.wait_ge(vep, i + 1)
            e.chain(e.activation(rstd[:], varep[:], AF.Ln))
            e.chain_inc(e.activation(rstd[:], rstd[:], AF.Exp,
                                     scale=-0.5), rsd)
            e.wait_ge(nbd, i + 1)
            if i >= 2:
                rst.wait(e, i - 2)
            e.activation(xnt[b][:], xt[b][:], AF.Identity, bias=nbias[:],
                         scale=rstd[:]).then_inc(nrm, 1)

    @p.on("gpsimd")
    def _(e):
        for i in range(nt):
            e.wait_ge(nrm, i + 1)
            rst.push(e.dma_start(xn_out[i * 128:(i + 1) * 128, :],
                                 xnt[i % 2][:]))
        rst.wait_group(e)



def phase_qkv(p, xn_own, w_attn, tabs, consts, cring,
              q_out, k_out, v_out, qonly=False):
    NF = D // 128
    QKC = 16 if qonly else 20
    TH = TOK // 512
    perm = consts["permsign"]

    rxn = p.ring(1)
    rtab = p.ring(1)
    rw = p.ring(4)
    rwv = p.ring(1)
    rstq = p.ring(4)
    rstv = p.ring(4)
    ped = p.sem()
    rotd = p.sem()
    qrd = p.sem()
    rpd = p.sem()
    pvd = p.sem()
    vevd = p.sem()

    xn_dm = p.sb("qkv_xndm", [128, NF * TOK], BF16)
    wtile = [p.sb("qkv_w", [128, NF * 128], BF16) for _ in range(2)]
    wv_sb = p.sb("qkv_wv", [128, NF * 2048], BF16)
    cq = p.sb("qkv_cq", [128, TOK], F32)
    sq = p.sb("qkv_sq", [128, TOK], F32)
    ck = p.sb("qkv_ck", [128, TOK], F32)
    sk = p.sb("qkv_sk", [128, TOK], F32)
    qraw = [p.sb("qkv_qraw", [128, 512], F32) for _ in range(2)]
    t1 = p.sb("qkv_t1", [128, 512], F32)
    t2 = p.sb("qkv_t2", [128, 512], F32)
    qo = [p.sb("qkv_qo", [128, 512], BF16) for _ in range(2)]
    vo = [p.sb("qkv_vo", [128, 512], BF16) for _ in range(2)]
    pqk = [p.ps("qkv_pqk", [128, 512], F32) for _ in range(2)]
    prot = [p.ps("qkv_prot", [128, 512], F32) for _ in range(2)]
    pv = [p.ps("qkv_pv", [128, 512], F32) for _ in range(2)]

    @p.on("sync")
    def _(e):
        for c in range(NF):
            rxn.push(e.dma_start(xn_dm[:, c * TOK:(c + 1) * TOK],
                                 xn_own[:, c * 128:(c + 1) * 128],
                                 transpose=True))
        for t, dst in (("cosq", cq), ("sinq", sq), ("cosk", ck), ("sink", sk)):
            rtab.push(e.dma_start(dst[:], tabs[t]))
        for c in range(QKC):
            for h in range(TH):
                n = 2 * c + h
                e.wait_ge(rpd, n + 1)
                dst = (q_out[c * 128:(c + 1) * 128, h * 512:(h + 1) * 512]
                       if c < 16 else
                       k_out[(c - 16) * 128:(c - 15) * 128,
                             h * 512:(h + 1) * 512])
                rstq.push(e.dma_start(dst, qo[n % 2][:]))
        if not qonly:
            for tc in range(TOK // 128):
                for vc in range(4):
                    n = 4 * tc + vc
                    e.wait_ge(vevd, n + 1)
                    rstv.push(e.dma_start(
                        v_out[tc * 128:(tc + 1) * 128,
                              vc * 512:(vc + 1) * 512],
                        vo[n % 2][:]))
        rstq.wait_group(e)
        rstv.wait_group(e)

    @p.on("gpsimd")
    def _(e):
        for c in range(QKC):
            if c >= 2:
                e.wait_ge(ped, 2 * (c - 1))
            rw.push(e.dma_start(
                wtile[c % 2][:].rearrange("p (a c) -> p a c", a=NF),
                w_attn[:, c * 128:(c + 1) * 128]
                .rearrange("(a p) c -> p a c", p=128)))
        if not qonly:
            wv_view = wv_sb[:].rearrange("p (a c) -> p a c", a=NF)
            for vc in range(4):
                rwv.push(e.dma_start(
                    wv_view[:, :, vc * 512:(vc + 1) * 512],
                    w_attn[:, 2560 + vc * 512:2560 + (vc + 1) * 512]
                    .rearrange("(a p) c -> p a c", p=128)))

    @p.on("tensor")
    def _(e):
        rxn.wait_group(e)
        cring.wait_group(e)
        for c in range(QKC):
            rw.wait(e, c)
            for h in range(TH):
                n = 2 * c + h
                if n >= 2:
                    e.wait_ge(qrd, n - 1)
                for kc in range(NF):
                    inst = e.matmul(
                        pqk[n % 2][:],
                        wtile[c % 2][:, kc * 128:(kc + 1) * 128],
                        xn_dm[:, kc * TOK + h * 512:kc * TOK + h * 512 + 512],
                        start=(kc == 0), stop=(kc == NF - 1))
                    if kc == NF - 1:
                        inst.then_inc(ped, 1)
                e.wait_ge(qrd, n + 1)
                if n >= 2:
                    e.wait_ge(rpd, n - 1)
                e.matmul(prot[n % 2][:], perm[:], qraw[n % 2][:],
                         start=True, stop=True).then_inc(rotd, 1)
        if not qonly:
            rwv.wait_group(e)
            for tc in range(TOK // 128):
                for vc in range(4):
                    n = 4 * tc + vc
                    if n >= 2:
                        e.wait_ge(vevd, n - 1)
                    for kc in range(NF):
                        inst = e.matmul(
                            pv[n % 2][:],
                            xn_dm[:, kc * TOK + tc * 128:kc * TOK + (tc + 1) * 128],
                            wv_sb[:, kc * 2048 + vc * 512:kc * 2048 + (vc + 1) * 512],
                            start=(kc == 0), stop=(kc == NF - 1))
                        if kc == NF - 1:
                            inst.then_inc(pvd, 1)

    @p.on("scalar")
    def _(e):
        for c in range(QKC):
            for h in range(TH):
                n = 2 * c + h
                e.wait_ge(ped, n + 1)
                if n >= 2:
                    e.wait_ge(rotd, n - 1)
                    e.wait_ge(rpd, n - 1)
                e.activation(qraw[n % 2][:], pqk[n % 2][:], AF.Identity
                             ).then_inc(qrd, 1)
        if not qonly:
            for tc in range(TOK // 128):
                for vc in range(4):
                    n = 4 * tc + vc
                    e.wait_ge(pvd, n + 1)
                    if n >= 2:
                        rstv.wait(e, n - 2)
                    e.activation(vo[n % 2][:], pv[n % 2][:], AF.Identity
                                 ).then_inc(vevd, 1)

    @p.on("vector")
    def _(e):
        rtab.wait_group(e)
        for c in range(QKC):
            ctab = cq if c < 16 else ck
            stab = sq if c < 16 else sk
            for h in range(TH):
                n = 2 * c + h
                e.wait_ge(qrd, n + 1)
                e.chain(e.tensor_mul(t1[:], qraw[n % 2][:],
                                     ctab[:, h * 512:(h + 1) * 512]))
                e.wait_ge(rotd, n + 1)
                e.chain(e.tensor_mul(t2[:], prot[n % 2][:],
                                     stab[:, h * 512:(h + 1) * 512]))
                if n >= 2:
                    rstq.wait(e, n - 2)
                e.chain_inc(e.tensor_add(qo[n % 2][:], t1[:], t2[:]), rpd)



def phase_attn(p, q_in, k_in, v_in, consts, cring, ctx_out):
    NKC = S // 128
    QT = TOK // 512
    NIT = 4 * QT * 4
    ones_col = consts["ones_col_bf"]
    ones_row = consts["ones_row_f32"]

    rq = p.ring(1)
    rk = p.ring(1)
    rv = p.ring(4)
    rst = p.ring(8)
    pq2 = p.sem()
    nqd = p.sem()
    nqe = p.sem()
    vcr = p.sem()
    scr = p.sem()
    bcm = p.sem()
    ngc = p.sem()
    scd = p.sem()
    exd = p.sem()
    pcd = p.sem()
    rsd = p.sem()
    mbc = p.sem()
    bcsd = p.sem()
    ctxe = p.sem()

    q_sb = p.sb("at_q", [128, 16 * TOK], BF16)
    k_sb = p.sb("at_k", [128, 4 * S], BF16)
    v_sb = p.sb("at_v", [128, NKC * 512], BF16)
    q2 = [p.sb("at_q2", [128, 512], BF16) for _ in range(2)]
    nrow = p.sb("at_nrow", [1, 32], F32)
    crow = p.sb("at_crow", [1, 16], F32)
    negC = p.sb("at_negC", [128, 16], F32)
    t1 = p.sb("at_t1", [1, 1], F32)
    pr = [p.sb("at_pr", [128, 512], BF16) for _ in range(2)]
    rsum = p.sb("at_rsum", [1, 512], F32)
    bc_sb = p.sb("at_bc", [128, 512], F32)
    ctx_sb = [p.sb("at_ctx", [128, 4 * 512], BF16) for _ in range(2)]
    ps_sc = [p.ps("at_psc", [128, 512], F32) for _ in range(2)]
    pc = [p.ps("at_pc", [128, 512], F32) for _ in range(4)]
    ps_sum = p.ps("at_psum", [1, 512], F32)
    ps_bc = p.ps("at_pbc", [128, 512], F32)
    ps_nr = [ps_sum, ps_bc]

    NQ = 16 * TOK // 512
    NK = 4 * S // 512

    @p.on("sync")
    def _(e):
        for h in range(16):
            rq.push(e.dma_start(q_sb[:, h * TOK:(h + 1) * TOK],
                                q_in[h * 128:(h + 1) * 128, :]))
        for kv in range(4):
            rk.push(e.dma_start(k_sb[:, kv * S:(kv + 1) * S],
                                k_in[kv * 128:(kv + 1) * 128, :]))
        for kv in range(4):
            if kv >= 1:
                e.wait_ge(ctxe, kv * 8)
            vv = v_in[:, kv * 512:(kv + 1) * 512].rearrange(
                "(a p) c -> p a c", p=128)
            for g8 in range(4):
                rv.push(e.dma_start(
                    v_sb[:].rearrange("p (a c) -> p a c", a=NKC)
                    [:, g8 * 16:(g8 + 1) * 16, :],
                    vv[:, g8 * 16:(g8 + 1) * 16, :]))

    @p.on("gpsimd")
    def _(e):
        for it in range(NIT):
            kv, r = divmod(it, QT * 4)
            qt, qh4 = divmod(r, 4)
            h = kv * 4 + qh4
            e.wait_ge(ctxe, it + 1)
            for dvs in range(4):
                rst.push(e.dma_start(
                    ctx_out[h * 512 + dvs * 128:h * 512 + (dvs + 1) * 128,
                            qt * 512:(qt + 1) * 512],
                    ctx_sb[it % 2][:, dvs * 512:(dvs + 1) * 512]))
        rst.wait_group(e)

    @p.on("scalar")
    def _(e):
        rq.wait_group(e)
        for it in range(NQ + NK):
            if it == NQ:
                rk.wait_group(e)
            if it >= 2:
                e.wait_ge(nqd, it - 1)
            src_ap = (q_sb[:, it * 512:(it + 1) * 512] if it < NQ
                      else k_sb[:, (it - NQ) * 512:(it - NQ + 1) * 512])
            e.activation(q2[it % 2][:], src_ap, AF.Square).then_inc(pq2, 1)
        e.wait_ge(vcr, 1)
        e.chain(e.activation(crow[:], crow[:], AF.Ln))
        e.chain_inc(e.activation(crow[:], crow[:], AF.Exp, scale=0.5), scr)
        e.wait_ge(bcm, 1)
        e.activation(negC[:], ps_bc[:, 0:16], AF.Identity,
                     scale=-1.0).then_inc(ngc, 1)

    @p.on("tensor")
    def _(e):
        cring.wait_group(e)
        for it in range(NQ + NK):
            e.wait_ge(pq2, it + 1)
            if it >= 2:
                e.wait_ge(nqe, it - 1)
            e.matmul(ps_nr[it % 2][0:1, :], ones_col[:], q2[it % 2][:],
                     start=True, stop=True).then_inc(nqd, 1)
        e.wait_ge(scr, 1)
        e.matmul(ps_bc[:, 0:16], ones_row[:], crow[:],
                 start=True, stop=True).then_inc(bcm, 1)

    @p.on("vector")
    def _(e):
        for it in range(NQ + NK):
            e.wait_ge(nqd, it + 1)
            e.chain(e.reduce_max(t1[:], ps_nr[it % 2][0:1, :], axis=AX.X))
            if it < NQ:
                col, first = it // 2, (it % 2 == 0)
            else:
                col = 16 + (it - NQ) // 16
                first = ((it - NQ) % 16 == 0)
            if first:
                e.chain(e.tensor_copy(nrow[:, col:col + 1], t1[:]))
            else:
                e.chain(e.tensor_max(nrow[:, col:col + 1],
                                     nrow[:, col:col + 1], t1[:]))
            e.nop().then_inc(nqe, 1)
        for h in range(16):
            e.chain(e.tensor_mul(crow[:, h:h + 1], nrow[:, h:h + 1],
                                 nrow[:, 16 + h // 4:17 + h // 4]))
        e.nop().then_inc(vcr, 1)

    @p.on("tensor")
    def _(e):
        for kv in range(4):
            rv.wait_group(e, 4 * (kv + 1) - 1)
            for qt in range(QT):
                for qh4 in range(4):
                    it = (kv * QT + qt) * 4 + qh4
                    h = kv * 4 + qh4
                    qs = q_sb[:, h * TOK + qt * 512:h * TOK + (qt + 1) * 512]
                    if it >= 1:
                        e.wait_ge(ctxe, it)
                        e.wait_ge(bcsd, it)
                    for kc in range(NKC):
                        g = it * NKC + kc
                        if g >= 2:
                            e.wait_ge(exd, g - 1)
                        e.matmul(ps_sc[g % 2][:],
                                 k_sb[:, kv * S + kc * 128:kv * S + (kc + 1) * 128],
                                 qs, start=True, stop=True).then_inc(scd, 1)
                        if kc >= 1:
                            e.wait_ge(exd, g)
                            first = (kc - 1 == 0)
                            for dvs in range(4):
                                e.matmul(pc[dvs][:],
                                         v_sb[:, (kc - 1) * 512 + dvs * 128:
                                              (kc - 1) * 512 + (dvs + 1) * 128],
                                         pr[(g - 1) % 2][:],
                                         start=first, stop=False,
                                         skip_group_check=True)
                            e.matmul(ps_sum[:], ones_col[:], pr[(g - 1) % 2][:],
                                     start=first, stop=False,
                                     skip_group_check=True).then_inc(pcd, 1)
                    g = it * NKC + NKC - 1
                    e.wait_ge(exd, g + 1)
                    for dvs in range(4):
                        e.matmul(pc[dvs][:],
                                 v_sb[:, (NKC - 1) * 512 + dvs * 128:
                                      (NKC - 1) * 512 + (dvs + 1) * 128],
                                 pr[g % 2][:],
                                 start=False, stop=True, skip_group_check=True)
                    e.matmul(ps_sum[:], ones_col[:], pr[g % 2][:],
                             start=False, stop=True, skip_group_check=True
                             ).then_inc(pcd, 1)
                    e.wait_ge(rsd, it + 1)
                    e.matmul(ps_bc[:], ones_row[:], rsum[:],
                             start=True, stop=True).then_inc(mbc, 1)

    @p.on("scalar")
    def _(e):
        e.wait_ge(ngc, 1)
        for it in range(NIT):
            kv = it // (QT * 4)
            h = kv * 4 + (it % 4)
            for kc in range(NKC):
                g = it * NKC + kc
                e.wait_ge(scd, g + 1)
                if g >= 2:
                    e.wait_ge(pcd, g - 1)
                e.activation(pr[g % 2][:], ps_sc[g % 2][:], AF.Exp,
                             bias=negC[:, h:h + 1]).then_inc(exd, 1)
            e.wait_ge(mbc, it + 1)
            e.activation(bc_sb[:], ps_bc[:], AF.Identity).then_inc(bcsd, 1)

    @p.on("vector")
    def _(e):
        for it in range(NIT):
            e.wait_ge(pcd, (it + 1) * NKC)
            e.chain(e.reciprocal(rsum[:], ps_sum[:]))
            e.nop().then_inc(rsd, 1)
            e.wait_ge(bcsd, it + 1)
            if it >= 2:
                for dvs in range(4):
                    rst.wait(e, 4 * (it - 2) + dvs)
            for dvs in range(4):
                inst = e.tensor_mul(
                    ctx_sb[it % 2][:, dvs * 512:(dvs + 1) * 512],
                    pc[dvs][:], bc_sb[:])
                if dvs == 3:
                    e.chain_inc(inst, ctxe)
                else:
                    e.chain(inst)



def phase_wout(p, ctx_in, w_out, b_out, consts, cring, s_out):
    NCF = 64
    NTT = TOK // 128
    GRP = 8
    NG = NCF // GRP
    ones_row = consts["ones_row_f32"]

    rb = p.ring(1)
    rc = p.ring(2, per_slot=GRP)
    rw = p.ring(2, per_slot=GRP)
    rst = p.ring(4)
    ped = p.sem()
    eva = p.sem()
    bmm = p.sem()
    bev = p.sem()

    ctxg = [p.sb("wo_ctx", [128, GRP * TOK], BF16) for _ in range(2)]
    wg = [p.sb("wo_w", [128, GRP * 2048], BF16) for _ in range(2)]
    brow = p.sb("wo_brow", [1, 2048], F32)
    bbc = p.sb("wo_bbc", [128, 2048], F32)
    s_sb = p.sb("wo_s", [128, NTT * 2048], F32)
    ps_o = [p.ps("wo_po", [128, 2048], F32) for _ in range(2)]

    @p.on("sync")
    def _(e):
        rb.push(e.dma_start(brow[:], b_out))
        for go in range(NG):
            if go >= 2:
                e.wait_ge(ped, (go - 1) * NTT)
            for ci in range(GRP):
                cf = go * GRP + ci
                rc.push_slot(e.dma_start(
                    ctxg[go % 2][:, ci * TOK:(ci + 1) * TOK],
                    ctx_in[cf * 128:(cf + 1) * 128, :]), go)

    @p.on("gpsimd")
    def _(e):
        for go in range(NG):
            if go >= 2:
                e.wait_ge(ped, (go - 1) * NTT)
            for ci in range(GRP):
                cf = go * GRP + ci
                rw.push_slot(e.dma_start(
                    wg[go % 2][:, ci * 2048:(ci + 1) * 2048],
                    w_out[cf * 128:(cf + 1) * 128, :]), go)
        for tt in range(NTT):
            e.wait_ge(eva, 1 + NG * NTT)
            rst.push(e.dma_start(s_out[tt * 128:(tt + 1) * 128, :],
                                 s_sb[:, tt * 2048:(tt + 1) * 2048]))
        rst.wait_group(e)

    @p.on("tensor")
    def _(e):
        rb.wait_group(e)
        cring.wait_group(e)
        for i in range(4):
            e.matmul(ps_o[0][:, i * 512:(i + 1) * 512], ones_row[:],
                     brow[:, i * 512:(i + 1) * 512],
                     start=True, stop=True).then_inc(bmm, 1)
        e.wait_ge(bev, 4)
        for go in range(NG):
            rc.wait_slot(e, go)
            rw.wait_slot(e, go)
            for tt in range(NTT):
                it = go * NTT + tt
                if it >= 2:
                    e.wait_ge(eva, it)
                for ci in range(GRP):
                    for c4 in range(4):
                        inst = e.matmul(
                            ps_o[it % 2][:, c4 * 512:(c4 + 1) * 512],
                            ctxg[go % 2][:, ci * TOK + tt * 128:
                                          ci * TOK + (tt + 1) * 128],
                            wg[go % 2][:, ci * 2048 + c4 * 512:
                                       ci * 2048 + (c4 + 1) * 512],
                            start=(ci == 0), stop=(ci == GRP - 1),
                            skip_group_check=True)
                        if ci == GRP - 1 and c4 == 3:
                            inst.then_inc(ped, 1)

    @p.on("scalar")
    def _(e):
        for i in range(4):
            e.wait_ge(bmm, i + 1)
            e.activation(bbc[:, i * 512:(i + 1) * 512],
                         ps_o[0][:, i * 512:(i + 1) * 512],
                         AF.Identity).then_inc(bev, 1)

    @p.on("vector")
    def _(e):
        e.wait_ge(bev, 4)
        for tt in range(NTT):
            e.chain(e.tensor_copy(s_sb[:, tt * 2048:(tt + 1) * 2048], bbc[:]))
        e.nop().then_inc(eva, 1)
        for it in range(NG * NTT):
            tt = it % NTT
            e.wait_ge(ped, it + 1)
            for c4 in range(4):
                inst = e.tensor_add(
                    s_sb[:, tt * 2048 + c4 * 512:tt * 2048 + (c4 + 1) * 512],
                    s_sb[:, tt * 2048 + c4 * 512:tt * 2048 + (c4 + 1) * 512],
                    ps_o[it % 2][:, c4 * 512:(c4 + 1) * 512])
                if c4 == 3:
                    e.chain_inc(inst, eva)
                else:
                    e.chain(inst)



def phase_absum(p, w1s, w2s, out):
    ones_col = None
    r1 = p.ring(4)
    red = p.sem()
    fin = p.sem()
    rst = p.ring(1)

    wbuf = [p.sb("ab_w", [128, 8192], F32) for _ in range(2)]
    rcol = p.sb("ab_r", [128, 1], F32)
    acc = p.sb("ab_acc", [128, 2], F32)
    onec = p.sb("ab_one", [128, 1], F32)
    osb = p.sb("ab_o", [2, 1], F32)
    pso = p.ps("ab_ps", [2, 1], F32)
    ones_in = p.din("ones_col_f32_ab", [128, 1], F32)
    ro = p.ring(1)

    chunks = [(w1s[i * 128:(i + 1) * 128, :], 2048, 0) for i in range(8)] + \
             [(w2s[i * 128:(i + 1) * 128, :], 8192, 1) for i in range(2)]

    @p.on("sync")
    def _(e):
        ro.push(e.dma_start(onec[:], ones_in))
        for i, (src, w, col) in enumerate(chunks):
            if i >= 2:
                e.wait_ge(red, i - 1)
            r1.push(e.dma_start(wbuf[i % 2][:, 0:w], src))

    @p.on("vector")
    def _(e):
        e.chain(e.memset(acc[:], 0.0))
        for i, (src, w, col) in enumerate(chunks):
            r1.wait(e, i)
            e.chain(e.reduce_sum(rcol[:], wbuf[i % 2][:, 0:w], axis=AX.X,
                                 apply_absolute_value=True))
            e.chain_inc(e.tensor_add(acc[:, col:col + 1], acc[:, col:col + 1],
                                     rcol[:]), red)
        e.nop().then_inc(red, 1)

    @p.on("tensor")
    def _(e):
        e.wait_ge(red, len(chunks) + 1)
        ro.wait_group(e)
        e.matmul(pso[:], acc[:], onec[:], start=True, stop=True
                 ).then_inc(fin, 1)

    @p.on("scalar")
    def _(e):
        e.wait_ge(fin, 1)
        e.activation(osb[:], pso[:], AF.Identity).then_inc(fin, 1)

    @p.on("gpsimd")
    def _(e):
        e.wait_ge(fin, 2)
        rst.push(e.dma_start(out.rearrange("a b -> b a"), osb[:]))
        rst.wait_group(e)



def phase_ffn_thr(p, absums, consts, cring, thr_bc, nthr_bc):
    ident = consts["ident_f32"]
    ones_row = consts["ones_row_f32"]
    ones_cf = consts["ones_col_f32"]
    ra = p.ring(1)
    ms = p.sem()

    asb = p.sb("th_a", [8, 2], F32)
    tsb = p.sb("th_t", [2, 1], F32)
    trow = p.sb("th_r", [1, 2], F32)
    pst = p.ps("th_p", [128, 2], F32, freeable=True)

    @p.on("sync")
    def _(e):
        ra.push(e.dma_start(asb[:], absums))

    @p.on("tensor")
    def _(e):
        ra.wait_group(e)
        cring.wait_group(e)
        e.matmul(pst[0:2, 0:1], asb[:], ones_cf[0:8, 0:1],
                 start=True, stop=True).then_inc(ms, 1)
        e.wait_ge(ms, 2)
        e.matmul(pst[0:1, 0:2], tsb[:], ident[0:2, 0:2],
                 start=True, stop=True).then_inc(ms, 1)
        e.wait_ge(ms, 4)
        e.matmul(pst[:, 0:2], ones_row[:], trow[:],
                 start=True, stop=True).then_inc(ms, 1)

    @p.on("scalar")
    def _(e):
        e.wait_ge(ms, 1)
        e.activation(tsb[:], pst[0:2, 0:1], AF.Identity,
                     scale=0.5 / (8192.0 * 2048.0)).then_inc(ms, 1)
        e.wait_ge(ms, 3)
        e.activation(trow[:], pst[0:1, 0:2], AF.Identity).then_inc(ms, 1)
        e.wait_ge(ms, 5)
        e.chain(e.activation(thr_bc[:], pst[:, 0:2], AF.Identity))
        e.chain_inc(e.activation(nthr_bc[:], thr_bc[:], AF.Identity,
                                 scale=-1.0), ms)


def phase_ffn_h(p, xn_own, w1, b1, thr_bc, nthr_bc, consts, cring, h_out,
                thr_ready):
    NFC = 64
    NF = D // 128
    TH = TOK // 512
    ident = consts["ident"]
    ones_cf = consts["ones_col_f32"]

    rxn = p.ring(1)
    rw1 = p.ring(4)
    rb1 = p.ring(1)
    rsth = p.ring(4)
    qzd = p.sem()
    trd = p.sem()
    tre = p.sem()
    hmm = p.sem()
    hev = p.sem()
    b1m = p.sem()
    b1e = p.sem()

    xn_dm = p.sb("fh_xndm", [128, NF * TOK], BF16, freeable=True)
    w1buf = [p.sb("fh_w1", [128, 2048], F32) for _ in range(2)]
    tle = p.sb("fh_tle", [128, 2048], BF16)
    qnat = [p.sb("fh_qn", [128, 2048], BF16) for _ in range(2)]
    w1qT = [p.sb("fh_w1qT", [128, 2048], BF16) for _ in range(2)]
    b1row = p.sb("fh_b1r", [1, 8192], F32)
    b1c = p.sb("fh_b1c", [128, 64], F32)
    h_sb = [p.sb("fh_h", [128, TOK], BF16) for _ in range(2)]
    ps_t = p.ps("fh_pst", [128, 2048], F32, freeable=True)
    ps_h = [p.ps("fh_psh", [128, 512], F32, freeable=True) for _ in range(2)]
    ps_b1 = p.ps("fh_psb", [128, 64], F32, freeable=True)

    @p.on("sync")
    def _(e):
        rb1.push(e.dma_start(b1row[:], b1))
        for c in range(NF):
            rxn.push(e.dma_start(xn_dm[:, c * TOK:(c + 1) * TOK],
                                 xn_own[:, c * 128:(c + 1) * 128],
                                 transpose=True))
        for fc in range(NFC):
            if fc >= 2:
                e.wait_ge(qzd, fc - 1)
            rw1.push(e.dma_start(w1buf[fc % 2][:],
                                 w1[fc * 128:(fc + 1) * 128, :]))

    @p.on("vector")
    def _(e):
        if thr_ready is not None:
            e.wait_ge(thr_ready[0], thr_ready[1])
        for fc in range(NFC):
            rw1.wait(e, fc)
            if fc >= 2:
                e.wait_ge(trd, (fc - 1) * NF)
            e.chain(e.tensor_scalar(tle[:], w1buf[fc % 2][:],
                                    nthr_bc[:, 0:1], None, ALU.is_le,
                                    ALU.bypass))
            e.chain_inc(e.scalar_tensor_tensor(
                qnat[fc % 2][:], w1buf[fc % 2][:], thr_bc[:, 0:1], tle[:],
                ALU.is_ge, ALU.subtract), qzd)

    @p.on("tensor")
    def _(e):
        cring.wait_group(e)
        rb1.wait_group(e)
        for c in range(NFC):
            inst = e.matmul(ps_b1[:, c:c + 1],
                            b1row[0:1, c * 128:(c + 1) * 128],
                            ones_cf[0:1, 0:1], start=True, stop=True)
            if c == NFC - 1:
                inst.then_inc(b1m, 1)
        rxn.wait_group(e)
        for fc in range(NFC):
            e.wait_ge(qzd, fc + 1)
            e.wait_ge(tre, fc)
            for kc in range(NF):
                inst = e.matmul(ps_t[:, kc * 128:(kc + 1) * 128],
                                qnat[fc % 2][:, kc * 128:(kc + 1) * 128],
                                ident[:], start=True, stop=True)
                if kc == NF - 1:
                    inst.then_inc(trd, NF)
            e.wait_ge(tre, fc + 1)
            for th in range(TH):
                hh = TH * fc + th
                if hh >= 2:
                    e.wait_ge(hev, hh - 1)
                for kc in range(NF):
                    inst = e.matmul(
                        ps_h[hh % 2][:],
                        w1qT[fc % 2][:, kc * 128:(kc + 1) * 128],
                        xn_dm[:, kc * TOK + th * 512:kc * TOK + (th + 1) * 512],
                        start=(kc == 0), stop=(kc == NF - 1))
                    if kc == NF - 1:
                        inst.then_inc(hmm, 1)

    @p.on("scalar")
    def _(e):
        e.wait_ge(b1m, 1)
        e.activation(b1c[:], ps_b1[:], AF.Identity).then_inc(b1e, 1)
        for fc in range(NFC):
            e.wait_ge(trd, (fc + 1) * NF)
            if fc >= 2:
                e.wait_ge(hmm, TH * (fc - 1))
            e.activation(w1qT[fc % 2][:], ps_t[:], AF.Identity
                         ).then_inc(tre, 1)
            for th in range(TH):
                hh = TH * fc + th
                e.wait_ge(hmm, hh + 1)
                if fc >= 2 and th == 0:
                    rsth.wait(e, fc - 2)
                e.activation(h_sb[fc % 2][:, th * 512:(th + 1) * 512],
                             ps_h[hh % 2][:], AF.Gelu,
                             bias=b1c[:, fc:fc + 1]).then_inc(hev, 1)

    @p.on("gpsimd")
    def _(e):
        for fc in range(NFC):
            e.wait_ge(hev, TH * (fc + 1))
            rsth.push(e.dma_start(h_out[fc * 128:(fc + 1) * 128, :],
                                  h_sb[fc % 2][:]))
        rsth.wait_group(e)



def phase_ffn_w2q(p, w2, thr_bc, nthr_bc, consts, w2q_out):
    ident = consts["ident"]
    NEC = 16
    NSC = 4
    rw2 = p.ring(4)
    rst = p.ring(4)
    qzd = p.sem()
    trd = p.sem()
    tre = p.sem()

    w2buf = [p.sb("fw_w2", [128, 2048], F32) for _ in range(2)]
    tle = p.sb("fw_tle", [128, 2048], BF16)
    qnat = [p.sb("fw_qn", [128, 2048], BF16) for _ in range(2)]
    qT = [p.sb("fw_qT", [128, 2048], BF16) for _ in range(2)]
    ps_t = p.ps("fw_pst", [128, 2048], F32, freeable=True)

    NIT = NEC * NSC

    @p.on("sync")
    def _(e):
        for it in range(NIT):
            ec, sc = divmod(it, NSC)
            if it >= 2:
                e.wait_ge(qzd, it - 1)
            rw2.push(e.dma_start(
                w2buf[it % 2][:],
                w2[ec * 128:(ec + 1) * 128, sc * 2048:(sc + 1) * 2048]))

    @p.on("vector")
    def _(e):
        for it in range(NIT):
            rw2.wait(e, it)
            if it >= 2:
                e.wait_ge(trd, it - 1)
            e.chain(e.tensor_scalar(tle[:], w2buf[it % 2][:],
                                    nthr_bc[:, 1:2], None, ALU.is_le,
                                    ALU.bypass))
            e.chain_inc(e.scalar_tensor_tensor(
                qnat[it % 2][:], w2buf[it % 2][:], thr_bc[:, 1:2], tle[:],
                ALU.is_ge, ALU.subtract), qzd)

    @p.on("tensor")
    def _(e):
        for it in range(NIT):
            e.wait_ge(qzd, it + 1)
            e.wait_ge(tre, it)
            for kc in range(16):
                inst = e.matmul(ps_t[:, kc * 128:(kc + 1) * 128],
                                qnat[it % 2][:, kc * 128:(kc + 1) * 128],
                                ident[:], start=True, stop=True)
                if kc == 15:
                    inst.then_inc(trd, 1)

    @p.on("scalar")
    def _(e):
        for it in range(NIT):
            e.wait_ge(trd, it + 1)
            if it >= 2:
                rst.wait(e, it - 2)
            e.activation(qT[it % 2][:], ps_t[:], AF.Identity).then_inc(tre, 1)

    @p.on("gpsimd")
    def _(e):
        for it in range(NIT):
            ec, sc = divmod(it, NSC)
            e.wait_ge(tre, it + 1)
            dst = w2q_out[sc * 2048:(sc + 1) * 2048,
                          ec * 128:(ec + 1) * 128].rearrange(
                "(a p) c -> p a c", p=128)
            rst.push(e.dma_start(
                dst, qT[it % 2][:].rearrange("p (a c) -> p a c", a=16)))
        rst.wait_group(e)



def phase_ffn_out(p, h_in, w2q_in, b2, s_in, x_in, consts, x1_out):
    NG = 8
    GRP = 8
    NTT = TOK // 128
    ones_row = consts["ones_row_f32"]

    rb2 = p.ring(1)
    rw = p.ring(2, per_slot=GRP)
    rh = p.ring(2, per_slot=GRP)
    rsx = p.ring(4)
    rst = p.ring(4)
    ped = p.sem()
    eva = p.sem()
    bmm = p.sem()
    bev = p.sem()
    sild = p.sem()
    find = p.sem()

    wgb = p.sb("fo_w", [128, GRP * 2048], BF16)
    hgb = [p.sb("fo_h", [128, GRP * TOK], BF16) for _ in range(2)]
    b2row = p.sb("fo_b2r", [1, 2048], F32)
    bbc2 = p.sb("fo_bbc", [128, 2048], F32)
    o_sb = p.sb("fo_o", [128, NTT * 2048], F32)
    g_sb = p.sb("fo_g", [128, 2048], F32)
    sx = [p.sb("fo_sx", [128, 2048], F32) for _ in range(2)]
    xx = [p.sb("fo_xx", [128, 2048], F32) for _ in range(2)]
    x1t = [p.sb("fo_x1", [128, 2048], F32) for _ in range(2)]
    ps_o = [p.ps("fo_po", [128, 2048], F32) for _ in range(2)]

    @p.on("sync")
    def _(e):
        rb2.push(e.dma_start(b2row[:], b2))
        for go in range(NG):
            if go >= 1:
                e.wait_ge(ped, go * NTT)
            for ci in range(GRP):
                cf = go * GRP + ci
                rw.push_slot(e.dma_start(
                    wgb[:, ci * 2048:(ci + 1) * 2048],
                    w2q_in[cf * 128:(cf + 1) * 128, :]), go)
        for tt in range(NTT):
            rsx.push(e.dma_start(sx[tt % 2][:],
                                 s_in[tt * 128:(tt + 1) * 128, :]))
            rsx.push(e.dma_start(xx[tt % 2][:],
                                 x_in[tt * 128:(tt + 1) * 128, :]))
            e.wait_ge(find, tt + 1)

    @p.on("gpsimd")
    def _(e):
        for go in range(NG):
            if go >= 2:
                e.wait_ge(ped, (go - 1) * NTT)
            for ci in range(GRP):
                cf = go * GRP + ci
                rh.push_slot(e.dma_start(
                    hgb[go % 2][:, ci * TOK:(ci + 1) * TOK],
                    h_in[cf * 128:(cf + 1) * 128, :]), go)
        for tt in range(NTT):
            e.wait_ge(find, tt + 1)
            rst.push(e.dma_start(x1_out[tt * 128:(tt + 1) * 128, :],
                                 x1t[tt % 2][:]))
        rst.wait_group(e)

    @p.on("tensor")
    def _(e):
        rb2.wait_group(e)
        for i in range(4):
            e.matmul(ps_o[0][:, i * 512:(i + 1) * 512], ones_row[:],
                     b2row[:, i * 512:(i + 1) * 512],
                     start=True, stop=True).then_inc(bmm, 1)
        e.wait_ge(bev, 4)
        for go in range(NG):
            rw.wait_slot(e, go)
            rh.wait_slot(e, go)
            for tt in range(NTT):
                it = go * NTT + tt
                if it >= 2:
                    e.wait_ge(eva, it)
                for ci in range(GRP):
                    for c4 in range(4):
                        inst = e.matmul(
                            ps_o[it % 2][:, c4 * 512:(c4 + 1) * 512],
                            hgb[go % 2][:, ci * TOK + tt * 128:
                                         ci * TOK + (tt + 1) * 128],
                            wgb[:, ci * 2048 + c4 * 512:
                                ci * 2048 + (c4 + 1) * 512],
                            start=(ci == 0), stop=(ci == GRP - 1),
                            skip_group_check=True)
                        if ci == GRP - 1 and c4 == 3:
                            inst.then_inc(ped, 1)

    @p.on("scalar")
    def _(e):
        for i in range(4):
            e.wait_ge(bmm, i + 1)
            e.activation(bbc2[:, i * 512:(i + 1) * 512],
                         ps_o[0][:, i * 512:(i + 1) * 512],
                         AF.Identity).then_inc(bev, 1)
        for tt in range(NTT):
            e.wait_ge(eva, 1 + (NG - 1) * NTT + tt + 1)
            if tt >= 1:
                e.wait_ge(find, tt)
            e.activation(g_sb[:], o_sb[:, tt * 2048:(tt + 1) * 2048],
                         AF.Silu).then_inc(sild, 1)

    @p.on("vector")
    def _(e):
        e.wait_ge(bev, 4)
        for tt in range(NTT):
            e.chain(e.tensor_copy(o_sb[:, tt * 2048:(tt + 1) * 2048],
                                  bbc2[:]))
        e.nop().then_inc(eva, 1)
        for it in range(NG * NTT):
            tt = it % NTT
            e.wait_ge(ped, it + 1)
            for c4 in range(4):
                inst = e.tensor_add(
                    o_sb[:, tt * 2048 + c4 * 512:tt * 2048 + (c4 + 1) * 512],
                    o_sb[:, tt * 2048 + c4 * 512:tt * 2048 + (c4 + 1) * 512],
                    ps_o[it % 2][:, c4 * 512:(c4 + 1) * 512])
                if c4 == 3:
                    e.chain_inc(inst, eva)
                else:
                    e.chain(inst)
        for tt in range(NTT):
            e.wait_ge(sild, tt + 1)
            rsx.wait(e, 2 * tt)
            rsx.wait(e, 2 * tt + 1)
            if tt >= 2:
                rst.wait(e, tt - 2)
            e.chain(e.tensor_mul(x1t[tt % 2][:], g_sb[:], sx[tt % 2][:]))
            e.chain_inc(e.tensor_add(x1t[tt % 2][:], x1t[tt % 2][:],
                                     xx[tt % 2][:]), find)



def emit_cols_from_row(p, row_ap, ncols, ones_cf, name):
    colt = p.sb(name, [128, ncols], F32)
    psc = p.ps(name + "_ps", [128, ncols], F32)
    dn = p.sem()

    @p.on("tensor")
    def _(e):
        for c in range(ncols):
            inst = e.matmul(psc[:, c:c + 1], row_ap[0:1, c * 128:(c + 1) * 128],
                            ones_cf[0:1, 0:1], start=True, stop=True)
            if c == ncols - 1:
                inst.then_inc(dn, 1)

    @p.on("scalar")
    def _(e):
        e.wait_ge(dn, 1)
        e.activation(colt[:], psc[:], AF.Identity).then_inc(dn, 1)
    return colt, dn



def phase_qproj(p, tq, g_state, b_state, wq, consts, cring, xi0_out):
    ones_cf = consts["ones_col_f32"]

    rgb = p.ring(1)
    grow = p.sb("xp2_gr", [1, D], F32)
    brow = p.sb("xp2_br", [1, D], F32)

    @p.on("sync")
    def _(e):
        rgb.push(e.dma_start(grow[:], g_state))
        rgb.push(e.dma_start(brow[:], b_state))

    @p.on("tensor")
    def _(e):
        rgb.wait_group(e)
        cring.wait_group(e)
    gcol, gdn = emit_cols_from_row(p, grow, 16, ones_cf, "xp2_gc")
    bcol, bdn = emit_cols_from_row(p, brow, 16, ones_cf, "xp2_bc")

    wq_bf = p.dtmp("xp2_wqbf", [1024, D], BF16)
    rwc = p.ring(2)
    rwt = p.ring(1)
    rxt = p.ring(1)
    rst = p.ring(4)
    gbd = p.sem()
    qmd = p.sem()
    qev = p.sem()

    wq_fm = p.sb("xp2_wqfm", [128, 16 * 1024], BF16)
    xng = p.sb("xp2_xng", [128, 16 * TOK], BF16)
    xio = [p.sb("xp2_xio", [128, 512], BF16) for _ in range(2)]
    ps_q = [p.ps("xp2_psq", [128, 512], F32) for _ in range(2)]

    @p.on("gpsimd")
    def _(e):
        for i in range(2):
            rwc.push(e.dma_start(wq_bf[i * 512:(i + 1) * 512, :],
                                 wq[i * 512:(i + 1) * 512, :]))
        for oc in range(8):
            for th in range(TOK // 512):
                n = oc * (TOK // 512) + th
                e.wait_ge(qev, n + 1)
                rst.push(e.dma_start(
                    xi0_out[oc * 128:(oc + 1) * 128,
                            th * 512:(th + 1) * 512], xio[n % 2][:]))
        rst.wait_group(e)

    @p.on("sync")
    def _(e):
        rwc.wait_group(e, 1)
        for c in range(16):
            rwt.push(e.dma_start(wq_fm[:, c * 1024:(c + 1) * 1024],
                                 wq_bf[:, c * 128:(c + 1) * 128],
                                 transpose=True))
        for c in range(16):
            rxt.push(e.dma_start(xng[:, c * TOK:(c + 1) * TOK],
                                 tq[:, c * 128:(c + 1) * 128],
                                 transpose=True))

    @p.on("vector")
    def _(e):
        rxt.wait_group(e)
        e.wait_ge(gdn, 2)
        e.wait_ge(bdn, 2)
        for c in range(16):
            inst = e.tensor_scalar(xng[:, c * TOK:(c + 1) * TOK],
                                   xng[:, c * TOK:(c + 1) * TOK],
                                   gcol[:, c:c + 1], bcol[:, c:c + 1],
                                   ALU.mult, ALU.add)
            e.chain(inst)
        e.nop().then_inc(gbd, 1)

    @p.on("tensor")
    def _(e):
        rwt.wait_group(e)
        e.wait_ge(gbd, 1)
        for oc in range(8):
            for th in range(TOK // 512):
                n = oc * (TOK // 512) + th
                if n >= 2:
                    e.wait_ge(qev, n - 1)
                for kc in range(16):
                    inst = e.matmul(
                        ps_q[n % 2][:],
                        wq_fm[:, kc * 1024 + oc * 128:kc * 1024 + (oc + 1) * 128],
                        xng[:, kc * TOK + th * 512:kc * TOK + (th + 1) * 512],
                        start=(kc == 0), stop=(kc == 15))
                    if kc == 15:
                        inst.then_inc(qmd, 1)

    @p.on("scalar")
    def _(e):
        for n in range(16):
            e.wait_ge(qmd, n + 1)
            if n >= 2:
                rst.wait(e, n - 2)
            e.activation(xio[n % 2][:], ps_q[n % 2][:], AF.Identity
                         ).then_inc(qev, 1)


def phase_ln2_from_bf(p, xn_in, out, T=TOK):
    nt = T // 128
    rld = p.ring(4)
    rst = p.ring(4)
    s2d = p.sem()
    vep = p.sem()
    rsd = p.sem()
    nbd = p.sem()
    nrm = p.sem()

    xt = [p.sb("l2_x", [128, D], BF16) for _ in range(2)]
    xnt = [p.sb("l2_o", [128, D], BF16) for _ in range(2)]
    scr_v = p.sb("l2_scrv", [128, D], F32)
    scr_s = p.sb("l2_scrs", [128, D], F32)
    s1 = [p.sb("l2_s1", [128, 1], F32) for _ in range(2)]
    s2 = [p.sb("l2_s2", [128, 1], F32) for _ in range(2)]
    mean = p.sb("l2_mean", [128, 1], F32)
    ex2 = p.sb("l2_ex2", [128, 1], F32)
    varep = p.sb("l2_var", [128, 1], F32)
    rstd = p.sb("l2_rstd", [128, 1], F32)
    nbias = p.sb("l2_nb", [128, 1], F32)

    @p.on("sync")
    def _(e):
        for i in range(nt):
            if i >= 2:
                e.wait_ge(nrm, i - 1)
            rld.push(e.dma_start(xt[i % 2][:], xn_in[i * 128:(i + 1) * 128, :]))

    @p.on("vector")
    def _(e):
        for i in range(nt):
            b = i % 2
            rld.wait(e, i)
            e.chain(e.tensor_scalar(scr_v[:], xt[b][:], 0.0, None,
                                    ALU.add, ALU.add, accum_out=s1[b][:]))
            e.wait_ge(s2d, i + 1)
            e.chain(e.tensor_scalar(mean[:], s1[b][:], 1.0 / D, None,
                                    ALU.mult, ALU.bypass))
            e.chain(e.tensor_scalar(ex2[:], s2[b][:], 1.0 / D, None,
                                    ALU.mult, ALU.bypass))
            e.chain(e.scalar_tensor_tensor(varep[:], mean[:], -1.0, mean[:],
                                           ALU.mult, ALU.mult))
            e.chain(e.tensor_add(varep[:], varep[:], ex2[:]))
            e.chain_inc(e.tensor_scalar(varep[:], varep[:], LN_EPS, None,
                                        ALU.add, ALU.bypass), vep)
            e.wait_ge(rsd, i + 1)
            e.chain_inc(e.scalar_tensor_tensor(nbias[:], rstd[:], -1.0,
                                               mean[:], ALU.mult, ALU.mult), nbd)

    @p.on("scalar")
    def _(e):
        for i in range(nt):
            b = i % 2
            rld.wait(e, i)
            e.activation(scr_s[:], xt[b][:], AF.Square,
                         accum_out=s2[b][:]).then_inc(s2d, 1)
            e.wait_ge(vep, i + 1)
            e.chain(e.activation(rstd[:], varep[:], AF.Ln))
            e.chain_inc(e.activation(rstd[:], rstd[:], AF.Exp,
                                     scale=-0.5), rsd)
            e.wait_ge(nbd, i + 1)
            if i >= 2:
                rst.wait(e, i - 2)
            e.activation(xnt[b][:], xt[b][:], AF.Identity, bias=nbias[:],
                         scale=rstd[:]).then_inc(nrm, 1)

    @p.on("gpsimd")
    def _(e):
        for i in range(nt):
            e.wait_ge(nrm, i + 1)
            rst.push(e.dma_start(out[i * 128:(i + 1) * 128, :],
                                 xnt[i % 2][:]))
        rst.wait_group(e)



def phase_memkv(p, mem_s, g_mem, b_mem, wk, wv, log_beta, consts, cring,
                kb_out, v_out):
    Md = 1024
    NR = 256
    ones_cf = consts["ones_col_f32"]

    mn_d = p.dtmp("mk_mn", [NR, Md], BF16)
    m0 = p.scope()
    phase_lnmem(p, mem_s, mn_d, NR, Md)
    p.barrier()
    p.pop_scope(m0)

    rgb = p.ring(1)
    grow = p.sb("mk_gr", [1, Md], F32)
    brow = p.sb("mk_br", [1, Md], F32)
    lbrow = p.sb("mk_lb", [1, 8], F32)
    betar = p.sb("mk_be", [1, 8], F32)
    beta_bc = p.sb("mk_bbc", [128, 8], F32)

    @p.on("sync")
    def _(e):
        rgb.push(e.dma_start(grow[:], g_mem))
        rgb.push(e.dma_start(brow[:], b_mem))
        rgb.push(e.dma_start(lbrow[:], log_beta))

    @p.on("tensor")
    def _(e):
        rgb.wait_group(e)
        cring.wait_group(e)
    gcol, gdn = emit_cols_from_row(p, grow, 8, ones_cf, "mk_gc")
    bcol, bdn = emit_cols_from_row(p, brow, 8, ones_cf, "mk_bc")

    wkv_bf = p.dtmp("mk_wkv", [2 * Md, Md], BF16)
    rwc = p.ring(2)
    rwt = p.ring(1)
    rmt = p.ring(1)
    rst = p.ring(4)
    bbd = p.sem()
    gbd = p.sem()
    kmd = p.sem()
    kev = p.sem()
    vmd = p.sem()
    vev = p.sem()

    wk_fm = p.sb("mk_wkfm", [128, 8 * Md], BF16)
    wv_fm = p.sb("mk_wvfm", [128, 8 * Md], BF16)
    mn_dm = p.sb("mk_mndm", [128, 8 * NR], BF16)
    kbo = [p.sb("mk_kbo", [128, NR], BF16) for _ in range(2)]
    vo = [p.sb("mk_vo", [128, 512], BF16) for _ in range(2)]
    ps_k = [p.ps("mk_psk", [128, NR], F32) for _ in range(2)]
    ps_v = [p.ps("mk_psv", [128, 512], F32) for _ in range(2)]
    ps_b = p.ps("mk_psb", [128, 8], F32)

    @p.on("gpsimd")
    def _(e):
        rwc.push(e.dma_start(wkv_bf[0:Md, :], wk))
        rwc.push(e.dma_start(wkv_bf[Md:2 * Md, :], wv))
        for oc in range(8):
            e.wait_ge(kev, oc + 1)
            rst.push(e.dma_start(kb_out[oc * 128:(oc + 1) * 128, :],
                                 kbo[oc % 2][:]))
        for n in range(4):
            e.wait_ge(vev, n + 1)
            tc, vc = divmod(n, 2)
            rst.push(e.dma_start(
                v_out[tc * 128:(tc + 1) * 128, vc * 512:(vc + 1) * 512],
                vo[n % 2][:]))
        rst.wait_group(e)

    @p.on("sync")
    def _(e):
        rwc.wait_group(e, 1)
        for c in range(8):
            rwt.push(e.dma_start(wk_fm[:, c * Md:(c + 1) * Md],
                                 wkv_bf[0:Md, c * 128:(c + 1) * 128],
                                 transpose=True))
        for c in range(8):
            rwt.push(e.dma_start(wv_fm[:, c * Md:(c + 1) * Md],
                                 wkv_bf[Md:2 * Md, c * 128:(c + 1) * 128],
                                 transpose=True))
        for c in range(8):
            rmt.push(e.dma_start(mn_dm[:, c * NR:(c + 1) * NR],
                                 mn_d[:, c * 128:(c + 1) * 128],
                                 transpose=True))

    @p.on("vector")
    def _(e):
        rmt.wait_group(e)
        e.wait_ge(gdn, 2)
        e.wait_ge(bdn, 2)
        for c in range(8):
            e.chain(e.tensor_scalar(mn_dm[:, c * NR:(c + 1) * NR],
                                    mn_dm[:, c * NR:(c + 1) * NR],
                                    gcol[:, c:c + 1], bcol[:, c:c + 1],
                                    ALU.mult, ALU.add))
        e.nop().then_inc(gbd, 1)
        for oc in range(8):
            e.wait_ge(kmd, oc + 1)
            if oc >= 2:
                rst.wait(e, oc - 2)
            e.chain_inc(e.tensor_scalar(kbo[oc % 2][:], ps_k[oc % 2][:],
                                        beta_bc[:, oc:oc + 1], None,
                                        ALU.mult, ALU.bypass), kev)

    @p.on("scalar")
    def _(e):
        rgb.wait_group(e)
        e.activation(betar[:], lbrow[:], AF.Exp).then_inc(bbd, 1)
        e.wait_ge(bbd, 2)
        e.activation(beta_bc[:], ps_b[:], AF.Identity).then_inc(bbd, 1)
        for n in range(4):
            e.wait_ge(vmd, n + 1)
            if n >= 2:
                rst.wait(e, 8 + n - 2)
            e.activation(vo[n % 2][:], ps_v[n % 2][:], AF.Identity
                         ).then_inc(vev, 1)

    @p.on("tensor")
    def _(e):
        e.wait_ge(bbd, 1)
        e.matmul(ps_b[:], consts["ones_row_f32"][:], betar[:],
                 start=True, stop=True).then_inc(bbd, 1)
        rwt.wait_group(e)
        e.wait_ge(gbd, 1)
        e.wait_ge(bbd, 3)
        for oc in range(8):
            if oc >= 2:
                e.wait_ge(kev, oc - 1)
            for kc in range(8):
                inst = e.matmul(
                    ps_k[oc % 2][:],
                    wk_fm[:, kc * Md + oc * 128:kc * Md + (oc + 1) * 128],
                    mn_dm[:, kc * NR:(kc + 1) * NR],
                    start=(kc == 0), stop=(kc == 7))
                if kc == 7:
                    inst.then_inc(kmd, 1)
        for n in range(4):
            tc, vc = divmod(n, 2)
            if n >= 2:
                e.wait_ge(vev, n - 1)
            for kc in range(8):
                inst = e.matmul(
                    ps_v[n % 2][:],
                    mn_dm[:, kc * NR + tc * 128:kc * NR + (tc + 1) * 128],
                    wv_fm[:, kc * Md + vc * 512:kc * Md + (vc + 1) * 512],
                    start=(kc == 0), stop=(kc == 7))
                if kc == 7:
                    inst.then_inc(vmd, 1)


def phase_lnmem(p, x_in, out, T, W):
    nt = T // 128
    rld = p.ring(2)
    rst = p.ring(2)
    s2d = p.sem()
    vep = p.sem()
    rsd = p.sem()
    nbd = p.sem()
    nrm = p.sem()

    xt = [p.sb("lm_x", [128, W], F32) for _ in range(2)]
    xnt = [p.sb("lm_o", [128, W], BF16) for _ in range(2)]
    scr_v = p.sb("lm_scrv", [128, W], F32)
    scr_s = p.sb("lm_scrs", [128, W], F32)
    s1 = [p.sb("lm_s1", [128, 1], F32) for _ in range(2)]
    s2 = [p.sb("lm_s2", [128, 1], F32) for _ in range(2)]
    mean = p.sb("lm_mean", [128, 1], F32)
    ex2 = p.sb("lm_ex2", [128, 1], F32)
    varep = p.sb("lm_var", [128, 1], F32)
    rstd = p.sb("lm_rstd", [128, 1], F32)
    nbias = p.sb("lm_nb", [128, 1], F32)

    @p.on("sync")
    def _(e):
        for i in range(nt):
            if i >= 2:
                e.wait_ge(nrm, i - 1)
            rld.push(e.dma_start(xt[i % 2][:], x_in[i * 128:(i + 1) * 128, :]))

    @p.on("vector")
    def _(e):
        for i in range(nt):
            b = i % 2
            rld.wait(e, i)
            e.chain(e.tensor_scalar(scr_v[:], xt[b][:], 0.0, None,
                                    ALU.add, ALU.add, accum_out=s1[b][:]))
            e.wait_ge(s2d, i + 1)
            e.chain(e.tensor_scalar(mean[:], s1[b][:], 1.0 / W, None,
                                    ALU.mult, ALU.bypass))
            e.chain(e.tensor_scalar(ex2[:], s2[b][:], 1.0 / W, None,
                                    ALU.mult, ALU.bypass))
            e.chain(e.scalar_tensor_tensor(varep[:], mean[:], -1.0, mean[:],
                                           ALU.mult, ALU.mult))
            e.chain(e.tensor_add(varep[:], varep[:], ex2[:]))
            e.chain_inc(e.tensor_scalar(varep[:], varep[:], LN_EPS, None,
                                        ALU.add, ALU.bypass), vep)
            e.wait_ge(rsd, i + 1)
            e.chain_inc(e.scalar_tensor_tensor(nbias[:], rstd[:], -1.0,
                                               mean[:], ALU.mult, ALU.mult), nbd)

    @p.on("scalar")
    def _(e):
        for i in range(nt):
            b = i % 2
            rld.wait(e, i)
            e.activation(scr_s[:], xt[b][:], AF.Square,
                         accum_out=s2[b][:]).then_inc(s2d, 1)
            e.wait_ge(vep, i + 1)
            e.chain(e.activation(rstd[:], varep[:], AF.Ln))
            e.chain_inc(e.activation(rstd[:], rstd[:], AF.Exp,
                                     scale=-0.5), rsd)
            e.wait_ge(nbd, i + 1)
            if i >= 2:
                rst.wait(e, i - 2)
            e.activation(xnt[b][:], xt[b][:], AF.Identity, bias=nbias[:],
                         scale=rstd[:]).then_inc(nrm, 1)

    @p.on("gpsimd")
    def _(e):
        for i in range(nt):
            e.wait_ge(nrm, i + 1)
            rst.push(e.dma_start(out[i * 128:(i + 1) * 128, :],
                                 xnt[i % 2][:]))
        rst.wait_group(e)



def phase_entmax(p, xi0_in, kb_in, v_in, consts, cring, xi3_out,
                 n_mich=4, n_newt=3, NSTEP=3, NH=8):
    NTC = TOK // 128
    NIT = n_mich + n_newt
    NJ = NIT + 1
    N = 2048
    ident = consts["ident"]

    rkb = p.ring(1)
    rv = p.ring(1)
    rxi = p.ring(1)
    rst = p.ring(4)
    smd = p.sem()
    sev = p.sem()
    rel = p.sem()
    sst = p.sem()
    ngp = p.sem()
    rowd = p.sem()
    dcd = p.sem()
    sqd = p.sem()
    ttd = p.sem()
    tte = p.sem()
    xmd = p.sem()
    xev = p.sem()

    kb_sb = p.sb("em_kb", [128, NH * N], BF16)
    v_sb = p.sb("em_v", [128, 16 * 1024], BF16)
    xi = [p.sb("em_xi", [128, NH * TOK], BF16) for _ in range(2)]
    xp = p.sb("em_xp", [128, NTC * N], F32)
    m = [p.sb("em_m", [128, N], F32) for _ in range(2)]
    msq = p.sb("em_msq", [128, N], BF16)
    w_bf = [p.sb("em_w", [128, N], BF16) for _ in range(2)]
    wT = p.sb("em_wT", [128, N], BF16)
    mx2 = p.sb("em_mx2", [128, NTC], F32)
    tau = p.sb("em_tau", [128, NTC], F32)
    a_r = p.sb("em_a", [128, NTC], F32)
    na_r = p.sb("em_na", [128, NTC], F32)
    s1c = p.sb("em_s1", [128, NTC], F32)
    s2c = p.sb("em_s2", [128, NTC], F32)
    ncc = p.sb("em_nc", [128, NTC], F32)
    t1 = p.sb("em_t1", [128, NTC], F32)
    t2 = p.sb("em_t2", [128, NTC], F32)
    t3 = p.sb("em_t3", [128, NTC], F32)
    disc = p.sb("em_dc", [128, NTC], F32)
    sqr = p.sb("em_sq", [128, NTC], F32)
    gj = p.sb("em_gj", [128, N], BF16)
    ps_s = p.ps("em_pss", [128, N], F32)
    ps_t = p.ps("em_pst", [128, N], F32)
    psA = [ps_s, ps_t]
    ps_xi = ps_s[:, 0:128]

    @p.on("sync")
    def _(e):
        for c in range(NH):
            rkb.push(e.dma_start(kb_sb[:, c * N:(c + 1) * N],
                                 kb_in[c * 128:(c + 1) * 128, :]))
        for c in range(16):
            rv.push(e.dma_start(v_sb[:, c * 1024:(c + 1) * 1024],
                                v_in[c * 128:(c + 1) * 128, :]))
        for c in range(NH):
            rxi.push(e.dma_start(xi[0][:, c * TOK:(c + 1) * TOK],
                                 xi0_in[c * 128:(c + 1) * 128, :]))

    @p.on("tensor")
    def _(e):
        cring.wait_group(e)
        rkb.wait_group(e)
        rv.wait_group(e)
        rxi.wait_group(e)
        for s in range(NSTEP):
            src = xi[s % 2]
            for h in range(NH):
                B = s * NH + h
                if s >= 1:
                    e.wait_ge(xev, s * NH * NTC)
                if B >= 1:
                    e.wait_ge(xev, (B - 1) * NTC + NTC)
                    e.wait_ge(tte, (B - 1) * NTC + NTC)
                for tc in range(NTC):
                    g = B * NTC + tc
                    if tc >= 2:
                        e.wait_ge(sev, g - 1)
                    for c4 in range(4):
                        inst = e.matmul(
                            psA[tc % 2][:, c4 * 512:(c4 + 1) * 512],
                            src[:, h * TOK + tc * 128:h * TOK + (tc + 1) * 128],
                            kb_sb[:, h * N + c4 * 512:h * N + (c4 + 1) * 512],
                            start=True, stop=True, skip_group_check=True)
                        if c4 == 3:
                            inst.then_inc(smd, 1)
                e.wait_ge(sev, B * NTC + NTC)
                for tc in range(NTC):
                    g = B * NTC + tc
                    e.wait_ge(sst, B * NJ * NTC + NIT * NTC + tc + 1)
                    if tc >= 1:
                        e.wait_ge(tte, g)
                    for kc in range(16):
                        inst = e.matmul(ps_t[:, kc * 128:(kc + 1) * 128],
                                        w_bf[tc % 2][:, kc * 128:(kc + 1) * 128],
                                        ident[:], start=True, stop=True)
                        if kc == 15:
                            inst.then_inc(ttd, 1)
                    e.wait_ge(tte, g + 1)
                    if tc >= 1:
                        e.wait_ge(xev, g)
                    for kc in range(16):
                        inst = e.matmul(
                            ps_xi,
                            v_sb[:, kc * 1024 + h * 128:kc * 1024 + (h + 1) * 128],
                            wT[:, kc * 128:(kc + 1) * 128],
                            start=(kc == 0), stop=(kc == 15),
                            skip_group_check=True)
                        if kc == 15:
                            inst.then_inc(xmd, 1)

    @p.on("scalar")
    def _(e):
        for s in range(NSTEP):
            for h in range(NH):
                B = s * NH + h
                for tc in range(NTC):
                    g = B * NTC + tc
                    e.wait_ge(smd, g + 1)
                    e.activation(xp[:, tc * N:(tc + 1) * N], psA[tc % 2][:],
                                 AF.Identity, scale=0.5).then_inc(sev, 1)
                for j in range(NIT):
                    e.wait_ge(rowd, B * (NJ + 1) + j + 1)
                    for tc in range(NTC):
                        gr = B * NJ * NTC + j * NTC + tc
                        if gr >= 2:
                            e.wait_ge(sst, gr - 1)
                        e.activation(m[gr % 2][:], xp[:, tc * N:(tc + 1) * N],
                                     AF.Relu, bias=na_r[:, tc:tc + 1],
                                     accum_out=s1c[:, tc:tc + 1]
                                     ).then_inc(rel, 1)
                    if j < n_mich:
                        e.wait_ge(dcd, B * n_mich + j + 1)
                        e.activation(sqr[:], disc[:], AF.Sqrt
                                     ).then_inc(sqd, 1)
                e.wait_ge(rowd, B * (NJ + 1) + NIT + 1)
                for tc in range(NTC):
                    g = B * NTC + tc
                    gr = B * NJ * NTC + NIT * NTC + tc
                    if gr >= 2:
                        e.wait_ge(sst, gr - 1)
                    e.activation(m[gr % 2][:], xp[:, tc * N:(tc + 1) * N],
                                 AF.Relu, bias=na_r[:, tc:tc + 1],
                                 accum_out=s1c[:, tc:tc + 1]).then_inc(rel, 1)
                    e.wait_ge(ttd, g + 1)
                    e.activation(wT[:], ps_t[:], AF.Identity).then_inc(tte, 1)
                    e.wait_ge(xmd, g + 1)
                    e.activation(xi[(s + 1) % 2][:, h * TOK + tc * 128:
                                                 h * TOK + (tc + 1) * 128],
                                 ps_xi, AF.Identity).then_inc(xev, 1)

    @p.on("vector")
    def _(e):
        for s in range(NSTEP):
            for h in range(NH):
                B = s * NH + h
                for tc in range(NTC):
                    e.wait_ge(sev, B * NTC + tc + 1)
                    e.chain(e.reduce_max(mx2[:, tc:tc + 1],
                                         xp[:, tc * N:(tc + 1) * N], axis=AX.X))
                e.chain(e.tensor_scalar(a_r[:], mx2[:], -1.0, None,
                                        ALU.add, ALU.bypass))
                e.chain(e.tensor_scalar(tau[:], a_r[:], 0.0, None,
                                        ALU.mult, ALU.bypass))
                e.chain(e.tensor_scalar(tau[:], tau[:], -1.0, None,
                                        ALU.add, ALU.bypass))
                e.chain_inc(e.tensor_scalar(na_r[:], a_r[:], -1.0, None,
                                            ALU.mult, ALU.bypass),
                            rowd)
                for j in range(NIT):
                    for tc in range(NTC):
                        gr = B * NJ * NTC + j * NTC + tc
                        e.wait_ge(rel, gr + 1)
                        e.chain_inc(e.scalar_tensor_tensor(
                            msq[:], m[gr % 2][:], 0.0, m[gr % 2][:],
                            ALU.add, ALU.mult,
                            accum_out=s2c[:, tc:tc + 1]), sst)
                    if j < n_mich:
                        for tc in range(NTC):
                            e.chain(e.tensor_scalar(
                                gj[:], xp[:, tc * N:(tc + 1) * N],
                                a_r[:, tc:tc + 1], None, ALU.is_gt, ALU.add,
                                accum_out=ncc[:, tc:tc + 1]))
                        e.chain(e.tensor_mul(t1[:], ncc[:], tau[:]))
                        e.chain(e.tensor_add(t1[:], t1[:], s1c[:]))
                        e.chain(e.tensor_mul(t2[:], tau[:], t1[:]))
                        e.chain(e.tensor_scalar(t2[:], t2[:], 2.0, None,
                                                ALU.mult, ALU.bypass))
                        e.chain(e.tensor_mul(t3[:], tau[:], tau[:]))
                        e.chain(e.tensor_mul(t3[:], ncc[:], t3[:]))
                        e.chain(e.tensor_sub(t2[:], t2[:], t3[:]))
                        e.chain(e.tensor_add(t2[:], t2[:], s2c[:]))
                        e.chain(e.tensor_scalar(t2[:], t2[:], -1.0, None,
                                                ALU.add, ALU.bypass))
                        e.chain(e.tensor_mul(t2[:], ncc[:], t2[:]))
                        e.chain(e.tensor_mul(disc[:], t1[:], t1[:]))
                        e.chain(e.tensor_sub(disc[:], disc[:], t2[:]))
                        e.chain_inc(e.tensor_scalar(disc[:], disc[:], 0.0,
                                                    None, ALU.max, ALU.bypass),
                                    dcd)
                        e.wait_ge(sqd, B * n_mich + j + 1)
                        e.chain(e.tensor_sub(t1[:], t1[:], sqr[:]))
                        e.chain(e.reciprocal(t2[:], ncc[:]))
                        e.chain(e.tensor_mul(tau[:], t1[:], t2[:]))
                    else:
                        e.chain(e.tensor_scalar(t1[:], s2c[:], -1.0, None,
                                                ALU.add, ALU.bypass))
                        e.chain(e.tensor_scalar(t2[:], s1c[:], 2.0, None,
                                                ALU.mult, ALU.bypass))
                        e.chain(e.reciprocal(t3[:], t2[:]))
                        e.chain(e.tensor_mul(t1[:], t1[:], t3[:]))
                        e.chain(e.tensor_add(tau[:], tau[:], t1[:]))
                    e.chain(e.tensor_add(a_r[:], mx2[:], tau[:]))
                    e.chain_inc(e.tensor_scalar(na_r[:], a_r[:], -1.0, None,
                                                ALU.mult, ALU.bypass), rowd)
                for tc in range(NTC):
                    gr = B * NJ * NTC + NIT * NTC + tc
                    e.wait_ge(rel, gr + 1)
                    if B * NTC + tc >= 2:
                        e.wait_ge(ttd, B * NTC + tc - 1)
                    e.chain_inc(e.scalar_tensor_tensor(
                        w_bf[tc % 2][:], m[gr % 2][:], 0.0, m[gr % 2][:],
                        ALU.add, ALU.mult), sst)
                e.nop().then_inc(rowd, 1)

    @p.on("gpsimd")
    def _(e):
        e.wait_ge(xev, NSTEP * NH * NTC)
        for c in range(NH):
            rst.push(e.dma_start(xi3_out[c * 128:(c + 1) * 128, :],
                                 xi[NSTEP % 2][:, c * TOK:(c + 1) * TOK]))
        rst.wait_group(e)



def phase_wo(p, xi3_in, wo, consts, cring, s2_out):
    NTT = TOK // 128
    wo_bf = p.dtmp("wo_bf", [D, 1024], BF16)
    rwc = p.ring(1)
    rwt = p.ring(1)
    rxi = p.ring(1)
    rst = p.ring(4)
    pmd = p.sem()
    pev = p.sem()

    wo_fm = p.sb("wo2_fm", [128, 8 * D], BF16)
    xi_sb = p.sb("wo2_xi", [128, 8 * TOK], BF16)
    so = [p.sb("wo2_so", [128, D], F32) for _ in range(2)]
    ps_o = [p.ps("wo2_po", [128, D], F32) for _ in range(2)]

    @p.on("gpsimd")
    def _(e):
        for i in range(2):
            rwc.push(e.dma_start(wo_bf[i * 1024:(i + 1) * 1024, :],
                                 wo[i * 1024:(i + 1) * 1024, :]))
        for tt in range(NTT):
            e.wait_ge(pev, tt + 1)
            rst.push(e.dma_start(s2_out[tt * 128:(tt + 1) * 128, :],
                                 so[tt % 2][:]))
        rst.wait_group(e)

    @p.on("sync")
    def _(e):
        rwc.wait_group(e, 1)
        for c in range(8):
            rwt.push(e.dma_start(wo_fm[:, c * D:(c + 1) * D],
                                 wo_bf[:, c * 128:(c + 1) * 128],
                                 transpose=True))
        for c in range(8):
            rxi.push(e.dma_start(xi_sb[:, c * TOK:(c + 1) * TOK],
                                 xi3_in[c * 128:(c + 1) * 128, :]))

    @p.on("tensor")
    def _(e):
        cring.wait_group(e)
        rwt.wait_group(e, 7)
        rxi.wait_group(e, 7)
        for tt in range(NTT):
            if tt >= 2:
                e.wait_ge(pev, tt - 1)
            for kc in range(8):
                for c4 in range(4):
                    inst = e.matmul(
                        ps_o[tt % 2][:, c4 * 512:(c4 + 1) * 512],
                        xi_sb[:, kc * TOK + tt * 128:kc * TOK + (tt + 1) * 128],
                        wo_fm[:, kc * D + c4 * 512:kc * D + (c4 + 1) * 512],
                        start=(kc == 0), stop=(kc == 7),
                        skip_group_check=True)
                    if kc == 7 and c4 == 3:
                        inst.then_inc(pmd, 1)

    @p.on("scalar")
    def _(e):
        for tt in range(NTT):
            e.wait_ge(pmd, tt + 1)
            if tt >= 2:
                rst.wait(e, tt - 2)
            e.activation(so[tt % 2][:], ps_o[tt % 2][:], AF.Identity
                         ).then_inc(pev, 1)



def phase_moe(p, xn2_in, gate_w, w1, w3, w2, s2_in, x1_in, consts, cring,
              x2_out):
    E = 16
    I = 1408
    NI = I // 128
    NTT = TOK // 128
    NF = D // 128
    TH = TOK // 512

    rxt = p.ring(1)
    rgw = p.ring(1)
    rw13 = p.ring(4)
    rsx = p.ring(4)
    rst = p.ring(4)
    lmd = p.sem()
    lev = p.sem()
    sgm = p.sem()
    cmb = p.sem()
    gmd = p.sem()
    gev = p.sem()
    gud = p.sem()
    ymd = p.sem()
    yac = p.sem()
    sil = p.sem()
    fin = p.sem()

    xn2_dm = p.sb("mo_xn", [128, NF * TOK], BF16)
    gate_sb = p.sb("mo_gw", [128, NF * 16], BF16)
    logit = p.sb("mo_lg", [128, NTT * 16], F32)
    m1 = p.sb("mo_m1", [128, 1], F32)
    m2 = p.sb("mo_m2", [128, 1], F32)
    d12 = p.sb("mo_d12", [128, 1], F32)
    c1 = p.sb("mo_c1", [128, 1], F32)
    c2 = p.sb("mo_c2", [128, 1], F32)
    msk1 = p.sb("mo_mk1", [128, 16], F32)
    tmpl = p.sb("mo_tmp", [128, 16], F32)
    comb = p.sb("mo_cmb", [128, NTT * 16], F32)
    we_sb = p.sb("mo_we", [128, NF * I], BF16)
    sg = p.sb("mo_sg", [128, NI * TOK], BF16)
    up = p.sb("mo_up", [128, NI * TOK], BF16)
    moe = p.sb("mo_moe", [128, NTT * D], F32)
    s2t = p.sb("mo_s2", [128, D], F32)
    x2t = p.sb("mo_x2", [128, D], F32)
    ps_l = p.ps("mo_pl", [128, 16], F32)
    ps_g = [p.ps("mo_pg", [128, 512], F32) for _ in range(2)]
    ps_y = p.ps("mo_py", [128, D], F32)

    W13_PE = 43

    @p.on("sync")
    def _(e):
        for c in range(NF):
            rxt.push(e.dma_start(xn2_dm[:, c * TOK:(c + 1) * TOK],
                                 xn2_in[:, c * 128:(c + 1) * 128],
                                 transpose=True))
        for tt in range(NTT):
            if tt >= 1:
                e.wait_ge(fin, tt)
            rsx.push(e.dma_start(s2t[:], s2_in[tt * 128:(tt + 1) * 128, :]))
            e.wait_ge(sil, tt + 1)
            rsx.push(e.dma_start(moe[:, tt * D:(tt + 1) * D],
                                 x1_in[tt * 128:(tt + 1) * 128, :]))

    @p.on("gpsimd")
    def _(e):
        rgw.push(e.dma_start(
            gate_sb[:].rearrange("p (a c) -> p a c", a=NF),
            gate_w.rearrange("(a p) c -> p a c", p=128)))
        for ex in range(E):
            if ex >= 1:
                e.wait_ge(ymd, ex * NTT)
            for kc in range(NF):
                rw13.push(e.dma_start(we_sb[:, kc * I:(kc + 1) * I],
                                      w1[ex, kc * 128:(kc + 1) * 128, :]))
            e.wait_ge(gmd, (2 * ex + 1) * NI * TH)
            for kc in range(NF):
                rw13.push(e.dma_start(we_sb[:, kc * I:(kc + 1) * I],
                                      w3[ex, kc * 128:(kc + 1) * 128, :]))
            e.wait_ge(gmd, (2 * ex + 2) * NI * TH)
            w2v = we_sb[:].rearrange("p (a c) -> p a c", a=NI, c=2048)
            for ic in range(NI):
                rw13.push(e.dma_start(w2v[:, ic, :],
                                      w2[ex, ic * 128:(ic + 1) * 128, :]))
        for tt in range(NTT):
            e.wait_ge(fin, tt + 1)
            rst.push(e.dma_start(x2_out[tt * 128:(tt + 1) * 128, :],
                                 x2t[:]))
        rst.wait_group(e)

    @p.on("tensor")
    def _(e):
        cring.wait_group(e)
        rxt.wait_group(e, NF - 1)
        rgw.wait_group(e, 0)
        for tt in range(NTT):
            if tt >= 1:
                e.wait_ge(lev, tt)
            for kc in range(NF):
                inst = e.matmul(ps_l[:],
                                xn2_dm[:, kc * TOK + tt * 128:
                                       kc * TOK + (tt + 1) * 128],
                                gate_sb[:, kc * 16:(kc + 1) * 16],
                                start=(kc == 0), stop=(kc == NF - 1))
                if kc == NF - 1:
                    inst.then_inc(lmd, 1)
        for ex in range(E):
            for half in range(2):
                rw13.wait_group(e, ex * W13_PE + (half + 1) * NF - 1)
                for ic in range(NI):
                    for th in range(TH):
                        n = (2 * ex + half) * NI * TH + ic * TH + th
                        if n >= 2:
                            e.wait_ge(gev, n - 1)
                        for kc in range(NF):
                            inst = e.matmul(
                                ps_g[n % 2][:],
                                we_sb[:, kc * I + ic * 128:
                                      kc * I + (ic + 1) * 128],
                                xn2_dm[:, kc * TOK + th * 512:
                                       kc * TOK + (th + 1) * 512],
                                start=(kc == 0), stop=(kc == NF - 1))
                            if kc == NF - 1:
                                inst.then_inc(gmd, 1)
            rw13.wait_group(e, ex * W13_PE + W13_PE - 1)
            e.wait_ge(gud, ex * NI + NI)
            for tt in range(NTT):
                yg = ex * NTT + tt
                e.wait_ge(yac, yg + 1)
                for ic in range(NI):
                    for c4 in range(4):
                        inst = e.matmul(
                            ps_y[:, c4 * 512:(c4 + 1) * 512],
                            up[:, ic * TOK + tt * 128:ic * TOK + (tt + 1) * 128],
                            we_sb[:, ic * 2048 + c4 * 512:
                                  ic * 2048 + (c4 + 1) * 512],
                            start=(ic == 0), stop=(ic == NI - 1),
                            skip_group_check=True)
                        if ic == NI - 1 and c4 == 3:
                            inst.then_inc(ymd, 1)

    @p.on("scalar")
    def _(e):
        for tt in range(NTT):
            e.wait_ge(lmd, tt + 1)
            e.activation(logit[:, tt * 16:(tt + 1) * 16], ps_l[:],
                         AF.Identity).then_inc(lev, 1)
        for tt in range(NTT):
            e.wait_ge(sgm, 3 * tt + 1)
            e.chain_inc(e.activation(c1[:], d12[:], AF.Sigmoid), sgm)
            e.chain_inc(e.activation(c2[:], d12[:], AF.Sigmoid,
                                     scale=-1.0), sgm)
        for ex in range(E):
            for half in range(2):
                for ic in range(NI):
                    for th in range(TH):
                        n = (2 * ex + half) * NI * TH + ic * TH + th
                        e.wait_ge(gmd, n + 1)
                        if half == 0:
                            e.wait_ge(gud, max(0, (ex - 1) * NI + ic + 1))
                            e.activation(sg[:, ic * TOK + th * 512:
                                            ic * TOK + (th + 1) * 512],
                                         ps_g[n % 2][:], AF.Silu
                                         ).then_inc(gev, 1)
                        else:
                            e.wait_ge(ymd, ex * NTT)
                            e.activation(up[:, ic * TOK + th * 512:
                                            ic * TOK + (th + 1) * 512],
                                         ps_g[n % 2][:], AF.Identity
                                         ).then_inc(gev, 1)
        for tt in range(NTT):
            e.wait_ge(yac, 1 + (E - 1) * NTT + tt + 1)
            e.wait_ge(fin, tt)
            if tt >= 1:
                rst.wait(e, tt - 1)
            e.activation(x2t[:], moe[:, tt * D:(tt + 1) * D], AF.Silu
                         ).then_inc(sil, 1)

    @p.on("vector")
    def _(e):
        for tt in range(NTT):
            e.wait_ge(lev, tt + 1)
            lg = logit[:, tt * 16:(tt + 1) * 16]
            e.chain(e.reduce_max(m1[:], lg, axis=AX.X))
            e.chain(e.tensor_scalar(msk1[:], lg, m1[:], None,
                                    ALU.is_equal, ALU.bypass))
            e.chain(e.scalar_tensor_tensor(tmpl[:], msk1[:], -1e30, lg,
                                           ALU.mult, ALU.add))
            e.chain(e.reduce_max(m2[:], tmpl[:], axis=AX.X))
            e.chain_inc(e.tensor_sub(d12[:], m1[:], m2[:]), sgm)
            e.wait_ge(sgm, 3 * tt + 3)
            e.chain(e.tensor_scalar(comb[:, tt * 16:(tt + 1) * 16],
                                    msk1[:], c1[:], None, ALU.mult,
                                    ALU.bypass))
            e.chain(e.tensor_scalar(tmpl[:], tmpl[:], m2[:], None,
                                    ALU.is_equal, ALU.bypass))
            e.chain_inc(e.scalar_tensor_tensor(
                comb[:, tt * 16:(tt + 1) * 16], tmpl[:], c2[:],
                comb[:, tt * 16:(tt + 1) * 16], ALU.mult, ALU.add), cmb)
        e.chain(e.memset(moe[:], 0.0))
        e.nop().then_inc(yac, 1)
        for ex in range(E):
            for ic in range(NI):
                for th in range(TH):
                    ng = (2 * ex + 1) * NI * TH + ic * TH + th
                    e.wait_ge(gev, ng + 1)
                e.chain_inc(e.tensor_mul(up[:, ic * TOK:(ic + 1) * TOK],
                                         sg[:, ic * TOK:(ic + 1) * TOK],
                                         up[:, ic * TOK:(ic + 1) * TOK]),
                            gud)
            for tt in range(NTT):
                yg = ex * NTT + tt
                e.wait_ge(ymd, yg + 1)
                e.wait_ge(cmb, tt + 1)
                for c4 in range(4):
                    inst = e.scalar_tensor_tensor(
                        moe[:, tt * D + c4 * 512:tt * D + (c4 + 1) * 512],
                        ps_y[:, c4 * 512:(c4 + 1) * 512],
                        comb[:, tt * 16 + ex:tt * 16 + ex + 1],
                        moe[:, tt * D + c4 * 512:tt * D + (c4 + 1) * 512],
                        ALU.mult, ALU.add)
                    if c4 == 3:
                        e.chain_inc(inst, yac)
                    else:
                        e.chain(inst)
        for tt in range(NTT):
            e.wait_ge(sil, tt + 1)
            rsx.wait(e, 2 * tt)
            rsx.wait(e, 2 * tt + 1)
            e.chain(e.tensor_mul(x2t[:], x2t[:], s2t[:]))
            e.chain_inc(e.tensor_add(x2t[:], x2t[:],
                                     moe[:, tt * D:(tt + 1) * D]), fin)



def phase_kv_full(p, xn_full, w_attn, tabk, consts, cring, k_out, v_out):
    NF = D // 128
    NB = S // TOK
    TH = TOK // 512
    perm = consts["permsign"]

    rxn = p.ring(2, per_slot=D // 128)
    rtab = p.ring(2, per_slot=2)
    rwk = p.ring(1)
    rwv = p.ring(1)
    rstk = p.ring(4)
    rstv = p.ring(4)
    ped = p.sem()
    rotd = p.sem()
    qrd = p.sem()
    rpd = p.sem()
    pvd = p.sem()
    vevd = p.sem()
    xnc = p.sem()

    xn_dm = [p.sb("kvf_xn", [128, NF * TOK], BF16) for _ in range(2)]
    wk_sb = p.sb("kvf_wk", [128, NF * 512], BF16)
    wv_sb = p.sb("kvf_wv", [128, NF * 2048], BF16)
    ck = [p.sb("kvf_ck", [128, TOK], F32) for _ in range(2)]
    sk = [p.sb("kvf_sk", [128, TOK], F32) for _ in range(2)]
    qraw = [p.sb("kvf_qraw", [128, 512], F32) for _ in range(2)]
    t1 = p.sb("kvf_t1", [128, 512], F32)
    t2 = p.sb("kvf_t2", [128, 512], F32)
    qo = [p.sb("kvf_qo", [128, 512], BF16) for _ in range(2)]
    vo = [p.sb("kvf_vo", [128, 512], BF16) for _ in range(2)]
    pqk = [p.ps("kvf_pqk", [128, 512], F32) for _ in range(2)]
    prot = [p.ps("kvf_prot", [128, 512], F32) for _ in range(2)]
    pv = [p.ps("kvf_pv", [128, 512], F32) for _ in range(2)]

    KC = 4
    NV = (TOK // 128) * 4

    @p.on("sync")
    def _(e):
        for tb in range(NB):
            if tb >= 2:
                e.wait_ge(xnc, tb - 1)
            for c in range(NF):
                rxn.push_slot(e.dma_start(
                    xn_dm[tb % 2][:, c * TOK:(c + 1) * TOK],
                    xn_full[tb * TOK:(tb + 1) * TOK, c * 128:(c + 1) * 128],
                    transpose=True), tb)
            rtab.push_slot(e.dma_start(
                ck[tb % 2][:], tabk["cosk"][:, tb * TOK:(tb + 1) * TOK]), tb)
            rtab.push_slot(e.dma_start(
                sk[tb % 2][:], tabk["sink"][:, tb * TOK:(tb + 1) * TOK]), tb)
            for c in range(KC):
                for h in range(TH):
                    n = (tb * KC + c) * TH + h
                    e.wait_ge(rpd, n + 1)
                    rstk.push(e.dma_start(
                        k_out[c * 128:(c + 1) * 128,
                              tb * TOK + h * 512:tb * TOK + (h + 1) * 512],
                        qo[n % 2][:]))
            for tc in range(TOK // 128):
                for vc in range(4):
                    n = tb * NV + 4 * tc + vc
                    e.wait_ge(vevd, n + 1)
                    rstv.push(e.dma_start(
                        v_out[tb * TOK + tc * 128:tb * TOK + (tc + 1) * 128,
                              vc * 512:(vc + 1) * 512],
                        vo[n % 2][:]))
        rstk.wait_group(e)
        rstv.wait_group(e)

    @p.on("gpsimd")
    def _(e):
        rwk.push(e.dma_start(
            wk_sb[:].rearrange("p (a c) -> p a c", a=NF),
            w_attn[:, 2048:2560].rearrange("(a p) c -> p a c", p=128)))
        rwv.push(e.dma_start(
            wv_sb[:].rearrange("p (a c) -> p a c", a=NF),
            w_attn[:, 2560:4608].rearrange("(a p) c -> p a c", p=128)))

    @p.on("tensor")
    def _(e):
        cring.wait_group(e)
        rwk.wait_group(e, 0)
        rwv.wait_group(e, 0)
        for tb in range(NB):
            rxn.wait_slot(e, tb)
            rtab.wait_slot(e, tb)
            for c in range(KC):
                for h in range(TH):
                    n = (tb * KC + c) * TH + h
                    if n >= 2:
                        e.wait_ge(qrd, n - 1)
                    for kc in range(NF):
                        inst = e.matmul(
                            pqk[n % 2][:],
                            wk_sb[:, kc * 512 + c * 128:kc * 512 + (c + 1) * 128],
                            xn_dm[tb % 2][:, kc * TOK + h * 512:
                                          kc * TOK + h * 512 + 512],
                            start=(kc == 0), stop=(kc == NF - 1))
                        if kc == NF - 1:
                            inst.then_inc(ped, 1)
                    e.wait_ge(qrd, n + 1)
                    if n >= 2:
                        e.wait_ge(rpd, n - 1)
                    e.matmul(prot[n % 2][:], perm[:], qraw[n % 2][:],
                             start=True, stop=True).then_inc(rotd, 1)
            for tc in range(TOK // 128):
                for vc in range(4):
                    n = tb * NV + 4 * tc + vc
                    if n >= 2:
                        e.wait_ge(vevd, n - 1)
                    for kc in range(NF):
                        inst = e.matmul(
                            pv[n % 2][:],
                            xn_dm[tb % 2][:, kc * TOK + tc * 128:
                                          kc * TOK + (tc + 1) * 128],
                            wv_sb[:, kc * 2048 + vc * 512:
                                  kc * 2048 + (vc + 1) * 512],
                            start=(kc == 0), stop=(kc == NF - 1))
                        if kc == NF - 1:
                            inst.then_inc(pvd, 1)
            e.nop().then_inc(xnc, 1)

    @p.on("scalar")
    def _(e):
        for tb in range(NB):
            for c in range(KC):
                for h in range(TH):
                    n = (tb * KC + c) * TH + h
                    e.wait_ge(ped, n + 1)
                    if n >= 2:
                        e.wait_ge(rotd, n - 1)
                        e.wait_ge(rpd, n - 1)
                    e.activation(qraw[n % 2][:], pqk[n % 2][:], AF.Identity
                                 ).then_inc(qrd, 1)
            for tc in range(TOK // 128):
                for vc in range(4):
                    n = tb * NV + 4 * tc + vc
                    e.wait_ge(pvd, n + 1)
                    if n >= 2:
                        rstv.wait(e, n - 2)
                    e.activation(vo[n % 2][:], pv[n % 2][:], AF.Identity
                                 ).then_inc(vevd, 1)

    @p.on("vector")
    def _(e):
        for tb in range(NB):
            for c in range(KC):
                for h in range(TH):
                    n = (tb * KC + c) * TH + h
                    e.wait_ge(qrd, n + 1)
                    e.chain(e.tensor_mul(
                        t1[:], qraw[n % 2][:],
                        ck[tb % 2][:, h * 512:(h + 1) * 512]))
                    e.wait_ge(rotd, n + 1)
                    e.chain(e.tensor_mul(
                        t2[:], prot[n % 2][:],
                        sk[tb % 2][:, h * 512:(h + 1) * 512]))
                    if n >= 2:
                        rstk.wait(e, n - 2)
                    e.chain_inc(e.tensor_add(qo[n % 2][:], t1[:], t2[:]), rpd)


def phase_absum_full(p, w1, w2, out):
    r1 = p.ring(4)
    red = p.sem()
    fin = p.sem()
    rst = p.ring(1)
    ro = p.ring(1)

    wbuf = [p.sb("abf_w", [128, 8192], F32) for _ in range(2)]
    rcol = p.sb("abf_r", [128, 1], F32)
    acc = p.sb("abf_acc", [128, 2], F32)
    onec = p.sb("abf_one", [128, 1], F32)
    osb = p.sb("abf_o", [2, 1], F32)
    pso = p.ps("abf_ps", [2, 1], F32)
    ones_in = p.din("ones_col_f32_ab", [128, 1], F32)

    chunks = [(w1[i * 128:(i + 1) * 128, :], 2048, 0) for i in range(64)] + \
             [(w2[i * 128:(i + 1) * 128, :], 8192, 1) for i in range(16)]

    @p.on("sync")
    def _(e):
        ro.push(e.dma_start(onec[:], ones_in))
        for i, (src, w, col) in enumerate(chunks):
            if i >= 2:
                e.wait_ge(red, i - 1)
            r1.push(e.dma_start(wbuf[i % 2][:, 0:w], src))

    @p.on("vector")
    def _(e):
        e.chain(e.memset(acc[:], 0.0))
        for i, (src, w, col) in enumerate(chunks):
            r1.wait(e, i)
            e.chain(e.reduce_sum(rcol[:], wbuf[i % 2][:, 0:w], axis=AX.X,
                                 apply_absolute_value=True))
            e.chain_inc(e.tensor_add(acc[:, col:col + 1], acc[:, col:col + 1],
                                     rcol[:]), red)
        e.nop().then_inc(red, 1)

    @p.on("tensor")
    def _(e):
        e.wait_ge(red, len(chunks) + 1)
        ro.wait_group(e)
        e.matmul(pso[:], acc[:], onec[:], start=True, stop=True
                 ).then_inc(fin, 1)

    @p.on("scalar")
    def _(e):
        e.wait_ge(fin, 1)
        e.activation(osb[:], pso[:], AF.Identity).then_inc(fin, 1)

    @p.on("gpsimd")
    def _(e):
        e.wait_ge(fin, 2)
        rst.push(e.dma_start(out, osb[:]))
        rst.wait_group(e)


def phase_ffn_thr2(p, absums, consts, cring, thr_bc, nthr_bc):
    ident = consts["ident_f32"]
    ones_row = consts["ones_row_f32"]
    ra = p.ring(1)
    ms = p.sem()

    tsb = p.sb("t2_t", [2, 1], F32)
    asb = p.sb("t2_a", [2, 1], F32)
    trow = p.sb("t2_r", [1, 2], F32)
    pst = p.ps("t2_p", [128, 2], F32)

    @p.on("sync")
    def _(e):
        ra.push(e.dma_start(asb[:], absums))

    @p.on("tensor")
    def _(e):
        cring.wait_group(e)
        e.wait_ge(ms, 1)
        e.matmul(pst[0:1, 0:2], tsb[:], ident[0:2, 0:2],
                 start=True, stop=True).then_inc(ms, 1)
        e.wait_ge(ms, 3)
        e.matmul(pst[:, 0:2], ones_row[:], trow[:],
                 start=True, stop=True).then_inc(ms, 1)

    @p.on("scalar")
    def _(e):
        ra.wait_group(e)
        e.activation(tsb[:], asb[:], AF.Identity,
                     scale=0.5 / (8192.0 * 2048.0)).then_inc(ms, 1)
        e.wait_ge(ms, 2)
        e.activation(trow[:], pst[0:1, 0:2], AF.Identity).then_inc(ms, 1)
        e.wait_ge(ms, 4)
        e.chain(e.activation(thr_bc[:], pst[:, 0:2], AF.Identity))
        e.chain_inc(e.activation(nthr_bc[:], thr_bc[:], AF.Identity,
                                 scale=-1.0), ms)


def phase_memkv_full(p, mem, g_mem, b_mem, wk, wv, log_beta, consts, cring,
                     kb_out, v_out):
    Md = 1024
    NR = 2048
    ones_cf = consts["ones_col_f32"]

    mn_d = p.dtmp("mkf_mn", [NR, Md], BF16)
    m0 = p.scope()
    phase_lnmem(p, mem, mn_d, NR, Md)
    p.barrier()
    p.pop_scope(m0)

    rgb = p.ring(1)
    grow = p.sb("mkf_gr", [1, Md], F32)
    brow = p.sb("mkf_br", [1, Md], F32)
    lbrow = p.sb("mkf_lb", [1, 8], F32)
    betar = p.sb("mkf_be", [1, 8], F32)
    beta_bc = p.sb("mkf_bbc", [128, 8], F32)

    @p.on("sync")
    def _(e):
        rgb.push(e.dma_start(grow[:], g_mem))
        rgb.push(e.dma_start(brow[:], b_mem))
        rgb.push(e.dma_start(lbrow[:], log_beta))

    @p.on("tensor")
    def _(e):
        rgb.wait_group(e, 2)
        cring.wait_group(e)
    gcol, gdn = emit_cols_from_row(p, grow, 8, ones_cf, "mkf_gc")
    bcol, bdn = emit_cols_from_row(p, brow, 8, ones_cf, "mkf_bc")

    wkv_bf = p.dtmp("mkf_wkv", [2 * Md, Md], BF16)
    rwc = p.ring(2)
    rwt = p.ring(1)
    rmt = p.ring(1)
    rstk = p.ring(4)
    rstv = p.ring(4)
    bbd = p.sem()
    gbd = p.sem()
    kmd = p.sem()
    kev = p.sem()
    vmd = p.sem()
    vev = p.sem()

    wk_fm = p.sb("mkf_wkfm", [128, 8 * Md], BF16)
    wv_fm = p.sb("mkf_wvfm", [128, 8 * Md], BF16)
    mn_dm = p.sb("mkf_mndm", [128, 8 * NR], BF16)
    kbo = [p.sb("mkf_kbo", [128, 512], BF16) for _ in range(2)]
    vo = [p.sb("mkf_vo", [128, 512], BF16) for _ in range(2)]
    ps_k = [p.ps("mkf_psk", [128, 512], F32) for _ in range(2)]
    ps_v = [p.ps("mkf_psv", [128, 512], F32) for _ in range(2)]
    ps_b = p.ps("mkf_psb", [128, 8], F32)

    NKG = 8 * 4
    NVG = 16 * 2

    @p.on("gpsimd")
    def _(e):
        rwc.push(e.dma_start(wkv_bf[0:Md, :], wk))
        rwc.push(e.dma_start(wkv_bf[Md:2 * Md, :], wv))
        for g in range(NKG):
            oc, nb = divmod(g, 4)
            e.wait_ge(kev, g + 1)
            rstk.push(e.dma_start(
                kb_out[oc * 128:(oc + 1) * 128, nb * 512:(nb + 1) * 512],
                kbo[g % 2][:]))
        for g in range(NVG):
            nc_, th = divmod(g, 2)
            e.wait_ge(vev, g + 1)
            rstv.push(e.dma_start(
                v_out[nc_ * 128:(nc_ + 1) * 128, th * 512:(th + 1) * 512],
                vo[g % 2][:]))
        rstk.wait_group(e)
        rstv.wait_group(e)

    @p.on("sync")
    def _(e):
        rwc.wait_group(e, 1)
        for c in range(8):
            rwt.push(e.dma_start(wk_fm[:, c * Md:(c + 1) * Md],
                                 wkv_bf[0:Md, c * 128:(c + 1) * 128],
                                 transpose=True))
        for c in range(8):
            rwt.push(e.dma_start(wv_fm[:, c * Md:(c + 1) * Md],
                                 wkv_bf[Md:2 * Md, c * 128:(c + 1) * 128],
                                 transpose=True))
        for c in range(8):
            rmt.push(e.dma_start(mn_dm[:, c * NR:(c + 1) * NR],
                                 mn_d[:, c * 128:(c + 1) * 128],
                                 transpose=True))

    @p.on("vector")
    def _(e):
        rmt.wait_group(e, 7)
        e.wait_ge(gdn, 2)
        e.wait_ge(bdn, 2)
        for c in range(8):
            e.chain(e.tensor_scalar(mn_dm[:, c * NR:(c + 1) * NR],
                                    mn_dm[:, c * NR:(c + 1) * NR],
                                    gcol[:, c:c + 1], bcol[:, c:c + 1],
                                    ALU.mult, ALU.add))
        e.nop().then_inc(gbd, 1)
        for g in range(NKG):
            oc = g // 4
            e.wait_ge(kmd, g + 1)
            if g >= 2:
                rstk.wait(e, g - 2)
            e.chain_inc(e.tensor_scalar(kbo[g % 2][:], ps_k[g % 2][:],
                                        beta_bc[:, oc:oc + 1], None,
                                        ALU.mult, ALU.bypass), kev)

    @p.on("scalar")
    def _(e):
        rgb.wait_group(e, 2)
        e.activation(betar[:], lbrow[:], AF.Exp).then_inc(bbd, 1)
        e.wait_ge(bbd, 2)
        e.activation(beta_bc[:], ps_b[:], AF.Identity).then_inc(bbd, 1)
        for g in range(NVG):
            e.wait_ge(vmd, g + 1)
            if g >= 2:
                rstv.wait(e, g - 2)
            e.activation(vo[g % 2][:], ps_v[g % 2][:], AF.Identity
                         ).then_inc(vev, 1)

    @p.on("tensor")
    def _(e):
        e.wait_ge(bbd, 1)
        e.matmul(ps_b[:], consts["ones_row_f32"][:], betar[:],
                 start=True, stop=True).then_inc(bbd, 1)
        rwt.wait_group(e, 15)
        e.wait_ge(gbd, 1)
        e.wait_ge(bbd, 3)
        for g in range(NKG):
            oc, nb = divmod(g, 4)
            if g >= 2:
                e.wait_ge(kev, g - 1)
            for kc in range(8):
                inst = e.matmul(
                    ps_k[g % 2][:],
                    wk_fm[:, kc * Md + oc * 128:kc * Md + (oc + 1) * 128],
                    mn_dm[:, kc * NR + nb * 512:kc * NR + (nb + 1) * 512],
                    start=(kc == 0), stop=(kc == 7))
                if kc == 7:
                    inst.then_inc(kmd, 1)
        for g in range(NVG):
            nc_, th = divmod(g, 2)
            if g >= 2:
                e.wait_ge(vev, g - 1)
            for kc in range(8):
                inst = e.matmul(
                    ps_v[g % 2][:],
                    mn_dm[:, kc * NR + nc_ * 128:kc * NR + (nc_ + 1) * 128],
                    wv_fm[:, kc * Md + th * 512:kc * Md + (th + 1) * 512],
                    start=(kc == 0), stop=(kc == 7))
                if kc == 7:
                    inst.then_inc(vmd, 1)



def build_pre():
    p = Prog()
    x_in = p.din("x", [TOK, D], F32)
    w_attn = p.din("w_attn", [D, 4608], F32)
    tabs = {t: p.din(t, [128, TOK], F32)
            for t in ("cosq", "sinq", "cosk", "sink")}
    xn_out = p.dout("xn", [TOK, D], BF16)
    q_out = p.dout("q", [2048, TOK], BF16)
    k_out = p.dout("k", [512, TOK], BF16)
    v_out = p.dout("v", [TOK, 2048], BF16)
    consts, cring = load_consts(p, ["permsign"])
    m = p.scope()
    phase_ln(p, x_in, xn_out)
    p.barrier()
    p.pop_scope(m)
    phase_qkv(p, xn_out, w_attn, tabs, consts, cring, q_out, k_out, v_out)
    return p.finalize()


def build_attn():
    p = Prog()
    q_in = p.din("q", [2048, TOK], BF16)
    k_in = p.din("k", [512, S], BF16)
    v_in = p.din("v", [S, 2048], BF16)
    ctx_out = p.dout("ctx", [8192, TOK], BF16)
    consts, cring = load_consts(p, ["ones_col_bf", "ones_row_f32"])
    phase_attn(p, q_in, k_in, v_in, consts, cring, ctx_out)
    return p.finalize()


def build_wout():
    p = Prog()
    ctx_in = p.din("ctx", [8192, TOK], BF16)
    w_out = p.din("w_out", [8192, D], F32)
    b_out = p.din("b_out", [1, D], F32)
    s_out = p.dout("s", [TOK, D], F32)
    consts, cring = load_consts(p, ["ones_row_f32"])
    phase_wout(p, ctx_in, w_out, b_out, consts, cring, s_out)
    return p.finalize()


def build_xpre2():
    p = Prog()
    x1_in = p.din("x1", [TOK, D], F32)
    g_state = p.din("g_state", [1, D], F32)
    b_state = p.din("b_state", [1, D], F32)
    wq = p.din("wq", [1024, D], F32)
    xn2_out = p.dout("xn2", [TOK, D], BF16)
    xi0_out = p.dout("xi0", [1024, TOK], BF16)
    tq = p.dtmp("tq", [TOK, D], BF16)
    consts, cring = load_consts(p, ["ones_col_f32", "ones_row_f32"])
    m = p.scope()
    phase_ln(p, x1_in, xn2_out)
    p.barrier()
    p.pop_scope(m)
    m = p.scope()
    phase_ln2_from_bf(p, xn2_out, tq)
    p.barrier()
    p.pop_scope(m)
    phase_qproj(p, tq, g_state, b_state, wq, consts, cring, xi0_out)
    return p.finalize()


def build_memkv():
    p = Prog()
    mem_s = p.din("mem", [2048 // NCORE, 1024], F32)
    g_mem = p.din("g_mem", [1, 1024], F32)
    b_mem = p.din("b_mem", [1, 1024], F32)
    wk = p.din("wk", [1024, 1024], F32)
    wv = p.din("wv", [1024, 1024], F32)
    log_beta = p.din("log_beta", [1, 8], F32)
    kb_out = p.dout("kb", [1024, 2048 // NCORE], BF16)
    v_out = p.dout("v", [2048 // NCORE, 1024], BF16)
    consts, cring = load_consts(p, ["ones_col_f32", "ones_row_f32"])
    phase_memkv(p, mem_s, g_mem, b_mem, wk, wv, log_beta, consts, cring,
                kb_out, v_out)
    return p.finalize()


def build_entmax(NSTEP=3, NH=8):
    p = Prog()
    xi0 = p.din("xi0", [1024, TOK], BF16)
    kb = p.din("kb", [1024, 2048], BF16)
    v = p.din("v", [2048, 1024], BF16)
    xi3 = p.dout("xi3", [1024, TOK], BF16)
    consts, cring = load_consts(p, ["ident"])
    phase_entmax(p, xi0, kb, v, consts, cring, xi3, NSTEP=NSTEP, NH=NH)
    return p.finalize()


def build_womoe():
    p = Prog()
    xi3 = p.din("xi3", [1024, TOK], BF16)
    wo = p.din("wo", [D, 1024], F32)
    xn2 = p.din("xn2", [TOK, D], BF16)
    gate_w = p.din("gate_w", [D, 16], F32)
    w1 = p.din("w1", [16, D, 1408], F32)
    w3 = p.din("w3", [16, D, 1408], F32)
    w2 = p.din("w2", [16, 1408, D], F32)
    x1 = p.din("x1", [TOK, D], F32)
    x2 = p.dout("x2", [TOK, D], F32)
    s2_d = p.dtmp("s2", [TOK, D], F32)
    consts, cring = load_consts(p, [])
    m = p.scope()
    phase_wo(p, xi3, wo, consts, cring, s2_d)
    p.barrier()
    p.pop_scope(m)
    phase_moe(p, xn2, gate_w, w1, w3, w2, s2_d, x1, consts, cring, x2)
    return p.finalize()


def build_absum():
    p = Prog()
    w1s = p.din("w1s", [8192 // NCORE, D], F32)
    w2s = p.din("w2s", [D // NCORE, 8192], F32)
    out = p.dout("absum", [1, 2], F32)
    phase_absum(p, w1s, w2s, out)
    return p.finalize()


def build_ffn():
    p = Prog()
    xn = p.din("xn", [TOK, D], BF16)
    w1 = p.din("w1", [8192, D], F32)
    b1 = p.din("b1", [1, 8192], F32)
    w2 = p.din("w2", [D, 8192], F32)
    b2 = p.din("b2", [1, D], F32)
    absums = p.din("absums", [8, 2], F32)
    s_in = p.din("s", [TOK, D], F32)
    x_in = p.din("x", [TOK, D], F32)
    x1_out = p.dout("x1", [TOK, D], F32)
    h_dram = p.dtmp("h", [8192, TOK], BF16)
    w2q_dram = p.dtmp("w2q", [8192, D], BF16)
    consts, cring = load_consts(
        p, ["ident", "ident_f32", "ones_row_f32", "ones_col_f32"])
    thr_bc = p.sb("thr", [128, 2], F32)
    nthr_bc = p.sb("nthr", [128, 2], F32)
    m = p.scope()
    phase_ffn_thr(p, absums, consts, cring, thr_bc, nthr_bc)
    p.barrier()
    p.pop_scope(m)
    m = p.scope()
    phase_ffn_h(p, xn, w1, b1, thr_bc, nthr_bc, consts, cring, h_dram, None)
    p.barrier()
    p.pop_scope(m)
    m = p.scope()
    phase_ffn_w2q(p, w2, thr_bc, nthr_bc, consts, w2q_dram)
    p.barrier()
    p.pop_scope(m)
    phase_ffn_out(p, h_dram, w2q_dram, b2, s_in, x_in, consts, x1_out)
    return p.finalize()


def build_fused(bases=None):
    p = Prog(bases=bases)
    x_full = p.din("x_full", [S, D], F32)
    x_own = p.din("x_own", [TOK, D], F32)
    w_attn = p.din("w_attn", [D, 4608], F32)
    tabs = {t: p.din(t, [128, TOK], F32)
            for t in ("cosq", "sinq", "cosk", "sink")}
    tabkf = {t: p.din(t + "_full", [128, S], F32) for t in ("cosk", "sink")}
    w_out = p.din("w_out", [8192, D], F32)
    b_out = p.din("b_out", [1, D], F32)
    f_w1 = p.din("f_w1", [8192, D], F32)
    f_b1 = p.din("f_b1", [1, 8192], F32)
    f_w2 = p.din("f_w2", [D, 8192], F32)
    f_b2 = p.din("f_b2", [1, D], F32)
    wq = p.din("wq", [1024, D], F32)
    g_state = p.din("g_state", [1, D], F32)
    b_state = p.din("b_state", [1, D], F32)
    mem = p.din("mem", [2048, 1024], F32)
    g_mem = p.din("g_mem", [1, 1024], F32)
    b_mem = p.din("b_mem", [1, 1024], F32)
    wk = p.din("wk", [1024, 1024], F32)
    wv = p.din("wv", [1024, 1024], F32)
    wo = p.din("wo", [D, 1024], F32)
    log_beta = p.din("log_beta", [1, 8], F32)
    gate_w = p.din("gate_w", [D, 16], F32)
    m_w1 = p.din("m_w1", [16, D, 1408], F32)
    m_w3 = p.din("m_w3", [16, D, 1408], F32)
    m_w2 = p.din("m_w2", [16, 1408, D], F32)
    x2_out = p.dout("x2", [TOK, D], F32)

    xn_own = p.dtmp("f_xnown", [TOK, D], BF16)
    xn_full = p.dtmp("f_xnfull", [S, D], BF16)
    q_d = p.dtmp("f_q", [2048, TOK], BF16)
    k_d = p.dtmp("f_k", [512, S], BF16)
    v_d = p.dtmp("f_v", [S, 2048], BF16)
    ctx_d = p.dtmp("f_ctx", [8192, TOK], BF16)
    s_d = p.dtmp("f_s", [TOK, D], F32)
    ab_d = p.dtmp("f_ab", [2, 1], F32)
    h_d = p.dtmp("f_h", [8192, TOK], BF16)
    w2q_d = p.dtmp("f_w2q", [8192, D], BF16)
    x1_d = p.dtmp("f_x1", [TOK, D], F32)
    xn2_d = p.dtmp("f_xn2", [TOK, D], BF16)
    tq_d = p.dtmp("f_tq", [TOK, D], BF16)
    xi0_d = p.dtmp("f_xi0", [1024, TOK], BF16)
    kb_d = p.dtmp("f_kb", [1024, 2048], BF16)
    v2_d = p.dtmp("f_v2", [2048, 1024], BF16)
    xi3_d = p.dtmp("f_xi3", [1024, TOK], BF16)
    s2_d = p.dtmp("f_s2", [TOK, D], F32)

    consts, cring = load_consts(
        p, ["permsign", "ident", "ident_f32", "ones_col_bf",
            "ones_row_f32", "ones_col_f32"])
    thr_bc = p.sb("f_thr", [128, 2], F32)
    nthr_bc = p.sb("f_nthr", [128, 2], F32)

    def ph(f):
        m = p.scope()
        f()
        p.barrier(reset=True)
        p.pop_scope(m)

    ph(lambda: phase_ln(p, x_own, xn_own))
    ph(lambda: phase_ln(p, x_full, xn_full, T=S))
    ph(lambda: phase_qkv(p, xn_own, w_attn, tabs, consts, cring,
                         q_d, None, None, qonly=True))
    ph(lambda: phase_kv_full(p, xn_full, w_attn, tabkf, consts, cring,
                             k_d, v_d))
    ph(lambda: phase_attn(p, q_d, k_d, v_d, consts, cring, ctx_d))
    ph(lambda: phase_wout(p, ctx_d, w_out, b_out, consts, cring, s_d))
    ph(lambda: phase_absum_full(p, f_w1, f_w2, ab_d))
    ph(lambda: phase_ffn_thr2(p, ab_d, consts, cring, thr_bc, nthr_bc))
    ph(lambda: phase_ffn_h(p, xn_own, f_w1, f_b1, thr_bc, nthr_bc, consts,
                           cring, h_d, None))
    ph(lambda: phase_ffn_w2q(p, f_w2, thr_bc, nthr_bc, consts, w2q_d))
    ph(lambda: phase_ffn_out(p, h_d, w2q_d, f_b2, s_d, x_own, consts, x1_d))
    ph(lambda: phase_ln(p, x1_d, xn2_d))
    ph(lambda: phase_ln2_from_bf(p, xn2_d, tq_d))
    ph(lambda: phase_qproj(p, tq_d, g_state, b_state, wq, consts, cring,
                           xi0_d))
    ph(lambda: phase_memkv_full(p, mem, g_mem, b_mem, wk, wv, log_beta,
                                consts, cring, kb_d, v2_d))
    ph(lambda: phase_entmax(p, xi0_d, kb_d, v2_d, consts, cring, xi3_d))
    ph(lambda: phase_wo(p, xi3_d, wo, consts, cring, s2_d))
    m = p.scope()
    phase_moe(p, xn2_d, gate_w, m_w1, m_w3, m_w2, s2_d, x1_d, consts, cring,
              x2_out)
    nc = p.finalize()
    return nc, p.counts


def build_fused_2pass():
    _, counts = build_fused()
    names = {nm for (_, nm) in counts}
    maxep = max((ep for (ep, _) in counts), default=0)
    bases = {}
    for nm in names:
        run = 0
        for ep in range(maxep + 2):
            bases[(ep, nm)] = run
            run += counts.get((ep, nm), 0)
    nc, _ = build_fused(bases)
    return nc


def kernel_fused(**inputs):
    CORES = list(range(NCORE))
    consts = host_constants()
    x = np.ascontiguousarray(inputs["L0_attn_x"].reshape(S, D))
    nc = _get("fused", build_fused_2pass)
    im = []
    for c in CORES:
        d = {
            "x_full": x,
            "x_own": x[c * TOK:(c + 1) * TOK],
            "w_attn": np.ascontiguousarray(inputs["L0_attn_W_attn"]),
            "w_out": np.ascontiguousarray(inputs["L0_attn_W_out"]),
            "b_out": np.ascontiguousarray(
                inputs["L0_attn_b_out"].reshape(1, D)),
            "f_w1": np.ascontiguousarray(inputs["L0_ffn_w1"]),
            "f_b1": np.ascontiguousarray(inputs["L0_ffn_b1"].reshape(1, 8192)),
            "f_w2": np.ascontiguousarray(inputs["L0_ffn_w2"]),
            "f_b2": np.ascontiguousarray(inputs["L0_ffn_b2"].reshape(1, D)),
            "wq": np.ascontiguousarray(inputs["L1_attn_Wq"]),
            "g_state": np.ascontiguousarray(
                inputs["L1_attn_g_state"].reshape(1, D)),
            "b_state": np.ascontiguousarray(
                inputs["L1_attn_b_state"].reshape(1, D)),
            "mem": np.ascontiguousarray(inputs["L1_attn_memory"]),
            "g_mem": np.ascontiguousarray(
                inputs["L1_attn_g_mem"].reshape(1, 1024)),
            "b_mem": np.ascontiguousarray(
                inputs["L1_attn_b_mem"].reshape(1, 1024)),
            "wk": np.ascontiguousarray(inputs["L1_attn_Wk"]),
            "wv": np.ascontiguousarray(inputs["L1_attn_Wv"]),
            "wo": np.ascontiguousarray(inputs["L1_attn_Wo"]),
            "log_beta": np.ascontiguousarray(
                inputs["L1_attn_log_beta"].reshape(1, 8)),
            "gate_w": np.ascontiguousarray(inputs["L1_ffn_gate_w"]),
            "m_w1": np.ascontiguousarray(inputs["L1_ffn_w1"]),
            "m_w3": np.ascontiguousarray(inputs["L1_ffn_w3"]),
            "m_w2": np.ascontiguousarray(inputs["L1_ffn_w2"]),
            "ones_col_f32_ab": consts["ones_col_f32"],
            "cosk_full": consts["cosk"],
            "sink_full": consts["sink"],
        }
        for t in ("cosq", "sinq", "cosk", "sink"):
            d[t] = np.ascontiguousarray(
                consts[t][:, c * TOK:(c + 1) * TOK])
        for w in ("permsign", "ident", "ident_f32", "ones_col_bf",
                  "ones_row_f32", "ones_col_f32"):
            d[w] = consts[w]
        im.append(d)
    res = run_bass_kernel_spmd(nc, im, core_ids=CORES).results
    x2 = np.concatenate([r["x2"] for r in res], 0)
    return x2.reshape(1, S, D)



_BUILT = {}


def _get(name, fn):
    if name not in _BUILT:
        _BUILT[name] = fn()
    return _BUILT[name]


def kernel(**inputs):
    return kernel_fused(**inputs)


def kernel_unfused(**inputs):
    CORES = list(range(NCORE))
    consts = host_constants()

    x = np.ascontiguousarray(inputs["L0_attn_x"].reshape(S, D))
    w_attn = np.ascontiguousarray(inputs["L0_attn_W_attn"])
    w_out = np.ascontiguousarray(inputs["L0_attn_W_out"])
    b_out = np.ascontiguousarray(inputs["L0_attn_b_out"].reshape(1, D))
    f_w1 = np.ascontiguousarray(inputs["L0_ffn_w1"])
    f_b1 = np.ascontiguousarray(inputs["L0_ffn_b1"].reshape(1, 8192))
    f_w2 = np.ascontiguousarray(inputs["L0_ffn_w2"])
    f_b2 = np.ascontiguousarray(inputs["L0_ffn_b2"].reshape(1, D))
    mem = np.ascontiguousarray(inputs["L1_attn_memory"])
    wq = np.ascontiguousarray(inputs["L1_attn_Wq"])
    wk = np.ascontiguousarray(inputs["L1_attn_Wk"])
    wv = np.ascontiguousarray(inputs["L1_attn_Wv"])
    wo = np.ascontiguousarray(inputs["L1_attn_Wo"])
    log_beta = np.ascontiguousarray(inputs["L1_attn_log_beta"].reshape(1, 8))
    g_state = np.ascontiguousarray(inputs["L1_attn_g_state"].reshape(1, D))
    b_state = np.ascontiguousarray(inputs["L1_attn_b_state"].reshape(1, D))
    g_mem = np.ascontiguousarray(inputs["L1_attn_g_mem"].reshape(1, 1024))
    b_mem = np.ascontiguousarray(inputs["L1_attn_b_mem"].reshape(1, 1024))
    gate_w = np.ascontiguousarray(inputs["L1_ffn_gate_w"])
    m_w1 = np.ascontiguousarray(inputs["L1_ffn_w1"])
    m_w3 = np.ascontiguousarray(inputs["L1_ffn_w3"])
    m_w2 = np.ascontiguousarray(inputs["L1_ffn_w2"])

    def sl(c):
        return slice(c * TOK, (c + 1) * TOK)

    nc = _get("pre", build_pre)
    im = []
    for c in CORES:
        im.append({"x": x[sl(c)], "w_attn": w_attn,
                   "permsign": consts["permsign"],
                   **{t: np.ascontiguousarray(consts[t][:, sl(c)])
                      for t in ("cosq", "sinq", "cosk", "sink")}})
    res = run_bass_kernel_spmd(nc, im, core_ids=CORES).results
    xn_all = [r["xn"] for r in res]
    q_all = [r["q"] for r in res]
    k_full = np.ascontiguousarray(np.concatenate([r["k"] for r in res], 1))
    v_full = np.ascontiguousarray(np.concatenate([r["v"] for r in res], 0))

    nc = _get("attn", build_attn)
    im = [{"q": q_all[c], "k": k_full, "v": v_full,
           "ones_col_bf": consts["ones_col_bf"],
           "ones_row_f32": consts["ones_row_f32"]} for c in CORES]
    res = run_bass_kernel_spmd(nc, im, core_ids=CORES).results
    ctx_all = [r["ctx"] for r in res]

    nc = _get("wout", build_wout)
    im = [{"ctx": ctx_all[c], "w_out": w_out, "b_out": b_out,
           "ones_row_f32": consts["ones_row_f32"]} for c in CORES]
    res = run_bass_kernel_spmd(nc, im, core_ids=CORES).results
    s_all = [r["s"] for r in res]

    nc = _get("absum", build_absum)
    im = [{"w1s": f_w1[c * 1024:(c + 1) * 1024],
           "w2s": f_w2[c * 256:(c + 1) * 256],
           "ones_col_f32_ab": consts["ones_col_f32"]} for c in CORES]
    res = run_bass_kernel_spmd(nc, im, core_ids=CORES).results
    absums = np.ascontiguousarray(
        np.concatenate([r["absum"] for r in res], 0))

    nc = _get("ffn", build_ffn)
    im = [{"xn": xn_all[c], "w1": f_w1, "b1": f_b1, "w2": f_w2, "b2": f_b2,
           "absums": absums, "s": s_all[c], "x": x[sl(c)],
           "ident": consts["ident"], "ident_f32": consts["ident_f32"],
           "ones_row_f32": consts["ones_row_f32"],
           "ones_col_f32": consts["ones_col_f32"]} for c in CORES]
    res = run_bass_kernel_spmd(nc, im, core_ids=CORES).results
    x1_all = [r["x1"] for r in res]

    nc = _get("xpre2", build_xpre2)
    im = [{"x1": x1_all[c], "g_state": g_state, "b_state": b_state,
           "wq": wq, "ones_col_f32": consts["ones_col_f32"],
           "ones_row_f32": consts["ones_row_f32"]} for c in CORES]
    res = run_bass_kernel_spmd(nc, im, core_ids=CORES).results
    xn2_all = [r["xn2"] for r in res]
    xi0_all = [r["xi0"] for r in res]

    nc = _get("memkv", build_memkv)
    NR = 2048 // NCORE
    im = [{"mem": mem[c * NR:(c + 1) * NR], "g_mem": g_mem, "b_mem": b_mem,
           "wk": wk, "wv": wv, "log_beta": log_beta,
           "ones_col_f32": consts["ones_col_f32"],
           "ones_row_f32": consts["ones_row_f32"]} for c in CORES]
    res = run_bass_kernel_spmd(nc, im, core_ids=CORES).results
    kb_full = np.ascontiguousarray(np.concatenate([r["kb"] for r in res], 1))
    v2_full = np.ascontiguousarray(np.concatenate([r["v"] for r in res], 0))

    nc = _get("entmax", build_entmax)
    im = [{"xi0": xi0_all[c], "kb": kb_full, "v": v2_full,
           "ident": consts["ident"]} for c in CORES]
    res = run_bass_kernel_spmd(nc, im, core_ids=CORES).results
    xi3_all = [r["xi3"] for r in res]

    nc = _get("womoe", build_womoe)
    im = [{"xi3": xi3_all[c], "wo": wo, "xn2": xn2_all[c],
           "gate_w": gate_w, "w1": m_w1, "w3": m_w3, "w2": m_w2,
           "x1": x1_all[c]} for c in CORES]
    res = run_bass_kernel_spmd(nc, im, core_ids=CORES).results
    x2 = np.concatenate([r["x2"] for r in res], 0)
    return x2.reshape(1, S, D)


# version 06
# speedup vs baseline: 1.0965x; 1.0965x over previous
import contextlib
import numpy as np
import ml_dtypes

import concourse.bass as bass
import concourse.mybir as mybir
from concourse.bass_utils import run_bass_kernel_spmd

F32 = mybir.dt.float32
BF16 = mybir.dt.bfloat16
AF = mybir.ActivationFunctionType
ALU = mybir.AluOpType
AX = mybir.AxisListType
BF = ml_dtypes.bfloat16

NCORE = 8
S = 8192
D = 2048
TOK = S // NCORE
LN_EPS = 1e-5



class InstProxy:

    def __init__(self, inst, prog, eng_name):
        self._i = inst
        self._p = prog
        self._e = eng_name

    def then_inc(self, sem, n=1):
        ep = self._p._eng_epoch[self._e]
        self._p.counts[(ep, sem.name)] = \
            self._p.counts.get((ep, sem.name), 0) + n
        self._i.then_inc(sem, n)
        return self

    def __getattr__(self, k):
        return getattr(self._i, k)


class Eng:

    def __init__(self, prog, name, handle):
        self.p = prog
        self.name = name
        self.e = handle
        self._sem = prog.sems["self_" + name]

    def wait_ge(self, sem, v):
        base = 0
        if sem.name.startswith("pool"):
            base = self.p.bases.get(
                (self.p._eng_epoch[self.name], sem.name), 0)
        return self.e.wait_ge(sem, base + v)

    def __getattr__(self, k):
        a = getattr(self.e, k)
        if not callable(a):
            return a
        p, nm = self.p, self.name

        def wrapped(*args, **kw):
            r = a(*args, **kw)
            if type(r).__name__ == "BassInstruction":
                return InstProxy(r, p, nm)
            return r
        return wrapped

    def chain(self, inst):
        self.p.selfcnt[self.name] += 1
        c = self.p.selfcnt[self.name]
        inst.then_inc(self._sem, 1)
        self.e.wait_ge(self._sem, c)
        return inst

    def chain_inc(self, inst, sem, n=1):
        self.chain(inst)
        self.nop().then_inc(sem, n)
        return inst


class Ring:

    def __init__(self, p, n=4, per_slot=None):
        self.sems = [p.sem() for _ in range(n)]
        self.R = n
        self.per_slot = per_slot
        self.npush = 0
        self.nslotpush = {}

    def push(self, inst):
        i = self.npush % self.R
        self.npush += 1
        inst.then_inc(self.sems[i], 16)
        return inst

    def wait(self, e, j):
        i = j % self.R
        e.wait_ge(self.sems[i], 16 * (j // self.R + 1))

    def wait_group(self, e, hi=None):
        if hi is None:
            hi = self.npush - 1
        n = hi + 1
        for i in range(self.R):
            cnt_i = (n - i + self.R - 1) // self.R
            if cnt_i > 0:
                e.wait_ge(self.sems[i], 16 * cnt_i)

    def push_slot(self, inst, slot):
        assert self.per_slot is not None
        i = slot % self.R
        self.nslotpush[slot] = self.nslotpush.get(slot, 0) + 1
        assert self.nslotpush[slot] <= self.per_slot
        inst.then_inc(self.sems[i], 16)
        return inst

    def wait_slot(self, e, slot):
        i = slot % self.R
        e.wait_ge(self.sems[i], 16 * self.per_slot * (slot // self.R + 1))


class Prog:
    ENG = ("sync", "gpsimd", "scalar", "vector", "tensor")

    def __init__(self, name="prog", bases=None):
        self.nc = bass.Bass("TRN2", target_bir_lowering=False, debug=False)
        self.ctx = contextlib.ExitStack()
        self.work = {e: [] for e in self.ENG}
        self.sems = {}
        self.selfcnt = {e: 0 for e in self.ENG}
        self.nbar = 0
        self._uniq = 0
        self._astack = []
        self._pool = []
        self._cursor = 0
        self.counts = {}
        self.bases = bases if bases is not None else {}
        self._eng_epoch = {e: 0 for e in self.ENG}
        for e in self.ENG:
            self.sem("self_" + e)

    def uniq(self, prefix):
        self._uniq += 1
        return f"{prefix}_{self._uniq}"

    def sem(self, name=None, persistent=False):
        if name is None and not persistent:
            if self._cursor < len(self._pool):
                s = self._pool[self._cursor]
            else:
                s = self.ctx.enter_context(
                    self.nc.semaphore(self.uniq("pool")))
                self._pool.append(s)
            self._cursor += 1
            return s
        if name is None:
            name = self.uniq("sem")
        if name not in self.sems:
            self.sems[name] = self.ctx.enter_context(self.nc.semaphore(name))
        return self.sems[name]

    def ring(self, n=4, per_slot=None):
        return Ring(self, n, per_slot)

    def on(self, eng):
        def deco(f):
            self.work[eng].append(f)
            return f
        return deco

    def sb(self, name, shape, dt, freeable=False):
        cm = self.nc.sbuf_tensor(self.uniq(name), shape, dt)
        h = cm.__enter__()
        self._astack.append(cm)
        return h

    def ps(self, name, shape, dt, freeable=False):
        cm = self.nc.psum_tensor(self.uniq(name), shape, dt)
        h = cm.__enter__()
        self._astack.append(cm)
        return h

    def scope(self):
        return len(self._astack)

    def pop_scope(self, mark):
        while len(self._astack) > mark:
            self._astack.pop().__exit__(None, None, None)

    def din(self, name, shape, dt):
        return self.nc.dram_tensor(name, list(shape), dt, kind="ExternalInput").ap()

    def dout(self, name, shape, dt):
        return self.nc.dram_tensor(name, list(shape), dt, kind="ExternalOutput").ap()

    def dtmp(self, name, shape, dt):
        return self.nc.dram_tensor(self.uniq(name), list(shape), dt,
                                   kind="Internal").ap()

    def barrier(self, reset=False):
        self.nbar += 1
        base = 5 * (self.nbar - 1)
        bar = self.sem("bar", persistent=True)
        for e in self.ENG:
            @self.on(e)
            def _(eng, base=base, bar=bar, e_name=e, reset=reset):
                eng.nop().then_inc(bar, 1)
                eng.wait_ge(bar, base + 5)
                if reset:
                    self._eng_epoch[e_name] += 1
        if reset:
            self._cursor = 0

    def finalize(self):
        nc = self.nc
        with nc.Block() as block:
            for name in self.ENG:
                def runner(e, name=name):
                    w = Eng(self, name, e)
                    for f in self.work[name]:
                        f(w)
                getattr(block, name)(runner)
        self.pop_scope(0)
        self.ctx.close()
        return nc



def host_constants():
    dk = 128
    inv_freq = 1.0 / (10000.0 ** (np.arange(0, dk, 2, dtype=np.float64) / dk))
    pos = np.arange(S, dtype=np.float64)
    fr = pos[:, None] * inv_freq[None, :]
    ang = np.concatenate([fr, fr], axis=1)
    cos = np.cos(ang).T.astype(np.float32)
    sin = np.sin(ang).T.astype(np.float32)
    alpha = np.float32(1.0 / np.sqrt(128.0))
    perm = np.zeros((128, 128), dtype=np.float32)
    for m in range(64):
        perm[m + 64, m] = -1.0
    for m in range(64, 128):
        perm[m - 64, m] = 1.0
    return {
        "cosq": np.ascontiguousarray(cos * alpha),
        "sinq": np.ascontiguousarray(sin * alpha),
        "cosk": np.ascontiguousarray(cos),
        "sink": np.ascontiguousarray(sin),
        "permsign": perm,
        "ident": np.eye(128, dtype=BF),
        "ident_f32": np.eye(128, dtype=np.float32),
        "ones_row_bf": np.ones((1, 128), dtype=BF),
        "ones_col_bf": np.ones((128, 1), dtype=BF),
        "ones_row_f32": np.ones((1, 128), dtype=np.float32),
        "ones_col_f32": np.ones((128, 1), dtype=np.float32),
    }


CONST_SHAPES = {
    "ident": ([128, 128], BF16), "ident_f32": ([128, 128], F32),
    "ones_row_bf": ([1, 128], BF16),
    "ones_col_bf": ([128, 1], BF16), "ones_row_f32": ([1, 128], F32),
    "ones_col_f32": ([128, 1], F32), "permsign": ([128, 128], F32),
}


def load_consts(p, which):
    ring = Ring(p, 1)
    ring.sems = [p.sem(persistent=True)]
    tiles = {}
    dins = {}
    for w in which:
        shp, dt = CONST_SHAPES[w]
        dins[w] = p.din(w, shp, dt)
        tiles[w] = p.sb("c_" + w, shp, dt)

    @p.on("sync")
    def _(e):
        for w in which:
            ring.push(e.dma_start(tiles[w][:], dins[w]))
    return tiles, ring



def phase_ln(p, x_in, xn_out, T=TOK):
    nt = T // 128
    rld = p.ring(4)
    rst = p.ring(4)
    s2d = p.sem()
    vep = p.sem()
    rsd = p.sem()
    nbd = p.sem()
    nrm = p.sem()

    xt = [p.sb("ln_x", [128, D], F32) for _ in range(2)]
    xnt = [p.sb("ln_o", [128, D], BF16) for _ in range(2)]
    scr_v = p.sb("ln_scrv", [128, D], F32)
    scr_s = p.sb("ln_scrs", [128, D], F32)
    s1 = [p.sb("ln_s1", [128, 1], F32) for _ in range(2)]
    s2 = [p.sb("ln_s2", [128, 1], F32) for _ in range(2)]
    mean = p.sb("ln_mean", [128, 1], F32)
    ex2 = p.sb("ln_ex2", [128, 1], F32)
    varep = p.sb("ln_var", [128, 1], F32)
    rstd = p.sb("ln_rstd", [128, 1], F32)
    nbias = p.sb("ln_nb", [128, 1], F32)

    @p.on("sync")
    def _(e):
        for i in range(nt):
            if i >= 2:
                e.wait_ge(nrm, i - 1)
            rld.push(e.dma_start(xt[i % 2][:], x_in[i * 128:(i + 1) * 128, :]))

    @p.on("vector")
    def _(e):
        for i in range(nt):
            b = i % 2
            rld.wait(e, i)
            e.chain(e.tensor_scalar(scr_v[:], xt[b][:], 0.0, None,
                                    ALU.add, ALU.add, accum_out=s1[b][:]))
            e.wait_ge(s2d, i + 1)
            e.chain(e.tensor_scalar(mean[:], s1[b][:], 1.0 / D, None,
                                    ALU.mult, ALU.bypass))
            e.chain(e.tensor_scalar(ex2[:], s2[b][:], 1.0 / D, None,
                                    ALU.mult, ALU.bypass))
            e.chain(e.scalar_tensor_tensor(varep[:], mean[:], -1.0, mean[:],
                                           ALU.mult, ALU.mult))
            e.chain(e.tensor_add(varep[:], varep[:], ex2[:]))
            e.chain_inc(e.tensor_scalar(varep[:], varep[:], LN_EPS, None,
                                        ALU.add, ALU.bypass), vep)
            e.wait_ge(rsd, i + 1)
            e.chain_inc(e.scalar_tensor_tensor(nbias[:], rstd[:], -1.0,
                                               mean[:], ALU.mult, ALU.mult), nbd)

    @p.on("scalar")
    def _(e):
        for i in range(nt):
            b = i % 2
            rld.wait(e, i)
            e.activation(scr_s[:], xt[b][:], AF.Square,
                         accum_out=s2[b][:]).then_inc(s2d, 1)
            e.wait_ge(vep, i + 1)
            e.chain(e.activation(rstd[:], varep[:], AF.Ln))
            e.chain_inc(e.activation(rstd[:], rstd[:], AF.Exp,
                                     scale=-0.5), rsd)
            e.wait_ge(nbd, i + 1)
            if i >= 2:
                rst.wait(e, i - 2)
            e.activation(xnt[b][:], xt[b][:], AF.Identity, bias=nbias[:],
                         scale=rstd[:]).then_inc(nrm, 1)

    @p.on("gpsimd")
    def _(e):
        for i in range(nt):
            e.wait_ge(nrm, i + 1)
            rst.push(e.dma_start(xn_out[i * 128:(i + 1) * 128, :],
                                 xnt[i % 2][:]))
        rst.wait_group(e)



def phase_qkv(p, xn_own, w_attn, tabs, consts, cring,
              q_out, k_out, v_out, qonly=False):
    NF = D // 128
    QKC = 16 if qonly else 20
    TH = TOK // 512
    perm = consts["permsign"]

    rxn = p.ring(1)
    rtab = p.ring(1)
    rw = p.ring(4)
    rwv = p.ring(1)
    rstq = p.ring(4)
    rstv = p.ring(4)
    ped = p.sem()
    rotd = p.sem()
    qrd = p.sem()
    rpd = p.sem()
    pvd = p.sem()
    vevd = p.sem()

    xn_dm = p.sb("qkv_xndm", [128, NF * TOK], BF16)
    wtile = [p.sb("qkv_w", [128, NF * 128], BF16) for _ in range(2)]
    wv_sb = p.sb("qkv_wv", [128, NF * 2048], BF16)
    cq = p.sb("qkv_cq", [128, TOK], F32)
    sq = p.sb("qkv_sq", [128, TOK], F32)
    ck = p.sb("qkv_ck", [128, TOK], F32)
    sk = p.sb("qkv_sk", [128, TOK], F32)
    qraw = [p.sb("qkv_qraw", [128, 512], F32) for _ in range(2)]
    t1 = p.sb("qkv_t1", [128, 512], F32)
    t2 = p.sb("qkv_t2", [128, 512], F32)
    qo = [p.sb("qkv_qo", [128, 512], BF16) for _ in range(2)]
    vo = [p.sb("qkv_vo", [128, 512], BF16) for _ in range(2)]
    pqk = [p.ps("qkv_pqk", [128, 512], F32) for _ in range(2)]
    prot = [p.ps("qkv_prot", [128, 512], F32) for _ in range(2)]
    pv = [p.ps("qkv_pv", [128, 512], F32) for _ in range(2)]

    @p.on("sync")
    def _(e):
        for c in range(NF):
            rxn.push(e.dma_start(xn_dm[:, c * TOK:(c + 1) * TOK],
                                 xn_own[:, c * 128:(c + 1) * 128],
                                 transpose=True))
        for t, dst in (("cosq", cq), ("sinq", sq), ("cosk", ck), ("sink", sk)):
            rtab.push(e.dma_start(dst[:], tabs[t]))
        for c in range(QKC):
            for h in range(TH):
                n = 2 * c + h
                e.wait_ge(rpd, n + 1)
                dst = (q_out[c * 128:(c + 1) * 128, h * 512:(h + 1) * 512]
                       if c < 16 else
                       k_out[(c - 16) * 128:(c - 15) * 128,
                             h * 512:(h + 1) * 512])
                rstq.push(e.dma_start(dst, qo[n % 2][:]))
        if not qonly:
            for tc in range(TOK // 128):
                for vc in range(4):
                    n = 4 * tc + vc
                    e.wait_ge(vevd, n + 1)
                    rstv.push(e.dma_start(
                        v_out[tc * 128:(tc + 1) * 128,
                              vc * 512:(vc + 1) * 512],
                        vo[n % 2][:]))
        rstq.wait_group(e)
        rstv.wait_group(e)

    @p.on("gpsimd")
    def _(e):
        for c in range(QKC):
            if c >= 2:
                e.wait_ge(ped, 2 * (c - 1))
            rw.push(e.dma_start(
                wtile[c % 2][:].rearrange("p (a c) -> p a c", a=NF),
                w_attn[:, c * 128:(c + 1) * 128]
                .rearrange("(a p) c -> p a c", p=128)))
        if not qonly:
            wv_view = wv_sb[:].rearrange("p (a c) -> p a c", a=NF)
            for vc in range(4):
                rwv.push(e.dma_start(
                    wv_view[:, :, vc * 512:(vc + 1) * 512],
                    w_attn[:, 2560 + vc * 512:2560 + (vc + 1) * 512]
                    .rearrange("(a p) c -> p a c", p=128)))

    @p.on("tensor")
    def _(e):
        rxn.wait_group(e)
        cring.wait_group(e)
        for c in range(QKC):
            rw.wait(e, c)
            for h in range(TH):
                n = 2 * c + h
                if n >= 2:
                    e.wait_ge(qrd, n - 1)
                for kc in range(NF):
                    inst = e.matmul(
                        pqk[n % 2][:],
                        wtile[c % 2][:, kc * 128:(kc + 1) * 128],
                        xn_dm[:, kc * TOK + h * 512:kc * TOK + h * 512 + 512],
                        start=(kc == 0), stop=(kc == NF - 1))
                    if kc == NF - 1:
                        inst.then_inc(ped, 1)
                e.wait_ge(qrd, n + 1)
                if n >= 2:
                    e.wait_ge(rpd, n - 1)
                e.matmul(prot[n % 2][:], perm[:], qraw[n % 2][:],
                         start=True, stop=True).then_inc(rotd, 1)
        if not qonly:
            rwv.wait_group(e)
            for tc in range(TOK // 128):
                for vc in range(4):
                    n = 4 * tc + vc
                    if n >= 2:
                        e.wait_ge(vevd, n - 1)
                    for kc in range(NF):
                        inst = e.matmul(
                            pv[n % 2][:],
                            xn_dm[:, kc * TOK + tc * 128:kc * TOK + (tc + 1) * 128],
                            wv_sb[:, kc * 2048 + vc * 512:kc * 2048 + (vc + 1) * 512],
                            start=(kc == 0), stop=(kc == NF - 1))
                        if kc == NF - 1:
                            inst.then_inc(pvd, 1)

    @p.on("scalar")
    def _(e):
        for c in range(QKC):
            for h in range(TH):
                n = 2 * c + h
                e.wait_ge(ped, n + 1)
                if n >= 2:
                    e.wait_ge(rotd, n - 1)
                    e.wait_ge(rpd, n - 1)
                e.activation(qraw[n % 2][:], pqk[n % 2][:], AF.Identity
                             ).then_inc(qrd, 1)
        if not qonly:
            for tc in range(TOK // 128):
                for vc in range(4):
                    n = 4 * tc + vc
                    e.wait_ge(pvd, n + 1)
                    if n >= 2:
                        rstv.wait(e, n - 2)
                    e.activation(vo[n % 2][:], pv[n % 2][:], AF.Identity
                                 ).then_inc(vevd, 1)

    @p.on("vector")
    def _(e):
        rtab.wait_group(e)
        for c in range(QKC):
            ctab = cq if c < 16 else ck
            stab = sq if c < 16 else sk
            for h in range(TH):
                n = 2 * c + h
                e.wait_ge(qrd, n + 1)
                e.chain(e.tensor_mul(t1[:], qraw[n % 2][:],
                                     ctab[:, h * 512:(h + 1) * 512]))
                e.wait_ge(rotd, n + 1)
                e.chain(e.tensor_mul(t2[:], prot[n % 2][:],
                                     stab[:, h * 512:(h + 1) * 512]))
                if n >= 2:
                    rstq.wait(e, n - 2)
                e.chain_inc(e.tensor_add(qo[n % 2][:], t1[:], t2[:]), rpd)



def phase_attn(p, q_in, k_in, v_in, consts, cring, ctx_out):
    NKC = S // 128
    QT = TOK // 512
    NIT = 4 * QT * 4
    ones_col = consts["ones_col_bf"]
    ones_row = consts["ones_row_f32"]

    rq = p.ring(1)
    rk = p.ring(1)
    rv = p.ring(4)
    rst = p.ring(8)
    pq2 = p.sem()
    nqd = p.sem()
    nqe = p.sem()
    vcr = p.sem()
    scr = p.sem()
    bcm = p.sem()
    ngc = p.sem()
    scd = p.sem()
    exd = p.sem()
    pcd = p.sem()
    rsd = p.sem()
    mbc = p.sem()
    bcsd = p.sem()
    ctxe = p.sem()

    q_sb = p.sb("at_q", [128, 16 * TOK], BF16)
    k_sb = p.sb("at_k", [128, 4 * S], BF16)
    v_sb = p.sb("at_v", [128, NKC * 512], BF16)
    q2 = [p.sb("at_q2", [128, 512], BF16) for _ in range(2)]
    nrow = p.sb("at_nrow", [1, 32], F32)
    crow = p.sb("at_crow", [1, 16], F32)
    negC = p.sb("at_negC", [128, 16], F32)
    t1 = p.sb("at_t1", [1, 1], F32)
    pr = [p.sb("at_pr", [128, 512], BF16) for _ in range(2)]
    rsum = p.sb("at_rsum", [1, 512], F32)
    bc_sb = p.sb("at_bc", [128, 512], F32)
    ctx_sb = [p.sb("at_ctx", [128, 4 * 512], BF16) for _ in range(2)]
    ps_sc = [p.ps("at_psc", [128, 512], F32) for _ in range(2)]
    pc = [p.ps("at_pc", [128, 512], F32) for _ in range(4)]
    ps_sum = p.ps("at_psum", [1, 512], F32)
    ps_bc = p.ps("at_pbc", [128, 512], F32)
    ps_nr = [ps_sum, ps_bc]

    NQ = 16 * TOK // 512
    NK = 4 * S // 512

    @p.on("sync")
    def _(e):
        for h in range(16):
            rq.push(e.dma_start(q_sb[:, h * TOK:(h + 1) * TOK],
                                q_in[h * 128:(h + 1) * 128, :]))
        for kv in range(4):
            rk.push(e.dma_start(k_sb[:, kv * S:(kv + 1) * S],
                                k_in[kv * 128:(kv + 1) * 128, :]))
        for kv in range(4):
            if kv >= 1:
                e.wait_ge(ctxe, kv * 8)
            vv = v_in[:, kv * 512:(kv + 1) * 512].rearrange(
                "(a p) c -> p a c", p=128)
            for g8 in range(4):
                rv.push(e.dma_start(
                    v_sb[:].rearrange("p (a c) -> p a c", a=NKC)
                    [:, g8 * 16:(g8 + 1) * 16, :],
                    vv[:, g8 * 16:(g8 + 1) * 16, :]))

    @p.on("gpsimd")
    def _(e):
        for it in range(NIT):
            kv, r = divmod(it, QT * 4)
            qt, qh4 = divmod(r, 4)
            h = kv * 4 + qh4
            e.wait_ge(ctxe, it + 1)
            for dvs in range(4):
                rst.push(e.dma_start(
                    ctx_out[h * 512 + dvs * 128:h * 512 + (dvs + 1) * 128,
                            qt * 512:(qt + 1) * 512],
                    ctx_sb[it % 2][:, dvs * 512:(dvs + 1) * 512]))
        rst.wait_group(e)

    @p.on("scalar")
    def _(e):
        rq.wait_group(e)
        for it in range(NQ + NK):
            if it == NQ:
                rk.wait_group(e)
            if it >= 2:
                e.wait_ge(nqd, it - 1)
            src_ap = (q_sb[:, it * 512:(it + 1) * 512] if it < NQ
                      else k_sb[:, (it - NQ) * 512:(it - NQ + 1) * 512])
            e.activation(q2[it % 2][:], src_ap, AF.Square).then_inc(pq2, 1)
        e.wait_ge(vcr, 1)
        e.chain(e.activation(crow[:], crow[:], AF.Ln))
        e.chain_inc(e.activation(crow[:], crow[:], AF.Exp, scale=0.5), scr)
        e.wait_ge(bcm, 1)
        e.activation(negC[:], ps_bc[:, 0:16], AF.Identity,
                     scale=-1.0).then_inc(ngc, 1)

    @p.on("tensor")
    def _(e):
        cring.wait_group(e)
        for it in range(NQ + NK):
            e.wait_ge(pq2, it + 1)
            if it >= 2:
                e.wait_ge(nqe, it - 1)
            e.matmul(ps_nr[it % 2][0:1, :], ones_col[:], q2[it % 2][:],
                     start=True, stop=True).then_inc(nqd, 1)
        e.wait_ge(scr, 1)
        e.matmul(ps_bc[:, 0:16], ones_row[:], crow[:],
                 start=True, stop=True).then_inc(bcm, 1)

    @p.on("vector")
    def _(e):
        for it in range(NQ + NK):
            e.wait_ge(nqd, it + 1)
            e.chain(e.reduce_max(t1[:], ps_nr[it % 2][0:1, :], axis=AX.X))
            if it < NQ:
                col, first = it // 2, (it % 2 == 0)
            else:
                col = 16 + (it - NQ) // 16
                first = ((it - NQ) % 16 == 0)
            if first:
                e.chain(e.tensor_copy(nrow[:, col:col + 1], t1[:]))
            else:
                e.chain(e.tensor_max(nrow[:, col:col + 1],
                                     nrow[:, col:col + 1], t1[:]))
            e.nop().then_inc(nqe, 1)
        for h in range(16):
            e.chain(e.tensor_mul(crow[:, h:h + 1], nrow[:, h:h + 1],
                                 nrow[:, 16 + h // 4:17 + h // 4]))
        e.nop().then_inc(vcr, 1)

    @p.on("tensor")
    def _(e):
        for kv in range(4):
            rv.wait_group(e, 4 * (kv + 1) - 1)
            for qt in range(QT):
                for qh4 in range(4):
                    it = (kv * QT + qt) * 4 + qh4
                    h = kv * 4 + qh4
                    qs = q_sb[:, h * TOK + qt * 512:h * TOK + (qt + 1) * 512]
                    if it >= 1:
                        e.wait_ge(ctxe, it)
                        e.wait_ge(bcsd, it)
                    for kc in range(NKC):
                        g = it * NKC + kc
                        if g >= 2:
                            e.wait_ge(exd, g - 1)
                        e.matmul(ps_sc[g % 2][:],
                                 k_sb[:, kv * S + kc * 128:kv * S + (kc + 1) * 128],
                                 qs, start=True, stop=True).then_inc(scd, 1)
                        if kc >= 1:
                            e.wait_ge(exd, g)
                            first = (kc - 1 == 0)
                            for dvs in range(4):
                                e.matmul(pc[dvs][:],
                                         v_sb[:, (kc - 1) * 512 + dvs * 128:
                                              (kc - 1) * 512 + (dvs + 1) * 128],
                                         pr[(g - 1) % 2][:],
                                         start=first, stop=False,
                                         skip_group_check=True)
                            e.matmul(ps_sum[:], ones_col[:], pr[(g - 1) % 2][:],
                                     start=first, stop=False,
                                     skip_group_check=True).then_inc(pcd, 1)
                    g = it * NKC + NKC - 1
                    e.wait_ge(exd, g + 1)
                    for dvs in range(4):
                        e.matmul(pc[dvs][:],
                                 v_sb[:, (NKC - 1) * 512 + dvs * 128:
                                      (NKC - 1) * 512 + (dvs + 1) * 128],
                                 pr[g % 2][:],
                                 start=False, stop=True, skip_group_check=True)
                    e.matmul(ps_sum[:], ones_col[:], pr[g % 2][:],
                             start=False, stop=True, skip_group_check=True
                             ).then_inc(pcd, 1)
                    e.wait_ge(rsd, it + 1)
                    e.matmul(ps_bc[:], ones_row[:], rsum[:],
                             start=True, stop=True).then_inc(mbc, 1)

    @p.on("scalar")
    def _(e):
        e.wait_ge(ngc, 1)
        for it in range(NIT):
            kv = it // (QT * 4)
            h = kv * 4 + (it % 4)
            for kc in range(NKC):
                g = it * NKC + kc
                e.wait_ge(scd, g + 1)
                if g >= 2:
                    e.wait_ge(pcd, g - 1)
                e.activation(pr[g % 2][:], ps_sc[g % 2][:], AF.Exp,
                             bias=negC[:, h:h + 1]).then_inc(exd, 1)
            e.wait_ge(mbc, it + 1)
            e.activation(bc_sb[:], ps_bc[:], AF.Identity).then_inc(bcsd, 1)

    @p.on("vector")
    def _(e):
        for it in range(NIT):
            e.wait_ge(pcd, (it + 1) * NKC)
            e.chain(e.reciprocal(rsum[:], ps_sum[:]))
            e.nop().then_inc(rsd, 1)
            e.wait_ge(bcsd, it + 1)
            if it >= 2:
                for dvs in range(4):
                    rst.wait(e, 4 * (it - 2) + dvs)
            for dvs in range(4):
                inst = e.tensor_mul(
                    ctx_sb[it % 2][:, dvs * 512:(dvs + 1) * 512],
                    pc[dvs][:], bc_sb[:])
                if dvs == 3:
                    e.chain_inc(inst, ctxe)
                else:
                    e.chain(inst)



def phase_wout(p, ctx_in, w_out, b_out, consts, cring, s_out):
    NCF = 64
    NTT = TOK // 128
    GRP = 8
    NG = NCF // GRP
    ones_row = consts["ones_row_f32"]

    rb = p.ring(1)
    rc = p.ring(2, per_slot=GRP)
    rw = p.ring(2, per_slot=GRP)
    rst = p.ring(4)
    ped = p.sem()
    eva = p.sem()
    bmm = p.sem()
    bev = p.sem()

    ctxg = [p.sb("wo_ctx", [128, GRP * TOK], BF16) for _ in range(2)]
    wg = [p.sb("wo_w", [128, GRP * 2048], BF16) for _ in range(2)]
    brow = p.sb("wo_brow", [1, 2048], F32)
    bbc = p.sb("wo_bbc", [128, 2048], F32)
    s_sb = p.sb("wo_s", [128, NTT * 2048], F32)
    ps_o = [p.ps("wo_po", [128, 2048], F32) for _ in range(2)]

    @p.on("sync")
    def _(e):
        rb.push(e.dma_start(brow[:], b_out))
        for go in range(NG):
            if go >= 2:
                e.wait_ge(ped, (go - 1) * NTT)
            for ci in range(GRP):
                cf = go * GRP + ci
                rc.push_slot(e.dma_start(
                    ctxg[go % 2][:, ci * TOK:(ci + 1) * TOK],
                    ctx_in[cf * 128:(cf + 1) * 128, :]), go)

    @p.on("gpsimd")
    def _(e):
        for go in range(NG):
            if go >= 2:
                e.wait_ge(ped, (go - 1) * NTT)
            for ci in range(GRP):
                cf = go * GRP + ci
                rw.push_slot(e.dma_start(
                    wg[go % 2][:, ci * 2048:(ci + 1) * 2048],
                    w_out[cf * 128:(cf + 1) * 128, :]), go)
        for tt in range(NTT):
            e.wait_ge(eva, 1 + NG * NTT)
            rst.push(e.dma_start(s_out[tt * 128:(tt + 1) * 128, :],
                                 s_sb[:, tt * 2048:(tt + 1) * 2048]))
        rst.wait_group(e)

    @p.on("tensor")
    def _(e):
        rb.wait_group(e)
        cring.wait_group(e)
        for i in range(4):
            e.matmul(ps_o[0][:, i * 512:(i + 1) * 512], ones_row[:],
                     brow[:, i * 512:(i + 1) * 512],
                     start=True, stop=True).then_inc(bmm, 1)
        e.wait_ge(bev, 4)
        for go in range(NG):
            rc.wait_slot(e, go)
            rw.wait_slot(e, go)
            for tt in range(NTT):
                it = go * NTT + tt
                if it >= 2:
                    e.wait_ge(eva, it)
                for ci in range(GRP):
                    for c4 in range(4):
                        inst = e.matmul(
                            ps_o[it % 2][:, c4 * 512:(c4 + 1) * 512],
                            ctxg[go % 2][:, ci * TOK + tt * 128:
                                          ci * TOK + (tt + 1) * 128],
                            wg[go % 2][:, ci * 2048 + c4 * 512:
                                       ci * 2048 + (c4 + 1) * 512],
                            start=(ci == 0), stop=(ci == GRP - 1),
                            skip_group_check=True)
                        if ci == GRP - 1 and c4 == 3:
                            inst.then_inc(ped, 1)

    @p.on("scalar")
    def _(e):
        for i in range(4):
            e.wait_ge(bmm, i + 1)
            e.activation(bbc[:, i * 512:(i + 1) * 512],
                         ps_o[0][:, i * 512:(i + 1) * 512],
                         AF.Identity).then_inc(bev, 1)

    @p.on("vector")
    def _(e):
        e.wait_ge(bev, 4)
        for tt in range(NTT):
            e.chain(e.tensor_copy(s_sb[:, tt * 2048:(tt + 1) * 2048], bbc[:]))
        e.nop().then_inc(eva, 1)
        for it in range(NG * NTT):
            tt = it % NTT
            e.wait_ge(ped, it + 1)
            for c4 in range(4):
                inst = e.tensor_add(
                    s_sb[:, tt * 2048 + c4 * 512:tt * 2048 + (c4 + 1) * 512],
                    s_sb[:, tt * 2048 + c4 * 512:tt * 2048 + (c4 + 1) * 512],
                    ps_o[it % 2][:, c4 * 512:(c4 + 1) * 512])
                if c4 == 3:
                    e.chain_inc(inst, eva)
                else:
                    e.chain(inst)



def phase_absum(p, w1s, w2s, out):
    ones_col = None
    r1 = p.ring(4)
    red = p.sem()
    fin = p.sem()
    rst = p.ring(1)

    wbuf = [p.sb("ab_w", [128, 8192], F32) for _ in range(2)]
    rcol = p.sb("ab_r", [128, 1], F32)
    acc = p.sb("ab_acc", [128, 2], F32)
    onec = p.sb("ab_one", [128, 1], F32)
    osb = p.sb("ab_o", [2, 1], F32)
    pso = p.ps("ab_ps", [2, 1], F32)
    ones_in = p.din("ones_col_f32_ab", [128, 1], F32)
    ro = p.ring(1)

    chunks = [(w1s[i * 128:(i + 1) * 128, :], 2048, 0) for i in range(8)] + \
             [(w2s[i * 128:(i + 1) * 128, :], 8192, 1) for i in range(2)]

    @p.on("sync")
    def _(e):
        ro.push(e.dma_start(onec[:], ones_in))
        for i, (src, w, col) in enumerate(chunks):
            if i >= 2:
                e.wait_ge(red, i - 1)
            r1.push(e.dma_start(wbuf[i % 2][:, 0:w], src))

    @p.on("vector")
    def _(e):
        e.chain(e.memset(acc[:], 0.0))
        for i, (src, w, col) in enumerate(chunks):
            r1.wait(e, i)
            e.chain(e.reduce_sum(rcol[:], wbuf[i % 2][:, 0:w], axis=AX.X,
                                 apply_absolute_value=True))
            e.chain_inc(e.tensor_add(acc[:, col:col + 1], acc[:, col:col + 1],
                                     rcol[:]), red)
        e.nop().then_inc(red, 1)

    @p.on("tensor")
    def _(e):
        e.wait_ge(red, len(chunks) + 1)
        ro.wait_group(e)
        e.matmul(pso[:], acc[:], onec[:], start=True, stop=True
                 ).then_inc(fin, 1)

    @p.on("scalar")
    def _(e):
        e.wait_ge(fin, 1)
        e.activation(osb[:], pso[:], AF.Identity).then_inc(fin, 1)

    @p.on("gpsimd")
    def _(e):
        e.wait_ge(fin, 2)
        rst.push(e.dma_start(out.rearrange("a b -> b a"), osb[:]))
        rst.wait_group(e)



def phase_ffn_thr(p, absums, consts, cring, thr_bc, nthr_bc):
    ident = consts["ident_f32"]
    ones_row = consts["ones_row_f32"]
    ones_cf = consts["ones_col_f32"]
    ra = p.ring(1)
    ms = p.sem()

    asb = p.sb("th_a", [8, 2], F32)
    tsb = p.sb("th_t", [2, 1], F32)
    trow = p.sb("th_r", [1, 2], F32)
    pst = p.ps("th_p", [128, 2], F32, freeable=True)

    @p.on("sync")
    def _(e):
        ra.push(e.dma_start(asb[:], absums))

    @p.on("tensor")
    def _(e):
        ra.wait_group(e)
        cring.wait_group(e)
        e.matmul(pst[0:2, 0:1], asb[:], ones_cf[0:8, 0:1],
                 start=True, stop=True).then_inc(ms, 1)
        e.wait_ge(ms, 2)
        e.matmul(pst[0:1, 0:2], tsb[:], ident[0:2, 0:2],
                 start=True, stop=True).then_inc(ms, 1)
        e.wait_ge(ms, 4)
        e.matmul(pst[:, 0:2], ones_row[:], trow[:],
                 start=True, stop=True).then_inc(ms, 1)

    @p.on("scalar")
    def _(e):
        e.wait_ge(ms, 1)
        e.activation(tsb[:], pst[0:2, 0:1], AF.Identity,
                     scale=0.5 / (8192.0 * 2048.0)).then_inc(ms, 1)
        e.wait_ge(ms, 3)
        e.activation(trow[:], pst[0:1, 0:2], AF.Identity).then_inc(ms, 1)
        e.wait_ge(ms, 5)
        e.chain(e.activation(thr_bc[:], pst[:, 0:2], AF.Identity))
        e.chain_inc(e.activation(nthr_bc[:], thr_bc[:], AF.Identity,
                                 scale=-1.0), ms)


def phase_ffn_h(p, xn_own, w1, b1, thr_bc, nthr_bc, consts, cring, h_out,
                thr_ready):
    NFC = 64
    NF = D // 128
    TH = TOK // 512
    ident = consts["ident"]
    ones_cf = consts["ones_col_f32"]

    rxn = p.ring(1)
    rw1 = p.ring(4)
    rb1 = p.ring(1)
    rsth = p.ring(4)
    qzd = p.sem()
    trd = p.sem()
    tre = p.sem()
    hmm = p.sem()
    hev = p.sem()
    b1m = p.sem()
    b1e = p.sem()

    xn_dm = p.sb("fh_xndm", [128, NF * TOK], BF16, freeable=True)
    w1buf = [p.sb("fh_w1", [128, 2048], F32) for _ in range(2)]
    tle = p.sb("fh_tle", [128, 2048], BF16)
    qnat = [p.sb("fh_qn", [128, 2048], BF16) for _ in range(2)]
    w1qT = [p.sb("fh_w1qT", [128, 2048], BF16) for _ in range(2)]
    b1row = p.sb("fh_b1r", [1, 8192], F32)
    b1c = p.sb("fh_b1c", [128, 64], F32)
    h_sb = [p.sb("fh_h", [128, TOK], BF16) for _ in range(2)]
    ps_t = p.ps("fh_pst", [128, 2048], F32, freeable=True)
    ps_h = [p.ps("fh_psh", [128, 512], F32, freeable=True) for _ in range(2)]
    ps_b1 = p.ps("fh_psb", [128, 64], F32, freeable=True)

    @p.on("sync")
    def _(e):
        rb1.push(e.dma_start(b1row[:], b1))
        for c in range(NF):
            rxn.push(e.dma_start(xn_dm[:, c * TOK:(c + 1) * TOK],
                                 xn_own[:, c * 128:(c + 1) * 128],
                                 transpose=True))
        for fc in range(NFC):
            if fc >= 2:
                e.wait_ge(qzd, fc - 1)
            rw1.push(e.dma_start(w1buf[fc % 2][:],
                                 w1[fc * 128:(fc + 1) * 128, :]))

    @p.on("vector")
    def _(e):
        if thr_ready is not None:
            e.wait_ge(thr_ready[0], thr_ready[1])
        for fc in range(NFC):
            rw1.wait(e, fc)
            if fc >= 2:
                e.wait_ge(trd, (fc - 1) * NF)
            e.chain(e.tensor_scalar(tle[:], w1buf[fc % 2][:],
                                    nthr_bc[:, 0:1], None, ALU.is_le,
                                    ALU.bypass))
            e.chain_inc(e.scalar_tensor_tensor(
                qnat[fc % 2][:], w1buf[fc % 2][:], thr_bc[:, 0:1], tle[:],
                ALU.is_ge, ALU.subtract), qzd)

    @p.on("tensor")
    def _(e):
        cring.wait_group(e)
        rb1.wait_group(e)
        for c in range(NFC):
            inst = e.matmul(ps_b1[:, c:c + 1],
                            b1row[0:1, c * 128:(c + 1) * 128],
                            ones_cf[0:1, 0:1], start=True, stop=True)
            if c == NFC - 1:
                inst.then_inc(b1m, 1)
        rxn.wait_group(e)
        for fc in range(NFC):
            e.wait_ge(qzd, fc + 1)
            e.wait_ge(tre, fc)
            for kc in range(NF):
                inst = e.matmul(ps_t[:, kc * 128:(kc + 1) * 128],
                                qnat[fc % 2][:, kc * 128:(kc + 1) * 128],
                                ident[:], start=True, stop=True)
                if kc == NF - 1:
                    inst.then_inc(trd, NF)
            e.wait_ge(tre, fc + 1)
            for th in range(TH):
                hh = TH * fc + th
                if hh >= 2:
                    e.wait_ge(hev, hh - 1)
                for kc in range(NF):
                    inst = e.matmul(
                        ps_h[hh % 2][:],
                        w1qT[fc % 2][:, kc * 128:(kc + 1) * 128],
                        xn_dm[:, kc * TOK + th * 512:kc * TOK + (th + 1) * 512],
                        start=(kc == 0), stop=(kc == NF - 1))
                    if kc == NF - 1:
                        inst.then_inc(hmm, 1)

    @p.on("scalar")
    def _(e):
        e.wait_ge(b1m, 1)
        e.activation(b1c[:], ps_b1[:], AF.Identity).then_inc(b1e, 1)
        for fc in range(NFC):
            e.wait_ge(trd, (fc + 1) * NF)
            if fc >= 2:
                e.wait_ge(hmm, TH * (fc - 1))
            e.activation(w1qT[fc % 2][:], ps_t[:], AF.Identity
                         ).then_inc(tre, 1)
            for th in range(TH):
                hh = TH * fc + th
                e.wait_ge(hmm, hh + 1)
                if fc >= 2 and th == 0:
                    rsth.wait(e, fc - 2)
                e.activation(h_sb[fc % 2][:, th * 512:(th + 1) * 512],
                             ps_h[hh % 2][:], AF.Gelu,
                             bias=b1c[:, fc:fc + 1]).then_inc(hev, 1)

    @p.on("gpsimd")
    def _(e):
        for fc in range(NFC):
            e.wait_ge(hev, TH * (fc + 1))
            rsth.push(e.dma_start(h_out[fc * 128:(fc + 1) * 128, :],
                                  h_sb[fc % 2][:]))
        rsth.wait_group(e)



def phase_ffn_w2q(p, w2, thr_bc, nthr_bc, consts, w2q_out):
    ident = consts["ident"]
    NEC = 16
    NSC = 4
    rw2 = p.ring(4)
    rst = p.ring(4)
    qzd = p.sem()
    trd = p.sem()
    tre = p.sem()

    w2buf = [p.sb("fw_w2", [128, 2048], F32) for _ in range(2)]
    tle = p.sb("fw_tle", [128, 2048], BF16)
    qnat = [p.sb("fw_qn", [128, 2048], BF16) for _ in range(2)]
    qT = [p.sb("fw_qT", [128, 2048], BF16) for _ in range(2)]
    ps_t = p.ps("fw_pst", [128, 2048], F32, freeable=True)

    NIT = NEC * NSC

    @p.on("sync")
    def _(e):
        for it in range(NIT):
            ec, sc = divmod(it, NSC)
            if it >= 2:
                e.wait_ge(qzd, it - 1)
            rw2.push(e.dma_start(
                w2buf[it % 2][:],
                w2[ec * 128:(ec + 1) * 128, sc * 2048:(sc + 1) * 2048]))

    @p.on("vector")
    def _(e):
        for it in range(NIT):
            rw2.wait(e, it)
            if it >= 2:
                e.wait_ge(trd, it - 1)
            e.chain(e.tensor_scalar(tle[:], w2buf[it % 2][:],
                                    nthr_bc[:, 1:2], None, ALU.is_le,
                                    ALU.bypass))
            e.chain_inc(e.scalar_tensor_tensor(
                qnat[it % 2][:], w2buf[it % 2][:], thr_bc[:, 1:2], tle[:],
                ALU.is_ge, ALU.subtract), qzd)

    @p.on("tensor")
    def _(e):
        for it in range(NIT):
            e.wait_ge(qzd, it + 1)
            e.wait_ge(tre, it)
            for kc in range(16):
                inst = e.matmul(ps_t[:, kc * 128:(kc + 1) * 128],
                                qnat[it % 2][:, kc * 128:(kc + 1) * 128],
                                ident[:], start=True, stop=True)
                if kc == 15:
                    inst.then_inc(trd, 1)

    @p.on("scalar")
    def _(e):
        for it in range(NIT):
            e.wait_ge(trd, it + 1)
            if it >= 2:
                rst.wait(e, it - 2)
            e.activation(qT[it % 2][:], ps_t[:], AF.Identity).then_inc(tre, 1)

    @p.on("gpsimd")
    def _(e):
        for it in range(NIT):
            ec, sc = divmod(it, NSC)
            e.wait_ge(tre, it + 1)
            dst = w2q_out[sc * 2048:(sc + 1) * 2048,
                          ec * 128:(ec + 1) * 128].rearrange(
                "(a p) c -> p a c", p=128)
            rst.push(e.dma_start(
                dst, qT[it % 2][:].rearrange("p (a c) -> p a c", a=16)))
        rst.wait_group(e)



def phase_ffn_out(p, h_in, w2q_in, b2, s_in, x_in, consts, x1_out):
    NG = 8
    GRP = 8
    NTT = TOK // 128
    ones_row = consts["ones_row_f32"]

    rb2 = p.ring(1)
    rw = p.ring(2, per_slot=GRP)
    rh = p.ring(2, per_slot=GRP)
    rsx = p.ring(4)
    rst = p.ring(4)
    ped = p.sem()
    eva = p.sem()
    bmm = p.sem()
    bev = p.sem()
    sild = p.sem()
    find = p.sem()

    wgb = p.sb("fo_w", [128, GRP * 2048], BF16)
    hgb = [p.sb("fo_h", [128, GRP * TOK], BF16) for _ in range(2)]
    b2row = p.sb("fo_b2r", [1, 2048], F32)
    bbc2 = p.sb("fo_bbc", [128, 2048], F32)
    o_sb = p.sb("fo_o", [128, NTT * 2048], F32)
    g_sb = p.sb("fo_g", [128, 2048], F32)
    sx = [p.sb("fo_sx", [128, 2048], F32) for _ in range(2)]
    xx = [p.sb("fo_xx", [128, 2048], F32) for _ in range(2)]
    x1t = [p.sb("fo_x1", [128, 2048], F32) for _ in range(2)]
    ps_o = [p.ps("fo_po", [128, 2048], F32) for _ in range(2)]

    @p.on("sync")
    def _(e):
        rb2.push(e.dma_start(b2row[:], b2))
        for go in range(NG):
            if go >= 1:
                e.wait_ge(ped, go * NTT)
            for ci in range(GRP):
                cf = go * GRP + ci
                rw.push_slot(e.dma_start(
                    wgb[:, ci * 2048:(ci + 1) * 2048],
                    w2q_in[cf * 128:(cf + 1) * 128, :]), go)
        for tt in range(NTT):
            rsx.push(e.dma_start(sx[tt % 2][:],
                                 s_in[tt * 128:(tt + 1) * 128, :]))
            rsx.push(e.dma_start(xx[tt % 2][:],
                                 x_in[tt * 128:(tt + 1) * 128, :]))
            e.wait_ge(find, tt + 1)

    @p.on("gpsimd")
    def _(e):
        for go in range(NG):
            if go >= 2:
                e.wait_ge(ped, (go - 1) * NTT)
            for ci in range(GRP):
                cf = go * GRP + ci
                rh.push_slot(e.dma_start(
                    hgb[go % 2][:, ci * TOK:(ci + 1) * TOK],
                    h_in[cf * 128:(cf + 1) * 128, :]), go)
        for tt in range(NTT):
            e.wait_ge(find, tt + 1)
            rst.push(e.dma_start(x1_out[tt * 128:(tt + 1) * 128, :],
                                 x1t[tt % 2][:]))
        rst.wait_group(e)

    @p.on("tensor")
    def _(e):
        rb2.wait_group(e)
        for i in range(4):
            e.matmul(ps_o[0][:, i * 512:(i + 1) * 512], ones_row[:],
                     b2row[:, i * 512:(i + 1) * 512],
                     start=True, stop=True).then_inc(bmm, 1)
        e.wait_ge(bev, 4)
        for go in range(NG):
            rw.wait_slot(e, go)
            rh.wait_slot(e, go)
            for tt in range(NTT):
                it = go * NTT + tt
                if it >= 2:
                    e.wait_ge(eva, it)
                for ci in range(GRP):
                    for c4 in range(4):
                        inst = e.matmul(
                            ps_o[it % 2][:, c4 * 512:(c4 + 1) * 512],
                            hgb[go % 2][:, ci * TOK + tt * 128:
                                         ci * TOK + (tt + 1) * 128],
                            wgb[:, ci * 2048 + c4 * 512:
                                ci * 2048 + (c4 + 1) * 512],
                            start=(ci == 0), stop=(ci == GRP - 1),
                            skip_group_check=True)
                        if ci == GRP - 1 and c4 == 3:
                            inst.then_inc(ped, 1)

    @p.on("scalar")
    def _(e):
        for i in range(4):
            e.wait_ge(bmm, i + 1)
            e.activation(bbc2[:, i * 512:(i + 1) * 512],
                         ps_o[0][:, i * 512:(i + 1) * 512],
                         AF.Identity).then_inc(bev, 1)
        for tt in range(NTT):
            e.wait_ge(eva, 1 + (NG - 1) * NTT + tt + 1)
            if tt >= 1:
                e.wait_ge(find, tt)
            e.activation(g_sb[:], o_sb[:, tt * 2048:(tt + 1) * 2048],
                         AF.Silu).then_inc(sild, 1)

    @p.on("vector")
    def _(e):
        e.wait_ge(bev, 4)
        for tt in range(NTT):
            e.chain(e.tensor_copy(o_sb[:, tt * 2048:(tt + 1) * 2048],
                                  bbc2[:]))
        e.nop().then_inc(eva, 1)
        for it in range(NG * NTT):
            tt = it % NTT
            e.wait_ge(ped, it + 1)
            for c4 in range(4):
                inst = e.tensor_add(
                    o_sb[:, tt * 2048 + c4 * 512:tt * 2048 + (c4 + 1) * 512],
                    o_sb[:, tt * 2048 + c4 * 512:tt * 2048 + (c4 + 1) * 512],
                    ps_o[it % 2][:, c4 * 512:(c4 + 1) * 512])
                if c4 == 3:
                    e.chain_inc(inst, eva)
                else:
                    e.chain(inst)
        for tt in range(NTT):
            e.wait_ge(sild, tt + 1)
            rsx.wait(e, 2 * tt)
            rsx.wait(e, 2 * tt + 1)
            if tt >= 2:
                rst.wait(e, tt - 2)
            e.chain(e.tensor_mul(x1t[tt % 2][:], g_sb[:], sx[tt % 2][:]))
            e.chain_inc(e.tensor_add(x1t[tt % 2][:], x1t[tt % 2][:],
                                     xx[tt % 2][:]), find)



def emit_cols_from_row(p, row_ap, ncols, ones_cf, name):
    colt = p.sb(name, [128, ncols], F32)
    psc = p.ps(name + "_ps", [128, ncols], F32)
    dn = p.sem()

    @p.on("tensor")
    def _(e):
        for c in range(ncols):
            inst = e.matmul(psc[:, c:c + 1], row_ap[0:1, c * 128:(c + 1) * 128],
                            ones_cf[0:1, 0:1], start=True, stop=True)
            if c == ncols - 1:
                inst.then_inc(dn, 1)

    @p.on("scalar")
    def _(e):
        e.wait_ge(dn, 1)
        e.activation(colt[:], psc[:], AF.Identity).then_inc(dn, 1)
    return colt, dn



def phase_qproj(p, tq, g_state, b_state, wq, consts, cring, xi0_out):
    ones_cf = consts["ones_col_f32"]

    rgb = p.ring(1)
    grow = p.sb("xp2_gr", [1, D], F32)
    brow = p.sb("xp2_br", [1, D], F32)

    @p.on("sync")
    def _(e):
        rgb.push(e.dma_start(grow[:], g_state))
        rgb.push(e.dma_start(brow[:], b_state))

    @p.on("tensor")
    def _(e):
        rgb.wait_group(e)
        cring.wait_group(e)
    gcol, gdn = emit_cols_from_row(p, grow, 16, ones_cf, "xp2_gc")
    bcol, bdn = emit_cols_from_row(p, brow, 16, ones_cf, "xp2_bc")

    wq_bf = p.dtmp("xp2_wqbf", [1024, D], BF16)
    rwc = p.ring(2)
    rwt = p.ring(1)
    rxt = p.ring(1)
    rst = p.ring(4)
    gbd = p.sem()
    qmd = p.sem()
    qev = p.sem()

    wq_fm = p.sb("xp2_wqfm", [128, 16 * 1024], BF16)
    xng = p.sb("xp2_xng", [128, 16 * TOK], BF16)
    xio = [p.sb("xp2_xio", [128, 512], BF16) for _ in range(2)]
    ps_q = [p.ps("xp2_psq", [128, 512], F32) for _ in range(2)]

    @p.on("gpsimd")
    def _(e):
        for i in range(2):
            rwc.push(e.dma_start(wq_bf[i * 512:(i + 1) * 512, :],
                                 wq[i * 512:(i + 1) * 512, :]))
        for oc in range(8):
            for th in range(TOK // 512):
                n = oc * (TOK // 512) + th
                e.wait_ge(qev, n + 1)
                rst.push(e.dma_start(
                    xi0_out[oc * 128:(oc + 1) * 128,
                            th * 512:(th + 1) * 512], xio[n % 2][:]))
        rst.wait_group(e)

    @p.on("sync")
    def _(e):
        rwc.wait_group(e, 1)
        for c in range(16):
            rwt.push(e.dma_start(wq_fm[:, c * 1024:(c + 1) * 1024],
                                 wq_bf[:, c * 128:(c + 1) * 128],
                                 transpose=True))
        for c in range(16):
            rxt.push(e.dma_start(xng[:, c * TOK:(c + 1) * TOK],
                                 tq[:, c * 128:(c + 1) * 128],
                                 transpose=True))

    @p.on("vector")
    def _(e):
        rxt.wait_group(e)
        e.wait_ge(gdn, 2)
        e.wait_ge(bdn, 2)
        for c in range(16):
            inst = e.tensor_scalar(xng[:, c * TOK:(c + 1) * TOK],
                                   xng[:, c * TOK:(c + 1) * TOK],
                                   gcol[:, c:c + 1], bcol[:, c:c + 1],
                                   ALU.mult, ALU.add)
            e.chain(inst)
        e.nop().then_inc(gbd, 1)

    @p.on("tensor")
    def _(e):
        rwt.wait_group(e)
        e.wait_ge(gbd, 1)
        for oc in range(8):
            for th in range(TOK // 512):
                n = oc * (TOK // 512) + th
                if n >= 2:
                    e.wait_ge(qev, n - 1)
                for kc in range(16):
                    inst = e.matmul(
                        ps_q[n % 2][:],
                        wq_fm[:, kc * 1024 + oc * 128:kc * 1024 + (oc + 1) * 128],
                        xng[:, kc * TOK + th * 512:kc * TOK + (th + 1) * 512],
                        start=(kc == 0), stop=(kc == 15))
                    if kc == 15:
                        inst.then_inc(qmd, 1)

    @p.on("scalar")
    def _(e):
        for n in range(16):
            e.wait_ge(qmd, n + 1)
            if n >= 2:
                rst.wait(e, n - 2)
            e.activation(xio[n % 2][:], ps_q[n % 2][:], AF.Identity
                         ).then_inc(qev, 1)


def phase_ln2_from_bf(p, xn_in, out, T=TOK):
    nt = T // 128
    rld = p.ring(4)
    rst = p.ring(4)
    s2d = p.sem()
    vep = p.sem()
    rsd = p.sem()
    nbd = p.sem()
    nrm = p.sem()

    xt = [p.sb("l2_x", [128, D], BF16) for _ in range(2)]
    xnt = [p.sb("l2_o", [128, D], BF16) for _ in range(2)]
    scr_v = p.sb("l2_scrv", [128, D], F32)
    scr_s = p.sb("l2_scrs", [128, D], F32)
    s1 = [p.sb("l2_s1", [128, 1], F32) for _ in range(2)]
    s2 = [p.sb("l2_s2", [128, 1], F32) for _ in range(2)]
    mean = p.sb("l2_mean", [128, 1], F32)
    ex2 = p.sb("l2_ex2", [128, 1], F32)
    varep = p.sb("l2_var", [128, 1], F32)
    rstd = p.sb("l2_rstd", [128, 1], F32)
    nbias = p.sb("l2_nb", [128, 1], F32)

    @p.on("sync")
    def _(e):
        for i in range(nt):
            if i >= 2:
                e.wait_ge(nrm, i - 1)
            rld.push(e.dma_start(xt[i % 2][:], xn_in[i * 128:(i + 1) * 128, :]))

    @p.on("vector")
    def _(e):
        for i in range(nt):
            b = i % 2
            rld.wait(e, i)
            e.chain(e.tensor_scalar(scr_v[:], xt[b][:], 0.0, None,
                                    ALU.add, ALU.add, accum_out=s1[b][:]))
            e.wait_ge(s2d, i + 1)
            e.chain(e.tensor_scalar(mean[:], s1[b][:], 1.0 / D, None,
                                    ALU.mult, ALU.bypass))
            e.chain(e.tensor_scalar(ex2[:], s2[b][:], 1.0 / D, None,
                                    ALU.mult, ALU.bypass))
            e.chain(e.scalar_tensor_tensor(varep[:], mean[:], -1.0, mean[:],
                                           ALU.mult, ALU.mult))
            e.chain(e.tensor_add(varep[:], varep[:], ex2[:]))
            e.chain_inc(e.tensor_scalar(varep[:], varep[:], LN_EPS, None,
                                        ALU.add, ALU.bypass), vep)
            e.wait_ge(rsd, i + 1)
            e.chain_inc(e.scalar_tensor_tensor(nbias[:], rstd[:], -1.0,
                                               mean[:], ALU.mult, ALU.mult), nbd)

    @p.on("scalar")
    def _(e):
        for i in range(nt):
            b = i % 2
            rld.wait(e, i)
            e.activation(scr_s[:], xt[b][:], AF.Square,
                         accum_out=s2[b][:]).then_inc(s2d, 1)
            e.wait_ge(vep, i + 1)
            e.chain(e.activation(rstd[:], varep[:], AF.Ln))
            e.chain_inc(e.activation(rstd[:], rstd[:], AF.Exp,
                                     scale=-0.5), rsd)
            e.wait_ge(nbd, i + 1)
            if i >= 2:
                rst.wait(e, i - 2)
            e.activation(xnt[b][:], xt[b][:], AF.Identity, bias=nbias[:],
                         scale=rstd[:]).then_inc(nrm, 1)

    @p.on("gpsimd")
    def _(e):
        for i in range(nt):
            e.wait_ge(nrm, i + 1)
            rst.push(e.dma_start(out[i * 128:(i + 1) * 128, :],
                                 xnt[i % 2][:]))
        rst.wait_group(e)



def phase_memkv(p, mem_s, g_mem, b_mem, wk, wv, log_beta, consts, cring,
                kb_out, v_out):
    Md = 1024
    NR = 256
    ones_cf = consts["ones_col_f32"]

    mn_d = p.dtmp("mk_mn", [NR, Md], BF16)
    m0 = p.scope()
    phase_lnmem(p, mem_s, mn_d, NR, Md)
    p.barrier()
    p.pop_scope(m0)

    rgb = p.ring(1)
    grow = p.sb("mk_gr", [1, Md], F32)
    brow = p.sb("mk_br", [1, Md], F32)
    lbrow = p.sb("mk_lb", [1, 8], F32)
    betar = p.sb("mk_be", [1, 8], F32)
    beta_bc = p.sb("mk_bbc", [128, 8], F32)

    @p.on("sync")
    def _(e):
        rgb.push(e.dma_start(grow[:], g_mem))
        rgb.push(e.dma_start(brow[:], b_mem))
        rgb.push(e.dma_start(lbrow[:], log_beta))

    @p.on("tensor")
    def _(e):
        rgb.wait_group(e)
        cring.wait_group(e)
    gcol, gdn = emit_cols_from_row(p, grow, 8, ones_cf, "mk_gc")
    bcol, bdn = emit_cols_from_row(p, brow, 8, ones_cf, "mk_bc")

    wkv_bf = p.dtmp("mk_wkv", [2 * Md, Md], BF16)
    rwc = p.ring(2)
    rwt = p.ring(1)
    rmt = p.ring(1)
    rst = p.ring(4)
    bbd = p.sem()
    gbd = p.sem()
    kmd = p.sem()
    kev = p.sem()
    vmd = p.sem()
    vev = p.sem()

    wk_fm = p.sb("mk_wkfm", [128, 8 * Md], BF16)
    wv_fm = p.sb("mk_wvfm", [128, 8 * Md], BF16)
    mn_dm = p.sb("mk_mndm", [128, 8 * NR], BF16)
    kbo = [p.sb("mk_kbo", [128, NR], BF16) for _ in range(2)]
    vo = [p.sb("mk_vo", [128, 512], BF16) for _ in range(2)]
    ps_k = [p.ps("mk_psk", [128, NR], F32) for _ in range(2)]
    ps_v = [p.ps("mk_psv", [128, 512], F32) for _ in range(2)]
    ps_b = p.ps("mk_psb", [128, 8], F32)

    @p.on("gpsimd")
    def _(e):
        rwc.push(e.dma_start(wkv_bf[0:Md, :], wk))
        rwc.push(e.dma_start(wkv_bf[Md:2 * Md, :], wv))
        for oc in range(8):
            e.wait_ge(kev, oc + 1)
            rst.push(e.dma_start(kb_out[oc * 128:(oc + 1) * 128, :],
                                 kbo[oc % 2][:]))
        for n in range(4):
            e.wait_ge(vev, n + 1)
            tc, vc = divmod(n, 2)
            rst.push(e.dma_start(
                v_out[tc * 128:(tc + 1) * 128, vc * 512:(vc + 1) * 512],
                vo[n % 2][:]))
        rst.wait_group(e)

    @p.on("sync")
    def _(e):
        rwc.wait_group(e, 1)
        for c in range(8):
            rwt.push(e.dma_start(wk_fm[:, c * Md:(c + 1) * Md],
                                 wkv_bf[0:Md, c * 128:(c + 1) * 128],
                                 transpose=True))
        for c in range(8):
            rwt.push(e.dma_start(wv_fm[:, c * Md:(c + 1) * Md],
                                 wkv_bf[Md:2 * Md, c * 128:(c + 1) * 128],
                                 transpose=True))
        for c in range(8):
            rmt.push(e.dma_start(mn_dm[:, c * NR:(c + 1) * NR],
                                 mn_d[:, c * 128:(c + 1) * 128],
                                 transpose=True))

    @p.on("vector")
    def _(e):
        rmt.wait_group(e)
        e.wait_ge(gdn, 2)
        e.wait_ge(bdn, 2)
        for c in range(8):
            e.chain(e.tensor_scalar(mn_dm[:, c * NR:(c + 1) * NR],
                                    mn_dm[:, c * NR:(c + 1) * NR],
                                    gcol[:, c:c + 1], bcol[:, c:c + 1],
                                    ALU.mult, ALU.add))
        e.nop().then_inc(gbd, 1)
        for oc in range(8):
            e.wait_ge(kmd, oc + 1)
            if oc >= 2:
                rst.wait(e, oc - 2)
            e.chain_inc(e.tensor_scalar(kbo[oc % 2][:], ps_k[oc % 2][:],
                                        beta_bc[:, oc:oc + 1], None,
                                        ALU.mult, ALU.bypass), kev)

    @p.on("scalar")
    def _(e):
        rgb.wait_group(e)
        e.activation(betar[:], lbrow[:], AF.Exp).then_inc(bbd, 1)
        e.wait_ge(bbd, 2)
        e.activation(beta_bc[:], ps_b[:], AF.Identity).then_inc(bbd, 1)
        for n in range(4):
            e.wait_ge(vmd, n + 1)
            if n >= 2:
                rst.wait(e, 8 + n - 2)
            e.activation(vo[n % 2][:], ps_v[n % 2][:], AF.Identity
                         ).then_inc(vev, 1)

    @p.on("tensor")
    def _(e):
        e.wait_ge(bbd, 1)
        e.matmul(ps_b[:], consts["ones_row_f32"][:], betar[:],
                 start=True, stop=True).then_inc(bbd, 1)
        rwt.wait_group(e)
        e.wait_ge(gbd, 1)
        e.wait_ge(bbd, 3)
        for oc in range(8):
            if oc >= 2:
                e.wait_ge(kev, oc - 1)
            for kc in range(8):
                inst = e.matmul(
                    ps_k[oc % 2][:],
                    wk_fm[:, kc * Md + oc * 128:kc * Md + (oc + 1) * 128],
                    mn_dm[:, kc * NR:(kc + 1) * NR],
                    start=(kc == 0), stop=(kc == 7))
                if kc == 7:
                    inst.then_inc(kmd, 1)
        for n in range(4):
            tc, vc = divmod(n, 2)
            if n >= 2:
                e.wait_ge(vev, n - 1)
            for kc in range(8):
                inst = e.matmul(
                    ps_v[n % 2][:],
                    mn_dm[:, kc * NR + tc * 128:kc * NR + (tc + 1) * 128],
                    wv_fm[:, kc * Md + vc * 512:kc * Md + (vc + 1) * 512],
                    start=(kc == 0), stop=(kc == 7))
                if kc == 7:
                    inst.then_inc(vmd, 1)


def phase_lnmem(p, x_in, out, T, W):
    nt = T // 128
    rld = p.ring(2)
    rst = p.ring(2)
    s2d = p.sem()
    vep = p.sem()
    rsd = p.sem()
    nbd = p.sem()
    nrm = p.sem()

    xt = [p.sb("lm_x", [128, W], F32) for _ in range(2)]
    xnt = [p.sb("lm_o", [128, W], BF16) for _ in range(2)]
    scr_v = p.sb("lm_scrv", [128, W], F32)
    scr_s = p.sb("lm_scrs", [128, W], F32)
    s1 = [p.sb("lm_s1", [128, 1], F32) for _ in range(2)]
    s2 = [p.sb("lm_s2", [128, 1], F32) for _ in range(2)]
    mean = p.sb("lm_mean", [128, 1], F32)
    ex2 = p.sb("lm_ex2", [128, 1], F32)
    varep = p.sb("lm_var", [128, 1], F32)
    rstd = p.sb("lm_rstd", [128, 1], F32)
    nbias = p.sb("lm_nb", [128, 1], F32)

    @p.on("sync")
    def _(e):
        for i in range(nt):
            if i >= 2:
                e.wait_ge(nrm, i - 1)
            rld.push(e.dma_start(xt[i % 2][:], x_in[i * 128:(i + 1) * 128, :]))

    @p.on("vector")
    def _(e):
        for i in range(nt):
            b = i % 2
            rld.wait(e, i)
            e.chain(e.tensor_scalar(scr_v[:], xt[b][:], 0.0, None,
                                    ALU.add, ALU.add, accum_out=s1[b][:]))
            e.wait_ge(s2d, i + 1)
            e.chain(e.tensor_scalar(mean[:], s1[b][:], 1.0 / W, None,
                                    ALU.mult, ALU.bypass))
            e.chain(e.tensor_scalar(ex2[:], s2[b][:], 1.0 / W, None,
                                    ALU.mult, ALU.bypass))
            e.chain(e.scalar_tensor_tensor(varep[:], mean[:], -1.0, mean[:],
                                           ALU.mult, ALU.mult))
            e.chain(e.tensor_add(varep[:], varep[:], ex2[:]))
            e.chain_inc(e.tensor_scalar(varep[:], varep[:], LN_EPS, None,
                                        ALU.add, ALU.bypass), vep)
            e.wait_ge(rsd, i + 1)
            e.chain_inc(e.scalar_tensor_tensor(nbias[:], rstd[:], -1.0,
                                               mean[:], ALU.mult, ALU.mult), nbd)

    @p.on("scalar")
    def _(e):
        for i in range(nt):
            b = i % 2
            rld.wait(e, i)
            e.activation(scr_s[:], xt[b][:], AF.Square,
                         accum_out=s2[b][:]).then_inc(s2d, 1)
            e.wait_ge(vep, i + 1)
            e.chain(e.activation(rstd[:], varep[:], AF.Ln))
            e.chain_inc(e.activation(rstd[:], rstd[:], AF.Exp,
                                     scale=-0.5), rsd)
            e.wait_ge(nbd, i + 1)
            if i >= 2:
                rst.wait(e, i - 2)
            e.activation(xnt[b][:], xt[b][:], AF.Identity, bias=nbias[:],
                         scale=rstd[:]).then_inc(nrm, 1)

    @p.on("gpsimd")
    def _(e):
        for i in range(nt):
            e.wait_ge(nrm, i + 1)
            rst.push(e.dma_start(out[i * 128:(i + 1) * 128, :],
                                 xnt[i % 2][:]))
        rst.wait_group(e)



def phase_entmax(p, xi0_in, kb_in, v_in, consts, cring, xi3_out,
                 n_mich=3, n_newt=3, NSTEP=3, NH=8):
    NTC = TOK // 128
    NIT = n_mich + n_newt
    NJ = NIT + 1
    N = 2048
    ident = consts["ident"]

    rkb = p.ring(1)
    rv = p.ring(1)
    rxi = p.ring(1)
    rst = p.ring(4)
    smd = p.sem()
    sev = p.sem()
    rel = p.sem()
    sst = p.sem()
    ngp = p.sem()
    rowd = p.sem()
    dcd = p.sem()
    sqd = p.sem()
    ttd = p.sem()
    tte = p.sem()
    xmd = p.sem()
    xev = p.sem()
    sgc = p.sem()

    kb_sb = p.sb("em_kb", [128, NH * N], BF16)
    v_sb = p.sb("em_v", [128, 16 * 1024], BF16)
    xi = [p.sb("em_xi", [128, NH * TOK], BF16) for _ in range(2)]
    xp = p.sb("em_xp", [128, NTC * N], F32)
    m = [p.sb("em_m", [128, N], F32) for _ in range(2)]
    msq = p.sb("em_msq", [128, N], BF16)
    w_bf = [p.sb("em_w", [128, N], BF16) for _ in range(2)]
    wT = [p.sb("em_wT", [128, N], BF16) for _ in range(2)]
    mx2 = p.sb("em_mx2", [128, NTC], F32)
    tau = p.sb("em_tau", [128, NTC], F32)
    a_r = p.sb("em_a", [128, NTC], F32)
    na_r = p.sb("em_na", [128, NTC], F32)
    s1c = p.sb("em_s1", [128, NTC], F32)
    s2c = p.sb("em_s2", [128, NTC], F32)
    ncc = p.sb("em_nc", [128, NTC], F32)
    ssg = p.sb("em_ssg", [128, NTC], F32)
    t1 = p.sb("em_t1", [128, NTC], F32)
    t2 = p.sb("em_t2", [128, NTC], F32)
    t3 = p.sb("em_t3", [128, NTC], F32)
    disc = p.sb("em_dc", [128, NTC], F32)
    sqr = p.sb("em_sq", [128, NTC], F32)
    gj = [p.sb("em_gj", [128, N], BF16) for _ in range(2)]
    ps_s = p.ps("em_pss", [128, N], F32)
    ps_t = p.ps("em_pst", [128, N], F32)
    psA = [ps_s, ps_t]
    ps_xi = ps_s[:, 0:128]

    @p.on("sync")
    def _(e):
        for c in range(NH):
            rkb.push(e.dma_start(kb_sb[:, c * N:(c + 1) * N],
                                 kb_in[c * 128:(c + 1) * 128, :]))
        for c in range(16):
            rv.push(e.dma_start(v_sb[:, c * 1024:(c + 1) * 1024],
                                v_in[c * 128:(c + 1) * 128, :]))
        for c in range(NH):
            rxi.push(e.dma_start(xi[0][:, c * TOK:(c + 1) * TOK],
                                 xi0_in[c * 128:(c + 1) * 128, :]))

    @p.on("tensor")
    def _(e):
        cring.wait_group(e)
        rkb.wait_group(e)
        rv.wait_group(e)
        rxi.wait_group(e)
        for s in range(NSTEP):
            src = xi[s % 2]
            for h in range(NH):
                B = s * NH + h
                if s >= 1:
                    e.wait_ge(xev, s * NH * NTC)
                if B >= 1:
                    e.wait_ge(xev, (B - 1) * NTC + NTC)
                    e.wait_ge(tte, (B - 1) * NTC + NTC)
                for tc in range(NTC):
                    g = B * NTC + tc
                    if tc >= 2:
                        e.wait_ge(sev, g - 1)
                    for c4 in range(4):
                        inst = e.matmul(
                            psA[tc % 2][:, c4 * 512:(c4 + 1) * 512],
                            src[:, h * TOK + tc * 128:h * TOK + (tc + 1) * 128],
                            kb_sb[:, h * N + c4 * 512:h * N + (c4 + 1) * 512],
                            start=True, stop=True, skip_group_check=True)
                        if c4 == 3:
                            inst.then_inc(smd, 1)
                e.wait_ge(sev, B * NTC + NTC)
                for tc in range(NTC + 1):
                    g = B * NTC + tc
                    if tc < NTC:
                        e.wait_ge(sst, B * NJ * NTC + NIT * NTC + tc + 1)
                        if tc >= 1:
                            e.wait_ge(tte, g)
                        for kc in range(16):
                            inst = e.matmul(
                                ps_t[:, kc * 128:(kc + 1) * 128],
                                w_bf[tc % 2][:, kc * 128:(kc + 1) * 128],
                                ident[:], start=True, stop=True)
                            if kc == 15:
                                inst.then_inc(ttd, 1)
                    if tc >= 1:
                        tp_ = tc - 1
                        gp = B * NTC + tp_
                        e.wait_ge(tte, gp + 1)
                        if tp_ >= 1:
                            e.wait_ge(xev, gp)
                        for kc in range(16):
                            inst = e.matmul(
                                ps_xi,
                                v_sb[:, kc * 1024 + h * 128:
                                     kc * 1024 + (h + 1) * 128],
                                wT[tp_ % 2][:, kc * 128:(kc + 1) * 128],
                                start=(kc == 0), stop=(kc == 15),
                                skip_group_check=True)
                            if kc == 15:
                                inst.then_inc(xmd, 1)

    @p.on("scalar")
    def _(e):
        for s in range(NSTEP):
            for h in range(NH):
                B = s * NH + h
                for tc in range(NTC):
                    g = B * NTC + tc
                    e.wait_ge(smd, g + 1)
                    e.activation(xp[:, tc * N:(tc + 1) * N], psA[tc % 2][:],
                                 AF.Identity, scale=0.5).then_inc(sev, 1)
                for j in range(NIT):
                    e.wait_ge(rowd, B * (NJ + 1) + j + 1)
                    for tc in range(NTC):
                        gr = B * NJ * NTC + j * NTC + tc
                        if gr >= 2:
                            e.wait_ge(sst, gr - 1)
                        e.activation(m[gr % 2][:], xp[:, tc * N:(tc + 1) * N],
                                     AF.Relu, bias=na_r[:, tc:tc + 1],
                                     accum_out=s1c[:, tc:tc + 1]
                                     ).then_inc(rel, 1)
                    if j < n_mich:
                        for tc in range(NTC):
                            gs = (B * n_mich + j) * NTC + tc
                            if gs >= 2:
                                e.wait_ge(sgc, gs - 1)
                            e.activation(gj[gs % 2][:],
                                         xp[:, tc * N:(tc + 1) * N],
                                         AF.Sign, bias=na_r[:, tc:tc + 1],
                                         accum_out=ssg[:, tc:tc + 1]
                                         ).then_inc(sgc, 1)
                        e.wait_ge(dcd, B * n_mich + j + 1)
                        e.activation(sqr[:], disc[:], AF.Sqrt
                                     ).then_inc(sqd, 1)
                e.wait_ge(rowd, B * (NJ + 1) + NIT + 1)
                gr0 = B * NJ * NTC + NIT * NTC
                e.activation(m[gr0 % 2][:], xp[:, 0:N], AF.Relu,
                             bias=na_r[:, 0:1],
                             accum_out=s1c[:, 0:1]).then_inc(rel, 1)
                for tc in range(NTC):
                    g = B * NTC + tc
                    if tc + 1 < NTC:
                        gr = gr0 + tc + 1
                        if gr >= 2:
                            e.wait_ge(sst, gr - 1)
                        e.activation(m[gr % 2][:],
                                     xp[:, (tc + 1) * N:(tc + 2) * N],
                                     AF.Relu, bias=na_r[:, tc + 1:tc + 2],
                                     accum_out=s1c[:, tc + 1:tc + 2]
                                     ).then_inc(rel, 1)
                    if g >= 1:
                        e.wait_ge(xmd, g)
                    e.wait_ge(ttd, g + 1)
                    e.activation(wT[tc % 2][:], ps_t[:], AF.Identity
                                 ).then_inc(tte, 1)
                    e.wait_ge(xmd, g + 1)
                    e.activation(xi[(s + 1) % 2][:, h * TOK + tc * 128:
                                                 h * TOK + (tc + 1) * 128],
                                 ps_xi, AF.Identity).then_inc(xev, 1)

    @p.on("vector")
    def _(e):
        for s in range(NSTEP):
            for h in range(NH):
                B = s * NH + h
                for tc in range(NTC):
                    e.wait_ge(sev, B * NTC + tc + 1)
                    e.chain(e.reduce_max(mx2[:, tc:tc + 1],
                                         xp[:, tc * N:(tc + 1) * N], axis=AX.X))
                e.chain(e.tensor_scalar(a_r[:], mx2[:], -1.0, None,
                                        ALU.add, ALU.bypass))
                e.chain(e.tensor_scalar(tau[:], a_r[:], 0.0, None,
                                        ALU.mult, ALU.bypass))
                e.chain(e.tensor_scalar(tau[:], tau[:], -1.0, None,
                                        ALU.add, ALU.bypass))
                e.chain_inc(e.tensor_scalar(na_r[:], a_r[:], -1.0, None,
                                            ALU.mult, ALU.bypass),
                            rowd)
                for j in range(NIT):
                    for tc in range(NTC):
                        gr = B * NJ * NTC + j * NTC + tc
                        e.wait_ge(rel, gr + 1)
                        e.chain_inc(e.scalar_tensor_tensor(
                            msq[:], m[gr % 2][:], 0.0, m[gr % 2][:],
                            ALU.add, ALU.mult,
                            accum_out=s2c[:, tc:tc + 1]), sst)
                    if j < n_mich:
                        e.wait_ge(sgc, (B * n_mich + j + 1) * NTC)
                        e.chain(e.tensor_scalar(ncc[:], ssg[:], float(N),
                                                0.5, ALU.add, ALU.mult))
                        e.chain(e.tensor_mul(t1[:], ncc[:], tau[:]))
                        e.chain(e.tensor_add(t1[:], t1[:], s1c[:]))
                        e.chain(e.tensor_add(t2[:], s1c[:], t1[:]))
                        e.chain(e.tensor_mul(t2[:], tau[:], t2[:]))
                        e.chain(e.scalar_tensor_tensor(t2[:], t2[:], -1.0,
                                                       s2c[:], ALU.add,
                                                       ALU.add))
                        e.chain(e.tensor_mul(t2[:], ncc[:], t2[:]))
                        e.chain(e.tensor_mul(disc[:], t1[:], t1[:]))
                        e.chain(e.tensor_sub(disc[:], disc[:], t2[:]))
                        e.chain_inc(e.tensor_scalar(disc[:], disc[:], 0.0,
                                                    None, ALU.max, ALU.bypass),
                                    dcd)
                        e.wait_ge(sqd, B * n_mich + j + 1)
                        e.chain(e.tensor_sub(t1[:], t1[:], sqr[:]))
                        e.chain(e.reciprocal(t2[:], ncc[:]))
                        e.chain(e.tensor_mul(tau[:], t1[:], t2[:]))
                    else:
                        e.chain(e.tensor_scalar(t1[:], s2c[:], -1.0, None,
                                                ALU.add, ALU.bypass))
                        e.chain(e.tensor_scalar(t2[:], s1c[:], 2.0, None,
                                                ALU.mult, ALU.bypass))
                        e.chain(e.reciprocal(t3[:], t2[:]))
                        e.chain(e.tensor_mul(t1[:], t1[:], t3[:]))
                        e.chain(e.tensor_add(tau[:], tau[:], t1[:]))
                    e.chain(e.tensor_add(a_r[:], mx2[:], tau[:]))
                    e.chain_inc(e.tensor_scalar(na_r[:], a_r[:], -1.0, None,
                                                ALU.mult, ALU.bypass), rowd)
                for tc in range(NTC):
                    gr = B * NJ * NTC + NIT * NTC + tc
                    e.wait_ge(rel, gr + 1)
                    if B * NTC + tc >= 2:
                        e.wait_ge(ttd, B * NTC + tc - 1)
                    e.chain_inc(e.scalar_tensor_tensor(
                        w_bf[tc % 2][:], m[gr % 2][:], 0.0, m[gr % 2][:],
                        ALU.add, ALU.mult), sst)
                e.nop().then_inc(rowd, 1)

    @p.on("gpsimd")
    def _(e):
        e.wait_ge(xev, NSTEP * NH * NTC)
        for c in range(NH):
            rst.push(e.dma_start(xi3_out[c * 128:(c + 1) * 128, :],
                                 xi[NSTEP % 2][:, c * TOK:(c + 1) * TOK]))
        rst.wait_group(e)



def phase_wo(p, xi3_in, wo, consts, cring, s2_out):
    NTT = TOK // 128
    wo_bf = p.dtmp("wo_bf", [D, 1024], BF16)
    rwc = p.ring(1)
    rwt = p.ring(1)
    rxi = p.ring(1)
    rst = p.ring(4)
    pmd = p.sem()
    pev = p.sem()

    wo_fm = p.sb("wo2_fm", [128, 8 * D], BF16)
    xi_sb = p.sb("wo2_xi", [128, 8 * TOK], BF16)
    so = [p.sb("wo2_so", [128, D], F32) for _ in range(2)]
    ps_o = [p.ps("wo2_po", [128, D], F32) for _ in range(2)]

    @p.on("gpsimd")
    def _(e):
        for i in range(2):
            rwc.push(e.dma_start(wo_bf[i * 1024:(i + 1) * 1024, :],
                                 wo[i * 1024:(i + 1) * 1024, :]))
        for tt in range(NTT):
            e.wait_ge(pev, tt + 1)
            rst.push(e.dma_start(s2_out[tt * 128:(tt + 1) * 128, :],
                                 so[tt % 2][:]))
        rst.wait_group(e)

    @p.on("sync")
    def _(e):
        rwc.wait_group(e, 1)
        for c in range(8):
            rwt.push(e.dma_start(wo_fm[:, c * D:(c + 1) * D],
                                 wo_bf[:, c * 128:(c + 1) * 128],
                                 transpose=True))
        for c in range(8):
            rxi.push(e.dma_start(xi_sb[:, c * TOK:(c + 1) * TOK],
                                 xi3_in[c * 128:(c + 1) * 128, :]))

    @p.on("tensor")
    def _(e):
        cring.wait_group(e)
        rwt.wait_group(e, 7)
        rxi.wait_group(e, 7)
        for tt in range(NTT):
            if tt >= 2:
                e.wait_ge(pev, tt - 1)
            for kc in range(8):
                for c4 in range(4):
                    inst = e.matmul(
                        ps_o[tt % 2][:, c4 * 512:(c4 + 1) * 512],
                        xi_sb[:, kc * TOK + tt * 128:kc * TOK + (tt + 1) * 128],
                        wo_fm[:, kc * D + c4 * 512:kc * D + (c4 + 1) * 512],
                        start=(kc == 0), stop=(kc == 7),
                        skip_group_check=True)
                    if kc == 7 and c4 == 3:
                        inst.then_inc(pmd, 1)

    @p.on("scalar")
    def _(e):
        for tt in range(NTT):
            e.wait_ge(pmd, tt + 1)
            if tt >= 2:
                rst.wait(e, tt - 2)
            e.activation(so[tt % 2][:], ps_o[tt % 2][:], AF.Identity
                         ).then_inc(pev, 1)



def phase_moe(p, xn2_in, gate_w, w1, w3, w2, s2_in, x1_in, consts, cring,
              x2_out):
    E = 16
    I = 1408
    NI = I // 128
    NTT = TOK // 128
    NF = D // 128
    TH = TOK // 512

    rxt = p.ring(1)
    rgw = p.ring(1)
    rw13 = p.ring(16)
    rsx = p.ring(4)
    rst = p.ring(4)
    lmd = p.sem()
    lev = p.sem()
    sgm = p.sem()
    cmb = p.sem()
    gmd = p.sem()
    gev = p.sem()
    gud = p.sem()
    ymd = p.sem()
    yac = p.sem()
    sil = p.sem()
    fin = p.sem()

    xn2_dm = p.sb("mo_xn", [128, NF * TOK], BF16)
    gate_sb = p.sb("mo_gw", [128, NF * 16], BF16)
    logit = p.sb("mo_lg", [128, NTT * 16], F32)
    m1 = p.sb("mo_m1", [128, 1], F32)
    m2 = p.sb("mo_m2", [128, 1], F32)
    d12 = p.sb("mo_d12", [128, 1], F32)
    c1 = p.sb("mo_c1", [128, 1], F32)
    c2 = p.sb("mo_c2", [128, 1], F32)
    msk1 = p.sb("mo_mk1", [128, 16], F32)
    tmpl = p.sb("mo_tmp", [128, 16], F32)
    comb = p.sb("mo_cmb", [128, NTT * 16], F32)
    we_sb = p.sb("mo_we", [128, NF * I], BF16)
    sg = p.sb("mo_sg", [128, NI * TOK], BF16)
    up = p.sb("mo_up", [128, NI * TOK], BF16)
    moe = p.sb("mo_moe", [128, NTT * D], F32)
    s2t = p.sb("mo_s2", [128, D], F32)
    x2t = p.sb("mo_x2", [128, D], F32)
    ps_l = p.ps("mo_pl", [128, 16], F32)
    ps_g = [p.ps("mo_pg", [128, 512], F32) for _ in range(2)]
    ps_y = p.ps("mo_py", [128, D], F32)

    W13_PE = 43

    @p.on("sync")
    def _(e):
        for c in range(NF):
            rxt.push(e.dma_start(xn2_dm[:, c * TOK:(c + 1) * TOK],
                                 xn2_in[:, c * 128:(c + 1) * 128],
                                 transpose=True))
        for tt in range(NTT):
            if tt >= 1:
                e.wait_ge(fin, tt)
            rsx.push(e.dma_start(s2t[:], s2_in[tt * 128:(tt + 1) * 128, :]))
            e.wait_ge(sil, tt + 1)
            rsx.push(e.dma_start(moe[:, tt * D:(tt + 1) * D],
                                 x1_in[tt * 128:(tt + 1) * 128, :]))

    @p.on("gpsimd")
    def _(e):
        rgw.push(e.dma_start(
            gate_sb[:].rearrange("p (a c) -> p a c", a=NF),
            gate_w.rearrange("(a p) c -> p a c", p=128)))
        for ex in range(E):
            if ex >= 1:
                e.wait_ge(ymd, ex * NTT)
            for kc in range(NF):
                rw13.push(e.dma_start(we_sb[:, kc * I:(kc + 1) * I],
                                      w1[ex, kc * 128:(kc + 1) * 128, :]))
            e.wait_ge(gmd, (2 * ex + 1) * NI * TH)
            for kc in range(NF):
                rw13.push(e.dma_start(we_sb[:, kc * I:(kc + 1) * I],
                                      w3[ex, kc * 128:(kc + 1) * 128, :]))
            e.wait_ge(gmd, (2 * ex + 2) * NI * TH)
            w2v = we_sb[:].rearrange("p (a c) -> p a c", a=NI, c=2048)
            for ic in range(NI):
                rw13.push(e.dma_start(w2v[:, ic, :],
                                      w2[ex, ic * 128:(ic + 1) * 128, :]))
        for tt in range(NTT):
            e.wait_ge(fin, tt + 1)
            rst.push(e.dma_start(x2_out[tt * 128:(tt + 1) * 128, :],
                                 x2t[:]))
        rst.wait_group(e)

    @p.on("tensor")
    def _(e):
        cring.wait_group(e)
        rxt.wait_group(e, NF - 1)
        rgw.wait_group(e, 0)
        for tt in range(NTT):
            if tt >= 1:
                e.wait_ge(lev, tt)
            for kc in range(NF):
                inst = e.matmul(ps_l[:],
                                xn2_dm[:, kc * TOK + tt * 128:
                                       kc * TOK + (tt + 1) * 128],
                                gate_sb[:, kc * 16:(kc + 1) * 16],
                                start=(kc == 0), stop=(kc == NF - 1))
                if kc == NF - 1:
                    inst.then_inc(lmd, 1)
        for ex in range(E):
            for half in range(2):
                for ic in range(NI):
                    for th in range(TH):
                        n = (2 * ex + half) * NI * TH + ic * TH + th
                        if n >= 2:
                            e.wait_ge(gev, n - 1)
                        for kc in range(NF):
                            if ic == 0 and th == 0:
                                rw13.wait(e, ex * W13_PE + half * NF + kc)
                            inst = e.matmul(
                                ps_g[n % 2][:],
                                we_sb[:, kc * I + ic * 128:
                                      kc * I + (ic + 1) * 128],
                                xn2_dm[:, kc * TOK + th * 512:
                                       kc * TOK + (th + 1) * 512],
                                start=(kc == 0), stop=(kc == NF - 1))
                            if kc == NF - 1:
                                inst.then_inc(gmd, 1)
            e.wait_ge(gud, ex * NI + NI)
            for tt in range(NTT):
                yg = ex * NTT + tt
                e.wait_ge(yac, yg + 1)
                for ic in range(NI):
                    if tt == 0:
                        rw13.wait(e, ex * W13_PE + 2 * NF + ic)
                    for c4 in range(4):
                        inst = e.matmul(
                            ps_y[:, c4 * 512:(c4 + 1) * 512],
                            up[:, ic * TOK + tt * 128:ic * TOK + (tt + 1) * 128],
                            we_sb[:, ic * 2048 + c4 * 512:
                                  ic * 2048 + (c4 + 1) * 512],
                            start=(ic == 0), stop=(ic == NI - 1),
                            skip_group_check=True)
                        if ic == NI - 1 and c4 == 3:
                            inst.then_inc(ymd, 1)

    @p.on("scalar")
    def _(e):
        for tt in range(NTT):
            e.wait_ge(lmd, tt + 1)
            e.activation(logit[:, tt * 16:(tt + 1) * 16], ps_l[:],
                         AF.Identity).then_inc(lev, 1)
        for tt in range(NTT):
            e.wait_ge(sgm, 3 * tt + 1)
            e.chain_inc(e.activation(c1[:], d12[:], AF.Sigmoid), sgm)
            e.chain_inc(e.activation(c2[:], d12[:], AF.Sigmoid,
                                     scale=-1.0), sgm)
        for ex in range(E):
            for half in range(2):
                for ic in range(NI):
                    for th in range(TH):
                        n = (2 * ex + half) * NI * TH + ic * TH + th
                        e.wait_ge(gmd, n + 1)
                        if half == 0:
                            e.wait_ge(gud, max(0, (ex - 1) * NI + ic + 1))
                            e.activation(sg[:, ic * TOK + th * 512:
                                            ic * TOK + (th + 1) * 512],
                                         ps_g[n % 2][:], AF.Silu
                                         ).then_inc(gev, 1)
                        else:
                            e.wait_ge(ymd, ex * NTT)
                            e.activation(up[:, ic * TOK + th * 512:
                                            ic * TOK + (th + 1) * 512],
                                         ps_g[n % 2][:], AF.Identity
                                         ).then_inc(gev, 1)
        for tt in range(NTT):
            e.wait_ge(yac, 1 + (E - 1) * NTT + tt + 1)
            e.wait_ge(fin, tt)
            if tt >= 1:
                rst.wait(e, tt - 1)
            e.activation(x2t[:], moe[:, tt * D:(tt + 1) * D], AF.Silu
                         ).then_inc(sil, 1)

    @p.on("vector")
    def _(e):
        for tt in range(NTT):
            e.wait_ge(lev, tt + 1)
            lg = logit[:, tt * 16:(tt + 1) * 16]
            e.chain(e.reduce_max(m1[:], lg, axis=AX.X))
            e.chain(e.tensor_scalar(msk1[:], lg, m1[:], None,
                                    ALU.is_equal, ALU.bypass))
            e.chain(e.scalar_tensor_tensor(tmpl[:], msk1[:], -1e30, lg,
                                           ALU.mult, ALU.add))
            e.chain(e.reduce_max(m2[:], tmpl[:], axis=AX.X))
            e.chain_inc(e.tensor_sub(d12[:], m1[:], m2[:]), sgm)
            e.wait_ge(sgm, 3 * tt + 3)
            e.chain(e.tensor_scalar(comb[:, tt * 16:(tt + 1) * 16],
                                    msk1[:], c1[:], None, ALU.mult,
                                    ALU.bypass))
            e.chain(e.tensor_scalar(tmpl[:], tmpl[:], m2[:], None,
                                    ALU.is_equal, ALU.bypass))
            e.chain_inc(e.scalar_tensor_tensor(
                comb[:, tt * 16:(tt + 1) * 16], tmpl[:], c2[:],
                comb[:, tt * 16:(tt + 1) * 16], ALU.mult, ALU.add), cmb)
        e.chain(e.memset(moe[:], 0.0))
        e.nop().then_inc(yac, 1)
        for ex in range(E):
            for ic in range(NI):
                for th in range(TH):
                    ng = (2 * ex + 1) * NI * TH + ic * TH + th
                    e.wait_ge(gev, ng + 1)
                e.chain_inc(e.tensor_mul(up[:, ic * TOK:(ic + 1) * TOK],
                                         sg[:, ic * TOK:(ic + 1) * TOK],
                                         up[:, ic * TOK:(ic + 1) * TOK]),
                            gud)
            for tt in range(NTT):
                yg = ex * NTT + tt
                e.wait_ge(ymd, yg + 1)
                e.wait_ge(cmb, tt + 1)
                for c4 in range(4):
                    inst = e.scalar_tensor_tensor(
                        moe[:, tt * D + c4 * 512:tt * D + (c4 + 1) * 512],
                        ps_y[:, c4 * 512:(c4 + 1) * 512],
                        comb[:, tt * 16 + ex:tt * 16 + ex + 1],
                        moe[:, tt * D + c4 * 512:tt * D + (c4 + 1) * 512],
                        ALU.mult, ALU.add)
                    if c4 == 3:
                        e.chain_inc(inst, yac)
                    else:
                        e.chain(inst)
        for tt in range(NTT):
            e.wait_ge(sil, tt + 1)
            rsx.wait(e, 2 * tt)
            rsx.wait(e, 2 * tt + 1)
            e.chain(e.tensor_mul(x2t[:], x2t[:], s2t[:]))
            e.chain_inc(e.tensor_add(x2t[:], x2t[:],
                                     moe[:, tt * D:(tt + 1) * D]), fin)



def phase_kv_full(p, xn_full, w_attn, tabk, consts, cring, k_out, v_out):
    NF = D // 128
    NB = S // TOK
    TH = TOK // 512
    perm = consts["permsign"]

    rxn = p.ring(2, per_slot=D // 128)
    rtab = p.ring(2, per_slot=2)
    rwk = p.ring(1)
    rwv = p.ring(1)
    rstk = p.ring(4)
    rstv = p.ring(4)
    ped = p.sem()
    rotd = p.sem()
    qrd = p.sem()
    rpd = p.sem()
    pvd = p.sem()
    vevd = p.sem()
    xnc = p.sem()

    xn_dm = [p.sb("kvf_xn", [128, NF * TOK], BF16) for _ in range(2)]
    wk_sb = p.sb("kvf_wk", [128, NF * 512], BF16)
    wv_sb = p.sb("kvf_wv", [128, NF * 2048], BF16)
    ck = [p.sb("kvf_ck", [128, TOK], F32) for _ in range(2)]
    sk = [p.sb("kvf_sk", [128, TOK], F32) for _ in range(2)]
    qraw = [p.sb("kvf_qraw", [128, 512], F32) for _ in range(2)]
    t1 = p.sb("kvf_t1", [128, 512], F32)
    t2 = p.sb("kvf_t2", [128, 512], F32)
    qo = [p.sb("kvf_qo", [128, 512], BF16) for _ in range(2)]
    vo = [p.sb("kvf_vo", [128, 512], BF16) for _ in range(2)]
    pqk = [p.ps("kvf_pqk", [128, 512], F32) for _ in range(2)]
    prot = [p.ps("kvf_prot", [128, 512], F32) for _ in range(2)]
    pv = [p.ps("kvf_pv", [128, 512], F32) for _ in range(2)]

    KC = 4
    NV = (TOK // 128) * 4

    @p.on("sync")
    def _(e):
        for tb in range(NB):
            if tb >= 2:
                e.wait_ge(xnc, tb - 1)
            for c in range(NF):
                rxn.push_slot(e.dma_start(
                    xn_dm[tb % 2][:, c * TOK:(c + 1) * TOK],
                    xn_full[tb * TOK:(tb + 1) * TOK, c * 128:(c + 1) * 128],
                    transpose=True), tb)
            rtab.push_slot(e.dma_start(
                ck[tb % 2][:], tabk["cosk"][:, tb * TOK:(tb + 1) * TOK]), tb)
            rtab.push_slot(e.dma_start(
                sk[tb % 2][:], tabk["sink"][:, tb * TOK:(tb + 1) * TOK]), tb)
            for c in range(KC):
                for h in range(TH):
                    n = (tb * KC + c) * TH + h
                    e.wait_ge(rpd, n + 1)
                    rstk.push(e.dma_start(
                        k_out[c * 128:(c + 1) * 128,
                              tb * TOK + h * 512:tb * TOK + (h + 1) * 512],
                        qo[n % 2][:]))
            for tc in range(TOK // 128):
                for vc in range(4):
                    n = tb * NV + 4 * tc + vc
                    e.wait_ge(vevd, n + 1)
                    rstv.push(e.dma_start(
                        v_out[tb * TOK + tc * 128:tb * TOK + (tc + 1) * 128,
                              vc * 512:(vc + 1) * 512],
                        vo[n % 2][:]))
        rstk.wait_group(e)
        rstv.wait_group(e)

    @p.on("gpsimd")
    def _(e):
        rwk.push(e.dma_start(
            wk_sb[:].rearrange("p (a c) -> p a c", a=NF),
            w_attn[:, 2048:2560].rearrange("(a p) c -> p a c", p=128)))
        rwv.push(e.dma_start(
            wv_sb[:].rearrange("p (a c) -> p a c", a=NF),
            w_attn[:, 2560:4608].rearrange("(a p) c -> p a c", p=128)))

    @p.on("tensor")
    def _(e):
        cring.wait_group(e)
        rwk.wait_group(e, 0)
        rwv.wait_group(e, 0)
        for tb in range(NB):
            rxn.wait_slot(e, tb)
            rtab.wait_slot(e, tb)
            for c in range(KC):
                for h in range(TH):
                    n = (tb * KC + c) * TH + h
                    if n >= 2:
                        e.wait_ge(qrd, n - 1)
                    for kc in range(NF):
                        inst = e.matmul(
                            pqk[n % 2][:],
                            wk_sb[:, kc * 512 + c * 128:kc * 512 + (c + 1) * 128],
                            xn_dm[tb % 2][:, kc * TOK + h * 512:
                                          kc * TOK + h * 512 + 512],
                            start=(kc == 0), stop=(kc == NF - 1))
                        if kc == NF - 1:
                            inst.then_inc(ped, 1)
                    e.wait_ge(qrd, n + 1)
                    if n >= 2:
                        e.wait_ge(rpd, n - 1)
                    e.matmul(prot[n % 2][:], perm[:], qraw[n % 2][:],
                             start=True, stop=True).then_inc(rotd, 1)
            for tc in range(TOK // 128):
                for vc in range(4):
                    n = tb * NV + 4 * tc + vc
                    if n >= 2:
                        e.wait_ge(vevd, n - 1)
                    for kc in range(NF):
                        inst = e.matmul(
                            pv[n % 2][:],
                            xn_dm[tb % 2][:, kc * TOK + tc * 128:
                                          kc * TOK + (tc + 1) * 128],
                            wv_sb[:, kc * 2048 + vc * 512:
                                  kc * 2048 + (vc + 1) * 512],
                            start=(kc == 0), stop=(kc == NF - 1))
                        if kc == NF - 1:
                            inst.then_inc(pvd, 1)
            e.nop().then_inc(xnc, 1)

    @p.on("scalar")
    def _(e):
        for tb in range(NB):
            for c in range(KC):
                for h in range(TH):
                    n = (tb * KC + c) * TH + h
                    e.wait_ge(ped, n + 1)
                    if n >= 2:
                        e.wait_ge(rotd, n - 1)
                        e.wait_ge(rpd, n - 1)
                    e.activation(qraw[n % 2][:], pqk[n % 2][:], AF.Identity
                                 ).then_inc(qrd, 1)
            for tc in range(TOK // 128):
                for vc in range(4):
                    n = tb * NV + 4 * tc + vc
                    e.wait_ge(pvd, n + 1)
                    if n >= 2:
                        rstv.wait(e, n - 2)
                    e.activation(vo[n % 2][:], pv[n % 2][:], AF.Identity
                                 ).then_inc(vevd, 1)

    @p.on("vector")
    def _(e):
        for tb in range(NB):
            for c in range(KC):
                for h in range(TH):
                    n = (tb * KC + c) * TH + h
                    e.wait_ge(qrd, n + 1)
                    e.chain(e.tensor_mul(
                        t1[:], qraw[n % 2][:],
                        ck[tb % 2][:, h * 512:(h + 1) * 512]))
                    e.wait_ge(rotd, n + 1)
                    e.chain(e.tensor_mul(
                        t2[:], prot[n % 2][:],
                        sk[tb % 2][:, h * 512:(h + 1) * 512]))
                    if n >= 2:
                        rstk.wait(e, n - 2)
                    e.chain_inc(e.tensor_add(qo[n % 2][:], t1[:], t2[:]), rpd)


def phase_absum_full(p, w1, w2, out):
    r1 = p.ring(4)
    red = p.sem()
    fin = p.sem()
    rst = p.ring(1)
    ro = p.ring(1)

    wbuf = [p.sb("abf_w", [128, 8192], F32) for _ in range(2)]
    rcol = p.sb("abf_r", [128, 1], F32)
    acc = p.sb("abf_acc", [128, 2], F32)
    onec = p.sb("abf_one", [128, 1], F32)
    osb = p.sb("abf_o", [2, 1], F32)
    pso = p.ps("abf_ps", [2, 1], F32)
    ones_in = p.din("ones_col_f32_ab", [128, 1], F32)

    chunks = [(w1[i * 128:(i + 1) * 128, :], 2048, 0) for i in range(64)] + \
             [(w2[i * 128:(i + 1) * 128, :], 8192, 1) for i in range(16)]

    @p.on("sync")
    def _(e):
        ro.push(e.dma_start(onec[:], ones_in))
        for i, (src, w, col) in enumerate(chunks):
            if i >= 2:
                e.wait_ge(red, i - 1)
            r1.push(e.dma_start(wbuf[i % 2][:, 0:w], src))

    @p.on("vector")
    def _(e):
        e.chain(e.memset(acc[:], 0.0))
        for i, (src, w, col) in enumerate(chunks):
            r1.wait(e, i)
            e.chain(e.reduce_sum(rcol[:], wbuf[i % 2][:, 0:w], axis=AX.X,
                                 apply_absolute_value=True))
            e.chain_inc(e.tensor_add(acc[:, col:col + 1], acc[:, col:col + 1],
                                     rcol[:]), red)
        e.nop().then_inc(red, 1)

    @p.on("tensor")
    def _(e):
        e.wait_ge(red, len(chunks) + 1)
        ro.wait_group(e)
        e.matmul(pso[:], acc[:], onec[:], start=True, stop=True
                 ).then_inc(fin, 1)

    @p.on("scalar")
    def _(e):
        e.wait_ge(fin, 1)
        e.activation(osb[:], pso[:], AF.Identity).then_inc(fin, 1)

    @p.on("gpsimd")
    def _(e):
        e.wait_ge(fin, 2)
        rst.push(e.dma_start(out, osb[:]))
        rst.wait_group(e)


def phase_ffn_thr2(p, absums, consts, cring, thr_bc, nthr_bc):
    ident = consts["ident_f32"]
    ones_row = consts["ones_row_f32"]
    ra = p.ring(1)
    ms = p.sem()

    tsb = p.sb("t2_t", [2, 1], F32)
    asb = p.sb("t2_a", [2, 1], F32)
    trow = p.sb("t2_r", [1, 2], F32)
    pst = p.ps("t2_p", [128, 2], F32)

    @p.on("sync")
    def _(e):
        ra.push(e.dma_start(asb[:], absums))

    @p.on("tensor")
    def _(e):
        cring.wait_group(e)
        e.wait_ge(ms, 1)
        e.matmul(pst[0:1, 0:2], tsb[:], ident[0:2, 0:2],
                 start=True, stop=True).then_inc(ms, 1)
        e.wait_ge(ms, 3)
        e.matmul(pst[:, 0:2], ones_row[:], trow[:],
                 start=True, stop=True).then_inc(ms, 1)

    @p.on("scalar")
    def _(e):
        ra.wait_group(e)
        e.activation(tsb[:], asb[:], AF.Identity,
                     scale=0.5 / (8192.0 * 2048.0)).then_inc(ms, 1)
        e.wait_ge(ms, 2)
        e.activation(trow[:], pst[0:1, 0:2], AF.Identity).then_inc(ms, 1)
        e.wait_ge(ms, 4)
        e.chain(e.activation(thr_bc[:], pst[:, 0:2], AF.Identity))
        e.chain_inc(e.activation(nthr_bc[:], thr_bc[:], AF.Identity,
                                 scale=-1.0), ms)


def phase_memkv_full(p, mem, g_mem, b_mem, wk, wv, log_beta, consts, cring,
                     kb_out, v_out):
    Md = 1024
    NR = 2048
    ones_cf = consts["ones_col_f32"]

    mn_d = p.dtmp("mkf_mn", [NR, Md], BF16)
    m0 = p.scope()
    phase_lnmem(p, mem, mn_d, NR, Md)
    p.barrier()
    p.pop_scope(m0)

    rgb = p.ring(1)
    grow = p.sb("mkf_gr", [1, Md], F32)
    brow = p.sb("mkf_br", [1, Md], F32)
    lbrow = p.sb("mkf_lb", [1, 8], F32)
    betar = p.sb("mkf_be", [1, 8], F32)
    beta_bc = p.sb("mkf_bbc", [128, 8], F32)

    @p.on("sync")
    def _(e):
        rgb.push(e.dma_start(grow[:], g_mem))
        rgb.push(e.dma_start(brow[:], b_mem))
        rgb.push(e.dma_start(lbrow[:], log_beta))

    @p.on("tensor")
    def _(e):
        rgb.wait_group(e, 2)
        cring.wait_group(e)
    gcol, gdn = emit_cols_from_row(p, grow, 8, ones_cf, "mkf_gc")
    bcol, bdn = emit_cols_from_row(p, brow, 8, ones_cf, "mkf_bc")

    wkv_bf = p.dtmp("mkf_wkv", [2 * Md, Md], BF16)
    rwc = p.ring(2)
    rwt = p.ring(1)
    rmt = p.ring(1)
    rstk = p.ring(4)
    rstv = p.ring(4)
    bbd = p.sem()
    gbd = p.sem()
    kmd = p.sem()
    kev = p.sem()
    vmd = p.sem()
    vev = p.sem()

    wk_fm = p.sb("mkf_wkfm", [128, 8 * Md], BF16)
    wv_fm = p.sb("mkf_wvfm", [128, 8 * Md], BF16)
    mn_dm = p.sb("mkf_mndm", [128, 8 * NR], BF16)
    kbo = [p.sb("mkf_kbo", [128, 512], BF16) for _ in range(2)]
    vo = [p.sb("mkf_vo", [128, 512], BF16) for _ in range(2)]
    ps_k = [p.ps("mkf_psk", [128, 512], F32) for _ in range(2)]
    ps_v = [p.ps("mkf_psv", [128, 512], F32) for _ in range(2)]
    ps_b = p.ps("mkf_psb", [128, 8], F32)

    NKG = 8 * 4
    NVG = 16 * 2

    @p.on("gpsimd")
    def _(e):
        rwc.push(e.dma_start(wkv_bf[0:Md, :], wk))
        rwc.push(e.dma_start(wkv_bf[Md:2 * Md, :], wv))
        for g in range(NKG):
            oc, nb = divmod(g, 4)
            e.wait_ge(kev, g + 1)
            rstk.push(e.dma_start(
                kb_out[oc * 128:(oc + 1) * 128, nb * 512:(nb + 1) * 512],
                kbo[g % 2][:]))
        for g in range(NVG):
            nc_, th = divmod(g, 2)
            e.wait_ge(vev, g + 1)
            rstv.push(e.dma_start(
                v_out[nc_ * 128:(nc_ + 1) * 128, th * 512:(th + 1) * 512],
                vo[g % 2][:]))
        rstk.wait_group(e)
        rstv.wait_group(e)

    @p.on("sync")
    def _(e):
        rwc.wait_group(e, 1)
        for c in range(8):
            rwt.push(e.dma_start(wk_fm[:, c * Md:(c + 1) * Md],
                                 wkv_bf[0:Md, c * 128:(c + 1) * 128],
                                 transpose=True))
        for c in range(8):
            rwt.push(e.dma_start(wv_fm[:, c * Md:(c + 1) * Md],
                                 wkv_bf[Md:2 * Md, c * 128:(c + 1) * 128],
                                 transpose=True))
        for c in range(8):
            rmt.push(e.dma_start(mn_dm[:, c * NR:(c + 1) * NR],
                                 mn_d[:, c * 128:(c + 1) * 128],
                                 transpose=True))

    @p.on("vector")
    def _(e):
        rmt.wait_group(e, 7)
        e.wait_ge(gdn, 2)
        e.wait_ge(bdn, 2)
        for c in range(8):
            e.chain(e.tensor_scalar(mn_dm[:, c * NR:(c + 1) * NR],
                                    mn_dm[:, c * NR:(c + 1) * NR],
                                    gcol[:, c:c + 1], bcol[:, c:c + 1],
                                    ALU.mult, ALU.add))
        e.nop().then_inc(gbd, 1)
        for g in range(NKG):
            oc = g // 4
            e.wait_ge(kmd, g + 1)
            if g >= 2:
                rstk.wait(e, g - 2)
            e.chain_inc(e.tensor_scalar(kbo[g % 2][:], ps_k[g % 2][:],
                                        beta_bc[:, oc:oc + 1], None,
                                        ALU.mult, ALU.bypass), kev)

    @p.on("scalar")
    def _(e):
        rgb.wait_group(e, 2)
        e.activation(betar[:], lbrow[:], AF.Exp).then_inc(bbd, 1)
        e.wait_ge(bbd, 2)
        e.activation(beta_bc[:], ps_b[:], AF.Identity).then_inc(bbd, 1)
        for g in range(NVG):
            e.wait_ge(vmd, g + 1)
            if g >= 2:
                rstv.wait(e, g - 2)
            e.activation(vo[g % 2][:], ps_v[g % 2][:], AF.Identity
                         ).then_inc(vev, 1)

    @p.on("tensor")
    def _(e):
        e.wait_ge(bbd, 1)
        e.matmul(ps_b[:], consts["ones_row_f32"][:], betar[:],
                 start=True, stop=True).then_inc(bbd, 1)
        rwt.wait_group(e, 15)
        e.wait_ge(gbd, 1)
        e.wait_ge(bbd, 3)
        for g in range(NKG):
            oc, nb = divmod(g, 4)
            if g >= 2:
                e.wait_ge(kev, g - 1)
            for kc in range(8):
                inst = e.matmul(
                    ps_k[g % 2][:],
                    wk_fm[:, kc * Md + oc * 128:kc * Md + (oc + 1) * 128],
                    mn_dm[:, kc * NR + nb * 512:kc * NR + (nb + 1) * 512],
                    start=(kc == 0), stop=(kc == 7))
                if kc == 7:
                    inst.then_inc(kmd, 1)
        for g in range(NVG):
            nc_, th = divmod(g, 2)
            if g >= 2:
                e.wait_ge(vev, g - 1)
            for kc in range(8):
                inst = e.matmul(
                    ps_v[g % 2][:],
                    mn_dm[:, kc * NR + nc_ * 128:kc * NR + (nc_ + 1) * 128],
                    wv_fm[:, kc * Md + th * 512:kc * Md + (th + 1) * 512],
                    start=(kc == 0), stop=(kc == 7))
                if kc == 7:
                    inst.then_inc(vmd, 1)



def build_pre():
    p = Prog()
    x_in = p.din("x", [TOK, D], F32)
    w_attn = p.din("w_attn", [D, 4608], F32)
    tabs = {t: p.din(t, [128, TOK], F32)
            for t in ("cosq", "sinq", "cosk", "sink")}
    xn_out = p.dout("xn", [TOK, D], BF16)
    q_out = p.dout("q", [2048, TOK], BF16)
    k_out = p.dout("k", [512, TOK], BF16)
    v_out = p.dout("v", [TOK, 2048], BF16)
    consts, cring = load_consts(p, ["permsign"])
    m = p.scope()
    phase_ln(p, x_in, xn_out)
    p.barrier()
    p.pop_scope(m)
    phase_qkv(p, xn_out, w_attn, tabs, consts, cring, q_out, k_out, v_out)
    return p.finalize()


def build_attn():
    p = Prog()
    q_in = p.din("q", [2048, TOK], BF16)
    k_in = p.din("k", [512, S], BF16)
    v_in = p.din("v", [S, 2048], BF16)
    ctx_out = p.dout("ctx", [8192, TOK], BF16)
    consts, cring = load_consts(p, ["ones_col_bf", "ones_row_f32"])
    phase_attn(p, q_in, k_in, v_in, consts, cring, ctx_out)
    return p.finalize()


def build_wout():
    p = Prog()
    ctx_in = p.din("ctx", [8192, TOK], BF16)
    w_out = p.din("w_out", [8192, D], F32)
    b_out = p.din("b_out", [1, D], F32)
    s_out = p.dout("s", [TOK, D], F32)
    consts, cring = load_consts(p, ["ones_row_f32"])
    phase_wout(p, ctx_in, w_out, b_out, consts, cring, s_out)
    return p.finalize()


def build_xpre2():
    p = Prog()
    x1_in = p.din("x1", [TOK, D], F32)
    g_state = p.din("g_state", [1, D], F32)
    b_state = p.din("b_state", [1, D], F32)
    wq = p.din("wq", [1024, D], F32)
    xn2_out = p.dout("xn2", [TOK, D], BF16)
    xi0_out = p.dout("xi0", [1024, TOK], BF16)
    tq = p.dtmp("tq", [TOK, D], BF16)
    consts, cring = load_consts(p, ["ones_col_f32", "ones_row_f32"])
    m = p.scope()
    phase_ln(p, x1_in, xn2_out)
    p.barrier()
    p.pop_scope(m)
    m = p.scope()
    phase_ln2_from_bf(p, xn2_out, tq)
    p.barrier()
    p.pop_scope(m)
    phase_qproj(p, tq, g_state, b_state, wq, consts, cring, xi0_out)
    return p.finalize()


def build_memkv():
    p = Prog()
    mem_s = p.din("mem", [2048 // NCORE, 1024], F32)
    g_mem = p.din("g_mem", [1, 1024], F32)
    b_mem = p.din("b_mem", [1, 1024], F32)
    wk = p.din("wk", [1024, 1024], F32)
    wv = p.din("wv", [1024, 1024], F32)
    log_beta = p.din("log_beta", [1, 8], F32)
    kb_out = p.dout("kb", [1024, 2048 // NCORE], BF16)
    v_out = p.dout("v", [2048 // NCORE, 1024], BF16)
    consts, cring = load_consts(p, ["ones_col_f32", "ones_row_f32"])
    phase_memkv(p, mem_s, g_mem, b_mem, wk, wv, log_beta, consts, cring,
                kb_out, v_out)
    return p.finalize()


def build_entmax(NSTEP=3, NH=8):
    p = Prog()
    xi0 = p.din("xi0", [1024, TOK], BF16)
    kb = p.din("kb", [1024, 2048], BF16)
    v = p.din("v", [2048, 1024], BF16)
    xi3 = p.dout("xi3", [1024, TOK], BF16)
    consts, cring = load_consts(p, ["ident"])
    phase_entmax(p, xi0, kb, v, consts, cring, xi3, NSTEP=NSTEP, NH=NH)
    return p.finalize()


def build_womoe():
    p = Prog()
    xi3 = p.din("xi3", [1024, TOK], BF16)
    wo = p.din("wo", [D, 1024], F32)
    xn2 = p.din("xn2", [TOK, D], BF16)
    gate_w = p.din("gate_w", [D, 16], F32)
    w1 = p.din("w1", [16, D, 1408], F32)
    w3 = p.din("w3", [16, D, 1408], F32)
    w2 = p.din("w2", [16, 1408, D], F32)
    x1 = p.din("x1", [TOK, D], F32)
    x2 = p.dout("x2", [TOK, D], F32)
    s2_d = p.dtmp("s2", [TOK, D], F32)
    consts, cring = load_consts(p, [])
    m = p.scope()
    phase_wo(p, xi3, wo, consts, cring, s2_d)
    p.barrier()
    p.pop_scope(m)
    phase_moe(p, xn2, gate_w, w1, w3, w2, s2_d, x1, consts, cring, x2)
    return p.finalize()


def build_absum():
    p = Prog()
    w1s = p.din("w1s", [8192 // NCORE, D], F32)
    w2s = p.din("w2s", [D // NCORE, 8192], F32)
    out = p.dout("absum", [1, 2], F32)
    phase_absum(p, w1s, w2s, out)
    return p.finalize()


def build_ffn():
    p = Prog()
    xn = p.din("xn", [TOK, D], BF16)
    w1 = p.din("w1", [8192, D], F32)
    b1 = p.din("b1", [1, 8192], F32)
    w2 = p.din("w2", [D, 8192], F32)
    b2 = p.din("b2", [1, D], F32)
    absums = p.din("absums", [8, 2], F32)
    s_in = p.din("s", [TOK, D], F32)
    x_in = p.din("x", [TOK, D], F32)
    x1_out = p.dout("x1", [TOK, D], F32)
    h_dram = p.dtmp("h", [8192, TOK], BF16)
    w2q_dram = p.dtmp("w2q", [8192, D], BF16)
    consts, cring = load_consts(
        p, ["ident", "ident_f32", "ones_row_f32", "ones_col_f32"])
    thr_bc = p.sb("thr", [128, 2], F32)
    nthr_bc = p.sb("nthr", [128, 2], F32)
    m = p.scope()
    phase_ffn_thr(p, absums, consts, cring, thr_bc, nthr_bc)
    p.barrier()
    p.pop_scope(m)
    m = p.scope()
    phase_ffn_h(p, xn, w1, b1, thr_bc, nthr_bc, consts, cring, h_dram, None)
    p.barrier()
    p.pop_scope(m)
    m = p.scope()
    phase_ffn_w2q(p, w2, thr_bc, nthr_bc, consts, w2q_dram)
    p.barrier()
    p.pop_scope(m)
    phase_ffn_out(p, h_dram, w2q_dram, b2, s_in, x_in, consts, x1_out)
    return p.finalize()


def build_fused(bases=None):
    p = Prog(bases=bases)
    x_full = p.din("x_full", [S, D], F32)
    x_own = p.din("x_own", [TOK, D], F32)
    w_attn = p.din("w_attn", [D, 4608], F32)
    tabs = {t: p.din(t, [128, TOK], F32)
            for t in ("cosq", "sinq", "cosk", "sink")}
    tabkf = {t: p.din(t + "_full", [128, S], F32) for t in ("cosk", "sink")}
    w_out = p.din("w_out", [8192, D], F32)
    b_out = p.din("b_out", [1, D], F32)
    f_w1 = p.din("f_w1", [8192, D], F32)
    f_b1 = p.din("f_b1", [1, 8192], F32)
    f_w2 = p.din("f_w2", [D, 8192], F32)
    f_b2 = p.din("f_b2", [1, D], F32)
    wq = p.din("wq", [1024, D], F32)
    g_state = p.din("g_state", [1, D], F32)
    b_state = p.din("b_state", [1, D], F32)
    mem = p.din("mem", [2048, 1024], F32)
    g_mem = p.din("g_mem", [1, 1024], F32)
    b_mem = p.din("b_mem", [1, 1024], F32)
    wk = p.din("wk", [1024, 1024], F32)
    wv = p.din("wv", [1024, 1024], F32)
    wo = p.din("wo", [D, 1024], F32)
    log_beta = p.din("log_beta", [1, 8], F32)
    gate_w = p.din("gate_w", [D, 16], F32)
    m_w1 = p.din("m_w1", [16, D, 1408], F32)
    m_w3 = p.din("m_w3", [16, D, 1408], F32)
    m_w2 = p.din("m_w2", [16, 1408, D], F32)
    x2_out = p.dout("x2", [TOK, D], F32)

    xn_own = p.dtmp("f_xnown", [TOK, D], BF16)
    xn_full = p.dtmp("f_xnfull", [S, D], BF16)
    q_d = p.dtmp("f_q", [2048, TOK], BF16)
    k_d = p.dtmp("f_k", [512, S], BF16)
    v_d = p.dtmp("f_v", [S, 2048], BF16)
    ctx_d = p.dtmp("f_ctx", [8192, TOK], BF16)
    s_d = p.dtmp("f_s", [TOK, D], F32)
    ab_d = p.dtmp("f_ab", [2, 1], F32)
    h_d = p.dtmp("f_h", [8192, TOK], BF16)
    w2q_d = p.dtmp("f_w2q", [8192, D], BF16)
    x1_d = p.dtmp("f_x1", [TOK, D], F32)
    xn2_d = p.dtmp("f_xn2", [TOK, D], BF16)
    tq_d = p.dtmp("f_tq", [TOK, D], BF16)
    xi0_d = p.dtmp("f_xi0", [1024, TOK], BF16)
    kb_d = p.dtmp("f_kb", [1024, 2048], BF16)
    v2_d = p.dtmp("f_v2", [2048, 1024], BF16)
    xi3_d = p.dtmp("f_xi3", [1024, TOK], BF16)
    s2_d = p.dtmp("f_s2", [TOK, D], F32)

    consts, cring = load_consts(
        p, ["permsign", "ident", "ident_f32", "ones_col_bf",
            "ones_row_f32", "ones_col_f32"])
    thr_bc = p.sb("f_thr", [128, 2], F32)
    nthr_bc = p.sb("f_nthr", [128, 2], F32)

    def ph(f):
        m = p.scope()
        f()
        p.barrier(reset=True)
        p.pop_scope(m)

    ph(lambda: phase_ln(p, x_own, xn_own))
    ph(lambda: phase_ln(p, x_full, xn_full, T=S))
    ph(lambda: phase_qkv(p, xn_own, w_attn, tabs, consts, cring,
                         q_d, None, None, qonly=True))
    ph(lambda: phase_kv_full(p, xn_full, w_attn, tabkf, consts, cring,
                             k_d, v_d))
    ph(lambda: phase_attn(p, q_d, k_d, v_d, consts, cring, ctx_d))
    ph(lambda: phase_wout(p, ctx_d, w_out, b_out, consts, cring, s_d))
    ph(lambda: phase_absum_full(p, f_w1, f_w2, ab_d))
    ph(lambda: phase_ffn_thr2(p, ab_d, consts, cring, thr_bc, nthr_bc))
    ph(lambda: phase_ffn_h(p, xn_own, f_w1, f_b1, thr_bc, nthr_bc, consts,
                           cring, h_d, None))
    ph(lambda: phase_ffn_w2q(p, f_w2, thr_bc, nthr_bc, consts, w2q_d))
    ph(lambda: phase_ffn_out(p, h_d, w2q_d, f_b2, s_d, x_own, consts, x1_d))
    ph(lambda: phase_ln(p, x1_d, xn2_d))
    ph(lambda: phase_ln2_from_bf(p, xn2_d, tq_d))
    ph(lambda: phase_qproj(p, tq_d, g_state, b_state, wq, consts, cring,
                           xi0_d))
    ph(lambda: phase_memkv_full(p, mem, g_mem, b_mem, wk, wv, log_beta,
                                consts, cring, kb_d, v2_d))
    ph(lambda: phase_entmax(p, xi0_d, kb_d, v2_d, consts, cring, xi3_d))
    ph(lambda: phase_wo(p, xi3_d, wo, consts, cring, s2_d))
    m = p.scope()
    phase_moe(p, xn2_d, gate_w, m_w1, m_w3, m_w2, s2_d, x1_d, consts, cring,
              x2_out)
    nc = p.finalize()
    return nc, p.counts


def build_fused_2pass():
    _, counts = build_fused()
    names = {nm for (_, nm) in counts}
    maxep = max((ep for (ep, _) in counts), default=0)
    bases = {}
    for nm in names:
        run = 0
        for ep in range(maxep + 2):
            bases[(ep, nm)] = run
            run += counts.get((ep, nm), 0)
    nc, _ = build_fused(bases)
    return nc


def kernel_fused(**inputs):
    CORES = list(range(NCORE))
    consts = host_constants()
    x = np.ascontiguousarray(inputs["L0_attn_x"].reshape(S, D))
    nc = _get("fused", build_fused_2pass)
    im = []
    for c in CORES:
        d = {
            "x_full": x,
            "x_own": x[c * TOK:(c + 1) * TOK],
            "w_attn": np.ascontiguousarray(inputs["L0_attn_W_attn"]),
            "w_out": np.ascontiguousarray(inputs["L0_attn_W_out"]),
            "b_out": np.ascontiguousarray(
                inputs["L0_attn_b_out"].reshape(1, D)),
            "f_w1": np.ascontiguousarray(inputs["L0_ffn_w1"]),
            "f_b1": np.ascontiguousarray(inputs["L0_ffn_b1"].reshape(1, 8192)),
            "f_w2": np.ascontiguousarray(inputs["L0_ffn_w2"]),
            "f_b2": np.ascontiguousarray(inputs["L0_ffn_b2"].reshape(1, D)),
            "wq": np.ascontiguousarray(inputs["L1_attn_Wq"]),
            "g_state": np.ascontiguousarray(
                inputs["L1_attn_g_state"].reshape(1, D)),
            "b_state": np.ascontiguousarray(
                inputs["L1_attn_b_state"].reshape(1, D)),
            "mem": np.ascontiguousarray(inputs["L1_attn_memory"]),
            "g_mem": np.ascontiguousarray(
                inputs["L1_attn_g_mem"].reshape(1, 1024)),
            "b_mem": np.ascontiguousarray(
                inputs["L1_attn_b_mem"].reshape(1, 1024)),
            "wk": np.ascontiguousarray(inputs["L1_attn_Wk"]),
            "wv": np.ascontiguousarray(inputs["L1_attn_Wv"]),
            "wo": np.ascontiguousarray(inputs["L1_attn_Wo"]),
            "log_beta": np.ascontiguousarray(
                inputs["L1_attn_log_beta"].reshape(1, 8)),
            "gate_w": np.ascontiguousarray(inputs["L1_ffn_gate_w"]),
            "m_w1": np.ascontiguousarray(inputs["L1_ffn_w1"]),
            "m_w3": np.ascontiguousarray(inputs["L1_ffn_w3"]),
            "m_w2": np.ascontiguousarray(inputs["L1_ffn_w2"]),
            "ones_col_f32_ab": consts["ones_col_f32"],
            "cosk_full": consts["cosk"],
            "sink_full": consts["sink"],
        }
        for t in ("cosq", "sinq", "cosk", "sink"):
            d[t] = np.ascontiguousarray(
                consts[t][:, c * TOK:(c + 1) * TOK])
        for w in ("permsign", "ident", "ident_f32", "ones_col_bf",
                  "ones_row_f32", "ones_col_f32"):
            d[w] = consts[w]
        im.append(d)
    res = run_bass_kernel_spmd(nc, im, core_ids=CORES).results
    x2 = np.concatenate([r["x2"] for r in res], 0)
    return x2.reshape(1, S, D)



_BUILT = {}


def _get(name, fn):
    if name not in _BUILT:
        _BUILT[name] = fn()
    return _BUILT[name]


def kernel(**inputs):
    return kernel_fused(**inputs)


def kernel_unfused(**inputs):
    CORES = list(range(NCORE))
    consts = host_constants()

    x = np.ascontiguousarray(inputs["L0_attn_x"].reshape(S, D))
    w_attn = np.ascontiguousarray(inputs["L0_attn_W_attn"])
    w_out = np.ascontiguousarray(inputs["L0_attn_W_out"])
    b_out = np.ascontiguousarray(inputs["L0_attn_b_out"].reshape(1, D))
    f_w1 = np.ascontiguousarray(inputs["L0_ffn_w1"])
    f_b1 = np.ascontiguousarray(inputs["L0_ffn_b1"].reshape(1, 8192))
    f_w2 = np.ascontiguousarray(inputs["L0_ffn_w2"])
    f_b2 = np.ascontiguousarray(inputs["L0_ffn_b2"].reshape(1, D))
    mem = np.ascontiguousarray(inputs["L1_attn_memory"])
    wq = np.ascontiguousarray(inputs["L1_attn_Wq"])
    wk = np.ascontiguousarray(inputs["L1_attn_Wk"])
    wv = np.ascontiguousarray(inputs["L1_attn_Wv"])
    wo = np.ascontiguousarray(inputs["L1_attn_Wo"])
    log_beta = np.ascontiguousarray(inputs["L1_attn_log_beta"].reshape(1, 8))
    g_state = np.ascontiguousarray(inputs["L1_attn_g_state"].reshape(1, D))
    b_state = np.ascontiguousarray(inputs["L1_attn_b_state"].reshape(1, D))
    g_mem = np.ascontiguousarray(inputs["L1_attn_g_mem"].reshape(1, 1024))
    b_mem = np.ascontiguousarray(inputs["L1_attn_b_mem"].reshape(1, 1024))
    gate_w = np.ascontiguousarray(inputs["L1_ffn_gate_w"])
    m_w1 = np.ascontiguousarray(inputs["L1_ffn_w1"])
    m_w3 = np.ascontiguousarray(inputs["L1_ffn_w3"])
    m_w2 = np.ascontiguousarray(inputs["L1_ffn_w2"])

    def sl(c):
        return slice(c * TOK, (c + 1) * TOK)

    nc = _get("pre", build_pre)
    im = []
    for c in CORES:
        im.append({"x": x[sl(c)], "w_attn": w_attn,
                   "permsign": consts["permsign"],
                   **{t: np.ascontiguousarray(consts[t][:, sl(c)])
                      for t in ("cosq", "sinq", "cosk", "sink")}})
    res = run_bass_kernel_spmd(nc, im, core_ids=CORES).results
    xn_all = [r["xn"] for r in res]
    q_all = [r["q"] for r in res]
    k_full = np.ascontiguousarray(np.concatenate([r["k"] for r in res], 1))
    v_full = np.ascontiguousarray(np.concatenate([r["v"] for r in res], 0))

    nc = _get("attn", build_attn)
    im = [{"q": q_all[c], "k": k_full, "v": v_full,
           "ones_col_bf": consts["ones_col_bf"],
           "ones_row_f32": consts["ones_row_f32"]} for c in CORES]
    res = run_bass_kernel_spmd(nc, im, core_ids=CORES).results
    ctx_all = [r["ctx"] for r in res]

    nc = _get("wout", build_wout)
    im = [{"ctx": ctx_all[c], "w_out": w_out, "b_out": b_out,
           "ones_row_f32": consts["ones_row_f32"]} for c in CORES]
    res = run_bass_kernel_spmd(nc, im, core_ids=CORES).results
    s_all = [r["s"] for r in res]

    nc = _get("absum", build_absum)
    im = [{"w1s": f_w1[c * 1024:(c + 1) * 1024],
           "w2s": f_w2[c * 256:(c + 1) * 256],
           "ones_col_f32_ab": consts["ones_col_f32"]} for c in CORES]
    res = run_bass_kernel_spmd(nc, im, core_ids=CORES).results
    absums = np.ascontiguousarray(
        np.concatenate([r["absum"] for r in res], 0))

    nc = _get("ffn", build_ffn)
    im = [{"xn": xn_all[c], "w1": f_w1, "b1": f_b1, "w2": f_w2, "b2": f_b2,
           "absums": absums, "s": s_all[c], "x": x[sl(c)],
           "ident": consts["ident"], "ident_f32": consts["ident_f32"],
           "ones_row_f32": consts["ones_row_f32"],
           "ones_col_f32": consts["ones_col_f32"]} for c in CORES]
    res = run_bass_kernel_spmd(nc, im, core_ids=CORES).results
    x1_all = [r["x1"] for r in res]

    nc = _get("xpre2", build_xpre2)
    im = [{"x1": x1_all[c], "g_state": g_state, "b_state": b_state,
           "wq": wq, "ones_col_f32": consts["ones_col_f32"],
           "ones_row_f32": consts["ones_row_f32"]} for c in CORES]
    res = run_bass_kernel_spmd(nc, im, core_ids=CORES).results
    xn2_all = [r["xn2"] for r in res]
    xi0_all = [r["xi0"] for r in res]

    nc = _get("memkv", build_memkv)
    NR = 2048 // NCORE
    im = [{"mem": mem[c * NR:(c + 1) * NR], "g_mem": g_mem, "b_mem": b_mem,
           "wk": wk, "wv": wv, "log_beta": log_beta,
           "ones_col_f32": consts["ones_col_f32"],
           "ones_row_f32": consts["ones_row_f32"]} for c in CORES]
    res = run_bass_kernel_spmd(nc, im, core_ids=CORES).results
    kb_full = np.ascontiguousarray(np.concatenate([r["kb"] for r in res], 1))
    v2_full = np.ascontiguousarray(np.concatenate([r["v"] for r in res], 0))

    nc = _get("entmax", build_entmax)
    im = [{"xi0": xi0_all[c], "kb": kb_full, "v": v2_full,
           "ident": consts["ident"]} for c in CORES]
    res = run_bass_kernel_spmd(nc, im, core_ids=CORES).results
    xi3_all = [r["xi3"] for r in res]

    nc = _get("womoe", build_womoe)
    im = [{"xi3": xi3_all[c], "wo": wo, "xn2": xn2_all[c],
           "gate_w": gate_w, "w1": m_w1, "w3": m_w3, "w2": m_w2,
           "x1": x1_all[c]} for c in CORES]
    res = run_bass_kernel_spmd(nc, im, core_ids=CORES).results
    x2 = np.concatenate([r["x2"] for r in res], 0)
    return x2.reshape(1, S, D)


# version 07
# speedup vs baseline: 1.1336x; 1.0338x over previous
import contextlib
import numpy as np
import ml_dtypes

import concourse.bass as bass
import concourse.mybir as mybir
from concourse.bass_utils import run_bass_kernel_spmd

F32 = mybir.dt.float32
BF16 = mybir.dt.bfloat16
AF = mybir.ActivationFunctionType
ALU = mybir.AluOpType
AX = mybir.AxisListType
BF = ml_dtypes.bfloat16

NCORE = 8
S = 8192
D = 2048
TOK = S // NCORE
LN_EPS = 1e-5



class InstProxy:

    def __init__(self, inst, prog, eng_name):
        self._i = inst
        self._p = prog
        self._e = eng_name

    def then_inc(self, sem, n=1):
        ep = self._p._eng_epoch[self._e]
        self._p.counts[(ep, sem.name)] = \
            self._p.counts.get((ep, sem.name), 0) + n
        self._i.then_inc(sem, n)
        return self

    def __getattr__(self, k):
        return getattr(self._i, k)


class Eng:

    def __init__(self, prog, name, handle):
        self.p = prog
        self.name = name
        self.e = handle
        self._sem = prog.sems["self_" + name]

    def wait_ge(self, sem, v):
        base = 0
        if sem.name.startswith("pool"):
            base = self.p.bases.get(
                (self.p._eng_epoch[self.name], sem.name), 0)
        return self.e.wait_ge(sem, base + v)

    def __getattr__(self, k):
        a = getattr(self.e, k)
        if not callable(a):
            return a
        p, nm = self.p, self.name

        def wrapped(*args, **kw):
            r = a(*args, **kw)
            if type(r).__name__ == "BassInstruction":
                return InstProxy(r, p, nm)
            return r
        return wrapped

    def chain(self, inst):
        self.p.selfcnt[self.name] += 1
        c = self.p.selfcnt[self.name]
        inst.then_inc(self._sem, 1)
        self.e.wait_ge(self._sem, c)
        return inst

    def chain_inc(self, inst, sem, n=1):
        self.chain(inst)
        self.nop().then_inc(sem, n)
        return inst


class Ring:

    def __init__(self, p, n=4, per_slot=None):
        self.sems = [p.sem() for _ in range(n)]
        self.R = n
        self.per_slot = per_slot
        self.npush = 0
        self.nslotpush = {}

    def push(self, inst):
        i = self.npush % self.R
        self.npush += 1
        inst.then_inc(self.sems[i], 16)
        return inst

    def wait(self, e, j):
        i = j % self.R
        e.wait_ge(self.sems[i], 16 * (j // self.R + 1))

    def wait_group(self, e, hi=None):
        if hi is None:
            hi = self.npush - 1
        n = hi + 1
        for i in range(self.R):
            cnt_i = (n - i + self.R - 1) // self.R
            if cnt_i > 0:
                e.wait_ge(self.sems[i], 16 * cnt_i)

    def push_slot(self, inst, slot):
        assert self.per_slot is not None
        i = slot % self.R
        self.nslotpush[slot] = self.nslotpush.get(slot, 0) + 1
        assert self.nslotpush[slot] <= self.per_slot
        inst.then_inc(self.sems[i], 16)
        return inst

    def wait_slot(self, e, slot):
        i = slot % self.R
        e.wait_ge(self.sems[i], 16 * self.per_slot * (slot // self.R + 1))


class Prog:
    ENG = ("sync", "gpsimd", "scalar", "vector", "tensor")

    def __init__(self, name="prog", bases=None):
        self.nc = bass.Bass("TRN2", target_bir_lowering=False, debug=False)
        self.ctx = contextlib.ExitStack()
        self.work = {e: [] for e in self.ENG}
        self.sems = {}
        self.selfcnt = {e: 0 for e in self.ENG}
        self.nbar = 0
        self._uniq = 0
        self._astack = []
        self._pool = []
        self._cursor = 0
        self.counts = {}
        self.bases = bases if bases is not None else {}
        self._eng_epoch = {e: 0 for e in self.ENG}
        for e in self.ENG:
            self.sem("self_" + e)

    def uniq(self, prefix):
        self._uniq += 1
        return f"{prefix}_{self._uniq}"

    def sem(self, name=None, persistent=False):
        if name is None and not persistent:
            if self._cursor < len(self._pool):
                s = self._pool[self._cursor]
            else:
                s = self.ctx.enter_context(
                    self.nc.semaphore(self.uniq("pool")))
                self._pool.append(s)
            self._cursor += 1
            return s
        if name is None:
            name = self.uniq("sem")
        if name not in self.sems:
            self.sems[name] = self.ctx.enter_context(self.nc.semaphore(name))
        return self.sems[name]

    def ring(self, n=4, per_slot=None):
        return Ring(self, n, per_slot)

    def on(self, eng):
        def deco(f):
            self.work[eng].append(f)
            return f
        return deco

    def sb(self, name, shape, dt, freeable=False):
        cm = self.nc.sbuf_tensor(self.uniq(name), shape, dt)
        h = cm.__enter__()
        self._astack.append(cm)
        return h

    def ps(self, name, shape, dt, freeable=False):
        cm = self.nc.psum_tensor(self.uniq(name), shape, dt)
        h = cm.__enter__()
        self._astack.append(cm)
        return h

    def scope(self):
        return len(self._astack)

    def pop_scope(self, mark):
        while len(self._astack) > mark:
            self._astack.pop().__exit__(None, None, None)

    def din(self, name, shape, dt):
        return self.nc.dram_tensor(name, list(shape), dt, kind="ExternalInput").ap()

    def dout(self, name, shape, dt):
        return self.nc.dram_tensor(name, list(shape), dt, kind="ExternalOutput").ap()

    def dtmp(self, name, shape, dt):
        return self.nc.dram_tensor(self.uniq(name), list(shape), dt,
                                   kind="Internal").ap()

    def barrier(self, reset=False):
        self.nbar += 1
        base = 5 * (self.nbar - 1)
        bar = self.sem("bar", persistent=True)
        for e in self.ENG:
            @self.on(e)
            def _(eng, base=base, bar=bar, e_name=e, reset=reset):
                eng.nop().then_inc(bar, 1)
                eng.wait_ge(bar, base + 5)
                if reset:
                    self._eng_epoch[e_name] += 1
        if reset:
            self._cursor = 0

    def finalize(self):
        nc = self.nc
        with nc.Block() as block:
            for name in self.ENG:
                def runner(e, name=name):
                    w = Eng(self, name, e)
                    for f in self.work[name]:
                        f(w)
                getattr(block, name)(runner)
        self.pop_scope(0)
        self.ctx.close()
        return nc



def host_constants():
    dk = 128
    inv_freq = 1.0 / (10000.0 ** (np.arange(0, dk, 2, dtype=np.float64) / dk))
    pos = np.arange(S, dtype=np.float64)
    fr = pos[:, None] * inv_freq[None, :]
    ang = np.concatenate([fr, fr], axis=1)
    cos = np.cos(ang).T.astype(np.float32)
    sin = np.sin(ang).T.astype(np.float32)
    alpha = np.float32(1.0 / np.sqrt(128.0))
    perm = np.zeros((128, 128), dtype=np.float32)
    for m in range(64):
        perm[m + 64, m] = -1.0
    for m in range(64, 128):
        perm[m - 64, m] = 1.0
    return {
        "cosq": np.ascontiguousarray(cos * alpha),
        "sinq": np.ascontiguousarray(sin * alpha),
        "cosk": np.ascontiguousarray(cos),
        "sink": np.ascontiguousarray(sin),
        "permsign": perm,
        "ident": np.eye(128, dtype=BF),
        "ident_f32": np.eye(128, dtype=np.float32),
        "ones_row_bf": np.ones((1, 128), dtype=BF),
        "ones_col_bf": np.ones((128, 1), dtype=BF),
        "ones_row_f32": np.ones((1, 128), dtype=np.float32),
        "ones_col_f32": np.ones((128, 1), dtype=np.float32),
    }


CONST_SHAPES = {
    "ident": ([128, 128], BF16), "ident_f32": ([128, 128], F32),
    "ones_row_bf": ([1, 128], BF16),
    "ones_col_bf": ([128, 1], BF16), "ones_row_f32": ([1, 128], F32),
    "ones_col_f32": ([128, 1], F32), "permsign": ([128, 128], F32),
}


def load_consts(p, which):
    ring = Ring(p, 1)
    ring.sems = [p.sem(persistent=True)]
    tiles = {}
    dins = {}
    for w in which:
        shp, dt = CONST_SHAPES[w]
        dins[w] = p.din(w, shp, dt)
        tiles[w] = p.sb("c_" + w, shp, dt)

    @p.on("sync")
    def _(e):
        for w in which:
            ring.push(e.dma_start(tiles[w][:], dins[w]))
    return tiles, ring



def phase_ln(p, x_in, xn_out, T=TOK):
    nt = T // 128
    rld = p.ring(4)
    rst = p.ring(4)
    s2d = p.sem()
    vep = p.sem()
    rsd = p.sem()
    nbd = p.sem()
    nrm = p.sem()

    xt = [p.sb("ln_x", [128, D], F32) for _ in range(2)]
    xnt = [p.sb("ln_o", [128, D], BF16) for _ in range(2)]
    scr_v = p.sb("ln_scrv", [128, D], F32)
    scr_s = p.sb("ln_scrs", [128, D], F32)
    s1 = [p.sb("ln_s1", [128, 1], F32) for _ in range(2)]
    s2 = [p.sb("ln_s2", [128, 1], F32) for _ in range(2)]
    mean = p.sb("ln_mean", [128, 1], F32)
    ex2 = p.sb("ln_ex2", [128, 1], F32)
    varep = p.sb("ln_var", [128, 1], F32)
    rstd = p.sb("ln_rstd", [128, 1], F32)
    nbias = p.sb("ln_nb", [128, 1], F32)

    @p.on("sync")
    def _(e):
        for i in range(nt):
            if i >= 2:
                e.wait_ge(nrm, i - 1)
            rld.push(e.dma_start(xt[i % 2][:], x_in[i * 128:(i + 1) * 128, :]))

    @p.on("vector")
    def _(e):
        for i in range(nt):
            b = i % 2
            rld.wait(e, i)
            e.chain(e.tensor_scalar(scr_v[:], xt[b][:], 0.0, None,
                                    ALU.add, ALU.add, accum_out=s1[b][:]))
            e.wait_ge(s2d, i + 1)
            e.chain(e.tensor_scalar(mean[:], s1[b][:], 1.0 / D, None,
                                    ALU.mult, ALU.bypass))
            e.chain(e.tensor_scalar(ex2[:], s2[b][:], 1.0 / D, None,
                                    ALU.mult, ALU.bypass))
            e.chain(e.scalar_tensor_tensor(varep[:], mean[:], -1.0, mean[:],
                                           ALU.mult, ALU.mult))
            e.chain(e.tensor_add(varep[:], varep[:], ex2[:]))
            e.chain_inc(e.tensor_scalar(varep[:], varep[:], LN_EPS, None,
                                        ALU.add, ALU.bypass), vep)
            e.wait_ge(rsd, i + 1)
            e.chain_inc(e.scalar_tensor_tensor(nbias[:], rstd[:], -1.0,
                                               mean[:], ALU.mult, ALU.mult), nbd)

    @p.on("scalar")
    def _(e):
        for i in range(nt):
            b = i % 2
            rld.wait(e, i)
            e.activation(scr_s[:], xt[b][:], AF.Square,
                         accum_out=s2[b][:]).then_inc(s2d, 1)
            e.wait_ge(vep, i + 1)
            e.chain(e.activation(rstd[:], varep[:], AF.Ln))
            e.chain_inc(e.activation(rstd[:], rstd[:], AF.Exp,
                                     scale=-0.5), rsd)
            e.wait_ge(nbd, i + 1)
            if i >= 2:
                rst.wait(e, i - 2)
            e.activation(xnt[b][:], xt[b][:], AF.Identity, bias=nbias[:],
                         scale=rstd[:]).then_inc(nrm, 1)

    @p.on("gpsimd")
    def _(e):
        for i in range(nt):
            e.wait_ge(nrm, i + 1)
            rst.push(e.dma_start(xn_out[i * 128:(i + 1) * 128, :],
                                 xnt[i % 2][:]))
        rst.wait_group(e)



def phase_qkv(p, xn_own, w_attn, tabs, consts, cring,
              q_out, k_out, v_out, qonly=False):
    NF = D // 128
    QKC = 16 if qonly else 20
    TH = TOK // 512
    perm = consts["permsign"]

    rxn = p.ring(1)
    rtab = p.ring(1)
    rw = p.ring(4)
    rwv = p.ring(1)
    rstq = p.ring(4)
    rstv = p.ring(4)
    ped = p.sem()
    rotd = p.sem()
    qrd = p.sem()
    rpd = p.sem()
    pvd = p.sem()
    vevd = p.sem()

    xn_dm = p.sb("qkv_xndm", [128, NF * TOK], BF16)
    wtile = [p.sb("qkv_w", [128, NF * 128], BF16) for _ in range(2)]
    wv_sb = p.sb("qkv_wv", [128, NF * 2048], BF16)
    cq = p.sb("qkv_cq", [128, TOK], F32)
    sq = p.sb("qkv_sq", [128, TOK], F32)
    ck = p.sb("qkv_ck", [128, TOK], F32)
    sk = p.sb("qkv_sk", [128, TOK], F32)
    qraw = [p.sb("qkv_qraw", [128, 512], F32) for _ in range(2)]
    t1 = p.sb("qkv_t1", [128, 512], F32)
    t2 = p.sb("qkv_t2", [128, 512], F32)
    qo = [p.sb("qkv_qo", [128, 512], BF16) for _ in range(2)]
    vo = [p.sb("qkv_vo", [128, 512], BF16) for _ in range(2)]
    pqk = [p.ps("qkv_pqk", [128, 512], F32) for _ in range(2)]
    prot = [p.ps("qkv_prot", [128, 512], F32) for _ in range(2)]
    pv = [p.ps("qkv_pv", [128, 512], F32) for _ in range(2)]

    @p.on("sync")
    def _(e):
        for c in range(NF):
            rxn.push(e.dma_start(xn_dm[:, c * TOK:(c + 1) * TOK],
                                 xn_own[:, c * 128:(c + 1) * 128],
                                 transpose=True))
        for t, dst in (("cosq", cq), ("sinq", sq), ("cosk", ck), ("sink", sk)):
            rtab.push(e.dma_start(dst[:], tabs[t]))
        for c in range(QKC):
            for h in range(TH):
                n = 2 * c + h
                e.wait_ge(rpd, n + 1)
                dst = (q_out[c * 128:(c + 1) * 128, h * 512:(h + 1) * 512]
                       if c < 16 else
                       k_out[(c - 16) * 128:(c - 15) * 128,
                             h * 512:(h + 1) * 512])
                rstq.push(e.dma_start(dst, qo[n % 2][:]))
        if not qonly:
            for tc in range(TOK // 128):
                for vc in range(4):
                    n = 4 * tc + vc
                    e.wait_ge(vevd, n + 1)
                    rstv.push(e.dma_start(
                        v_out[tc * 128:(tc + 1) * 128,
                              vc * 512:(vc + 1) * 512],
                        vo[n % 2][:]))
        rstq.wait_group(e)
        rstv.wait_group(e)

    @p.on("gpsimd")
    def _(e):
        for c in range(QKC):
            if c >= 2:
                e.wait_ge(ped, 2 * (c - 1))
            rw.push(e.dma_start(
                wtile[c % 2][:].rearrange("p (a c) -> p a c", a=NF),
                w_attn[:, c * 128:(c + 1) * 128]
                .rearrange("(a p) c -> p a c", p=128)))
        if not qonly:
            wv_view = wv_sb[:].rearrange("p (a c) -> p a c", a=NF)
            for vc in range(4):
                rwv.push(e.dma_start(
                    wv_view[:, :, vc * 512:(vc + 1) * 512],
                    w_attn[:, 2560 + vc * 512:2560 + (vc + 1) * 512]
                    .rearrange("(a p) c -> p a c", p=128)))

    @p.on("tensor")
    def _(e):
        rxn.wait_group(e)
        cring.wait_group(e)
        for c in range(QKC):
            rw.wait(e, c)
            for h in range(TH):
                n = 2 * c + h
                if n >= 2:
                    e.wait_ge(qrd, n - 1)
                for kc in range(NF):
                    inst = e.matmul(
                        pqk[n % 2][:],
                        wtile[c % 2][:, kc * 128:(kc + 1) * 128],
                        xn_dm[:, kc * TOK + h * 512:kc * TOK + h * 512 + 512],
                        start=(kc == 0), stop=(kc == NF - 1))
                    if kc == NF - 1:
                        inst.then_inc(ped, 1)
                e.wait_ge(qrd, n + 1)
                if n >= 2:
                    e.wait_ge(rpd, n - 1)
                e.matmul(prot[n % 2][:], perm[:], qraw[n % 2][:],
                         start=True, stop=True).then_inc(rotd, 1)
        if not qonly:
            rwv.wait_group(e)
            for tc in range(TOK // 128):
                for vc in range(4):
                    n = 4 * tc + vc
                    if n >= 2:
                        e.wait_ge(vevd, n - 1)
                    for kc in range(NF):
                        inst = e.matmul(
                            pv[n % 2][:],
                            xn_dm[:, kc * TOK + tc * 128:kc * TOK + (tc + 1) * 128],
                            wv_sb[:, kc * 2048 + vc * 512:kc * 2048 + (vc + 1) * 512],
                            start=(kc == 0), stop=(kc == NF - 1))
                        if kc == NF - 1:
                            inst.then_inc(pvd, 1)

    @p.on("scalar")
    def _(e):
        for c in range(QKC):
            for h in range(TH):
                n = 2 * c + h
                e.wait_ge(ped, n + 1)
                if n >= 2:
                    e.wait_ge(rotd, n - 1)
                    e.wait_ge(rpd, n - 1)
                e.activation(qraw[n % 2][:], pqk[n % 2][:], AF.Identity
                             ).then_inc(qrd, 1)
        if not qonly:
            for tc in range(TOK // 128):
                for vc in range(4):
                    n = 4 * tc + vc
                    e.wait_ge(pvd, n + 1)
                    if n >= 2:
                        rstv.wait(e, n - 2)
                    e.activation(vo[n % 2][:], pv[n % 2][:], AF.Identity
                                 ).then_inc(vevd, 1)

    @p.on("vector")
    def _(e):
        rtab.wait_group(e)
        for c in range(QKC):
            ctab = cq if c < 16 else ck
            stab = sq if c < 16 else sk
            for h in range(TH):
                n = 2 * c + h
                e.wait_ge(qrd, n + 1)
                e.chain(e.tensor_mul(t1[:], qraw[n % 2][:],
                                     ctab[:, h * 512:(h + 1) * 512]))
                e.wait_ge(rotd, n + 1)
                e.chain(e.tensor_mul(t2[:], prot[n % 2][:],
                                     stab[:, h * 512:(h + 1) * 512]))
                if n >= 2:
                    rstq.wait(e, n - 2)
                e.chain_inc(e.tensor_add(qo[n % 2][:], t1[:], t2[:]), rpd)



def phase_attn(p, q_in, k_in, v_in, consts, cring, ctx_out):
    NKC = S // 128
    QT = TOK // 512
    NIT = 4 * QT * 4
    ones_col = consts["ones_col_bf"]
    ones_row = consts["ones_row_f32"]

    rq = p.ring(1)
    rk = p.ring(1)
    rv = p.ring(4)
    rst = p.ring(8)
    pq2 = p.sem()
    nqd = p.sem()
    nqe = p.sem()
    vcr = p.sem()
    scr = p.sem()
    bcm = p.sem()
    ngc = p.sem()
    scd = p.sem()
    exd = p.sem()
    pcd = p.sem()
    rsd = p.sem()
    mbc = p.sem()
    bcsd = p.sem()
    ctxe = p.sem()

    q_sb = p.sb("at_q", [128, 16 * TOK], BF16)
    k_sb = p.sb("at_k", [128, 4 * S], BF16)
    v_sb = p.sb("at_v", [128, NKC * 512], BF16)
    q2 = [p.sb("at_q2", [128, 512], BF16) for _ in range(2)]
    nrow = p.sb("at_nrow", [1, 32], F32)
    crow = p.sb("at_crow", [1, 16], F32)
    negC = p.sb("at_negC", [128, 16], F32)
    t1 = p.sb("at_t1", [1, 1], F32)
    pr = [p.sb("at_pr", [128, 512], BF16) for _ in range(2)]
    rsum = p.sb("at_rsum", [1, 512], F32)
    bc_sb = p.sb("at_bc", [128, 512], F32)
    ctx_sb = [p.sb("at_ctx", [128, 4 * 512], BF16) for _ in range(2)]
    ps_sc = [p.ps("at_psc", [128, 512], F32) for _ in range(2)]
    pc = [p.ps("at_pc", [128, 512], F32) for _ in range(4)]
    ps_sum = p.ps("at_psum", [1, 512], F32)
    ps_bc = p.ps("at_pbc", [128, 512], F32)
    ps_nr = [ps_sum, ps_bc]

    NQ = 16 * TOK // 512
    NK = 4 * S // 512

    @p.on("sync")
    def _(e):
        for h in range(16):
            rq.push(e.dma_start(q_sb[:, h * TOK:(h + 1) * TOK],
                                q_in[h * 128:(h + 1) * 128, :]))
        for kv in range(4):
            rk.push(e.dma_start(k_sb[:, kv * S:(kv + 1) * S],
                                k_in[kv * 128:(kv + 1) * 128, :]))
        for kv in range(4):
            if kv >= 1:
                e.wait_ge(ctxe, kv * 8)
            vv = v_in[:, kv * 512:(kv + 1) * 512].rearrange(
                "(a p) c -> p a c", p=128)
            for g8 in range(4):
                rv.push(e.dma_start(
                    v_sb[:].rearrange("p (a c) -> p a c", a=NKC)
                    [:, g8 * 16:(g8 + 1) * 16, :],
                    vv[:, g8 * 16:(g8 + 1) * 16, :]))

    @p.on("gpsimd")
    def _(e):
        for it in range(NIT):
            kv, r = divmod(it, QT * 4)
            qt, qh4 = divmod(r, 4)
            h = kv * 4 + qh4
            e.wait_ge(ctxe, it + 1)
            for dvs in range(4):
                rst.push(e.dma_start(
                    ctx_out[h * 512 + dvs * 128:h * 512 + (dvs + 1) * 128,
                            qt * 512:(qt + 1) * 512],
                    ctx_sb[it % 2][:, dvs * 512:(dvs + 1) * 512]))
        rst.wait_group(e)

    @p.on("scalar")
    def _(e):
        rq.wait_group(e)
        for it in range(NQ + NK):
            if it == NQ:
                rk.wait_group(e)
            if it >= 2:
                e.wait_ge(nqd, it - 1)
            src_ap = (q_sb[:, it * 512:(it + 1) * 512] if it < NQ
                      else k_sb[:, (it - NQ) * 512:(it - NQ + 1) * 512])
            e.activation(q2[it % 2][:], src_ap, AF.Square).then_inc(pq2, 1)
        e.wait_ge(vcr, 1)
        e.chain(e.activation(crow[:], crow[:], AF.Ln))
        e.chain_inc(e.activation(crow[:], crow[:], AF.Exp, scale=0.5), scr)
        e.wait_ge(bcm, 1)
        e.activation(negC[:], ps_bc[:, 0:16], AF.Identity,
                     scale=-1.0).then_inc(ngc, 1)

    @p.on("tensor")
    def _(e):
        cring.wait_group(e)
        for it in range(NQ + NK):
            e.wait_ge(pq2, it + 1)
            if it >= 2:
                e.wait_ge(nqe, it - 1)
            e.matmul(ps_nr[it % 2][0:1, :], ones_col[:], q2[it % 2][:],
                     start=True, stop=True).then_inc(nqd, 1)
        e.wait_ge(scr, 1)
        e.matmul(ps_bc[:, 0:16], ones_row[:], crow[:],
                 start=True, stop=True).then_inc(bcm, 1)

    @p.on("vector")
    def _(e):
        for it in range(NQ + NK):
            e.wait_ge(nqd, it + 1)
            e.chain(e.reduce_max(t1[:], ps_nr[it % 2][0:1, :], axis=AX.X))
            if it < NQ:
                col, first = it // 2, (it % 2 == 0)
            else:
                col = 16 + (it - NQ) // 16
                first = ((it - NQ) % 16 == 0)
            if first:
                e.chain(e.tensor_copy(nrow[:, col:col + 1], t1[:]))
            else:
                e.chain(e.tensor_max(nrow[:, col:col + 1],
                                     nrow[:, col:col + 1], t1[:]))
            e.nop().then_inc(nqe, 1)
        for h in range(16):
            e.chain(e.tensor_mul(crow[:, h:h + 1], nrow[:, h:h + 1],
                                 nrow[:, 16 + h // 4:17 + h // 4]))
        e.nop().then_inc(vcr, 1)

    @p.on("tensor")
    def _(e):
        for kv in range(4):
            rv.wait_group(e, 4 * (kv + 1) - 1)
            for qt in range(QT):
                for qh4 in range(4):
                    it = (kv * QT + qt) * 4 + qh4
                    h = kv * 4 + qh4
                    qs = q_sb[:, h * TOK + qt * 512:h * TOK + (qt + 1) * 512]
                    if it >= 1:
                        e.wait_ge(ctxe, it)
                        e.wait_ge(bcsd, it)
                    for kc in range(NKC):
                        g = it * NKC + kc
                        if g >= 2:
                            e.wait_ge(exd, g - 1)
                        e.matmul(ps_sc[g % 2][:],
                                 k_sb[:, kv * S + kc * 128:kv * S + (kc + 1) * 128],
                                 qs, start=True, stop=True).then_inc(scd, 1)
                        if kc >= 1:
                            e.wait_ge(exd, g)
                            first = (kc - 1 == 0)
                            for dvs in range(4):
                                e.matmul(pc[dvs][:],
                                         v_sb[:, (kc - 1) * 512 + dvs * 128:
                                              (kc - 1) * 512 + (dvs + 1) * 128],
                                         pr[(g - 1) % 2][:],
                                         start=first, stop=False,
                                         skip_group_check=True)
                            e.matmul(ps_sum[:], ones_col[:], pr[(g - 1) % 2][:],
                                     start=first, stop=False,
                                     skip_group_check=True).then_inc(pcd, 1)
                    g = it * NKC + NKC - 1
                    e.wait_ge(exd, g + 1)
                    for dvs in range(4):
                        e.matmul(pc[dvs][:],
                                 v_sb[:, (NKC - 1) * 512 + dvs * 128:
                                      (NKC - 1) * 512 + (dvs + 1) * 128],
                                 pr[g % 2][:],
                                 start=False, stop=True, skip_group_check=True)
                    e.matmul(ps_sum[:], ones_col[:], pr[g % 2][:],
                             start=False, stop=True, skip_group_check=True
                             ).then_inc(pcd, 1)
                    e.wait_ge(rsd, it + 1)
                    e.matmul(ps_bc[:], ones_row[:], rsum[:],
                             start=True, stop=True).then_inc(mbc, 1)

    @p.on("scalar")
    def _(e):
        e.wait_ge(ngc, 1)
        for it in range(NIT):
            kv = it // (QT * 4)
            h = kv * 4 + (it % 4)
            for kc in range(NKC):
                g = it * NKC + kc
                e.wait_ge(scd, g + 1)
                if g >= 2:
                    e.wait_ge(pcd, g - 1)
                e.activation(pr[g % 2][:], ps_sc[g % 2][:], AF.Exp,
                             bias=negC[:, h:h + 1]).then_inc(exd, 1)
            e.wait_ge(mbc, it + 1)
            e.activation(bc_sb[:], ps_bc[:], AF.Identity).then_inc(bcsd, 1)

    @p.on("vector")
    def _(e):
        for it in range(NIT):
            e.wait_ge(pcd, (it + 1) * NKC)
            e.chain(e.reciprocal(rsum[:], ps_sum[:]))
            e.nop().then_inc(rsd, 1)
            e.wait_ge(bcsd, it + 1)
            if it >= 2:
                for dvs in range(4):
                    rst.wait(e, 4 * (it - 2) + dvs)
            for dvs in range(4):
                inst = e.tensor_mul(
                    ctx_sb[it % 2][:, dvs * 512:(dvs + 1) * 512],
                    pc[dvs][:], bc_sb[:])
                if dvs == 3:
                    e.chain_inc(inst, ctxe)
                else:
                    e.chain(inst)



def phase_wout(p, ctx_in, w_out, b_out, consts, cring, s_out):
    NCF = 64
    NTT = TOK // 128
    GRP = 8
    NG = NCF // GRP
    ones_row = consts["ones_row_f32"]

    rb = p.ring(1)
    rc = p.ring(2, per_slot=GRP)
    rw = p.ring(2, per_slot=GRP)
    rst = p.ring(4)
    ped = p.sem()
    eva = p.sem()
    bmm = p.sem()
    bev = p.sem()

    ctxg = [p.sb("wo_ctx", [128, GRP * TOK], BF16) for _ in range(2)]
    wg = [p.sb("wo_w", [128, GRP * 2048], BF16) for _ in range(2)]
    brow = p.sb("wo_brow", [1, 2048], F32)
    bbc = p.sb("wo_bbc", [128, 2048], F32)
    s_sb = p.sb("wo_s", [128, NTT * 2048], F32)
    ps_o = [p.ps("wo_po", [128, 2048], F32) for _ in range(2)]

    @p.on("sync")
    def _(e):
        rb.push(e.dma_start(brow[:], b_out))
        for go in range(NG):
            if go >= 2:
                e.wait_ge(ped, (go - 1) * NTT)
            for ci in range(GRP):
                cf = go * GRP + ci
                rc.push_slot(e.dma_start(
                    ctxg[go % 2][:, ci * TOK:(ci + 1) * TOK],
                    ctx_in[cf * 128:(cf + 1) * 128, :]), go)

    @p.on("gpsimd")
    def _(e):
        for go in range(NG):
            if go >= 2:
                e.wait_ge(ped, (go - 1) * NTT)
            for ci in range(GRP):
                cf = go * GRP + ci
                rw.push_slot(e.dma_start(
                    wg[go % 2][:, ci * 2048:(ci + 1) * 2048],
                    w_out[cf * 128:(cf + 1) * 128, :]), go)
        for tt in range(NTT):
            e.wait_ge(eva, 1 + NG * NTT)
            rst.push(e.dma_start(s_out[tt * 128:(tt + 1) * 128, :],
                                 s_sb[:, tt * 2048:(tt + 1) * 2048]))
        rst.wait_group(e)

    @p.on("tensor")
    def _(e):
        rb.wait_group(e)
        cring.wait_group(e)
        for i in range(4):
            e.matmul(ps_o[0][:, i * 512:(i + 1) * 512], ones_row[:],
                     brow[:, i * 512:(i + 1) * 512],
                     start=True, stop=True).then_inc(bmm, 1)
        e.wait_ge(bev, 4)
        for go in range(NG):
            rc.wait_slot(e, go)
            rw.wait_slot(e, go)
            for tt in range(NTT):
                it = go * NTT + tt
                if it >= 2:
                    e.wait_ge(eva, it)
                for ci in range(GRP):
                    for c4 in range(4):
                        inst = e.matmul(
                            ps_o[it % 2][:, c4 * 512:(c4 + 1) * 512],
                            ctxg[go % 2][:, ci * TOK + tt * 128:
                                          ci * TOK + (tt + 1) * 128],
                            wg[go % 2][:, ci * 2048 + c4 * 512:
                                       ci * 2048 + (c4 + 1) * 512],
                            start=(ci == 0), stop=(ci == GRP - 1),
                            skip_group_check=True)
                        if ci == GRP - 1 and c4 == 3:
                            inst.then_inc(ped, 1)

    @p.on("scalar")
    def _(e):
        for i in range(4):
            e.wait_ge(bmm, i + 1)
            e.activation(bbc[:, i * 512:(i + 1) * 512],
                         ps_o[0][:, i * 512:(i + 1) * 512],
                         AF.Identity).then_inc(bev, 1)

    @p.on("vector")
    def _(e):
        e.wait_ge(bev, 4)
        for tt in range(NTT):
            e.chain(e.tensor_copy(s_sb[:, tt * 2048:(tt + 1) * 2048], bbc[:]))
        e.nop().then_inc(eva, 1)
        for it in range(NG * NTT):
            tt = it % NTT
            e.wait_ge(ped, it + 1)
            for c4 in range(4):
                inst = e.tensor_add(
                    s_sb[:, tt * 2048 + c4 * 512:tt * 2048 + (c4 + 1) * 512],
                    s_sb[:, tt * 2048 + c4 * 512:tt * 2048 + (c4 + 1) * 512],
                    ps_o[it % 2][:, c4 * 512:(c4 + 1) * 512])
                if c4 == 3:
                    e.chain_inc(inst, eva)
                else:
                    e.chain(inst)



def phase_absum(p, w1s, w2s, out):
    ones_col = None
    r1 = p.ring(4)
    red = p.sem()
    fin = p.sem()
    rst = p.ring(1)

    wbuf = [p.sb("ab_w", [128, 8192], F32) for _ in range(2)]
    rcol = p.sb("ab_r", [128, 1], F32)
    acc = p.sb("ab_acc", [128, 2], F32)
    onec = p.sb("ab_one", [128, 1], F32)
    osb = p.sb("ab_o", [2, 1], F32)
    pso = p.ps("ab_ps", [2, 1], F32)
    ones_in = p.din("ones_col_f32_ab", [128, 1], F32)
    ro = p.ring(1)

    chunks = [(w1s[i * 128:(i + 1) * 128, :], 2048, 0) for i in range(8)] + \
             [(w2s[i * 128:(i + 1) * 128, :], 8192, 1) for i in range(2)]

    @p.on("sync")
    def _(e):
        ro.push(e.dma_start(onec[:], ones_in))
        for i, (src, w, col) in enumerate(chunks):
            if i >= 2:
                e.wait_ge(red, i - 1)
            r1.push(e.dma_start(wbuf[i % 2][:, 0:w], src))

    @p.on("vector")
    def _(e):
        e.chain(e.memset(acc[:], 0.0))
        for i, (src, w, col) in enumerate(chunks):
            r1.wait(e, i)
            e.chain(e.reduce_sum(rcol[:], wbuf[i % 2][:, 0:w], axis=AX.X,
                                 apply_absolute_value=True))
            e.chain_inc(e.tensor_add(acc[:, col:col + 1], acc[:, col:col + 1],
                                     rcol[:]), red)
        e.nop().then_inc(red, 1)

    @p.on("tensor")
    def _(e):
        e.wait_ge(red, len(chunks) + 1)
        ro.wait_group(e)
        e.matmul(pso[:], acc[:], onec[:], start=True, stop=True
                 ).then_inc(fin, 1)

    @p.on("scalar")
    def _(e):
        e.wait_ge(fin, 1)
        e.activation(osb[:], pso[:], AF.Identity).then_inc(fin, 1)

    @p.on("gpsimd")
    def _(e):
        e.wait_ge(fin, 2)
        rst.push(e.dma_start(out.rearrange("a b -> b a"), osb[:]))
        rst.wait_group(e)



def phase_ffn_thr(p, absums, consts, cring, thr_bc, nthr_bc):
    ident = consts["ident_f32"]
    ones_row = consts["ones_row_f32"]
    ones_cf = consts["ones_col_f32"]
    ra = p.ring(1)
    ms = p.sem()

    asb = p.sb("th_a", [8, 2], F32)
    tsb = p.sb("th_t", [2, 1], F32)
    trow = p.sb("th_r", [1, 2], F32)
    pst = p.ps("th_p", [128, 2], F32, freeable=True)

    @p.on("sync")
    def _(e):
        ra.push(e.dma_start(asb[:], absums))

    @p.on("tensor")
    def _(e):
        ra.wait_group(e)
        cring.wait_group(e)
        e.matmul(pst[0:2, 0:1], asb[:], ones_cf[0:8, 0:1],
                 start=True, stop=True).then_inc(ms, 1)
        e.wait_ge(ms, 2)
        e.matmul(pst[0:1, 0:2], tsb[:], ident[0:2, 0:2],
                 start=True, stop=True).then_inc(ms, 1)
        e.wait_ge(ms, 4)
        e.matmul(pst[:, 0:2], ones_row[:], trow[:],
                 start=True, stop=True).then_inc(ms, 1)

    @p.on("scalar")
    def _(e):
        e.wait_ge(ms, 1)
        e.activation(tsb[:], pst[0:2, 0:1], AF.Identity,
                     scale=0.5 / (8192.0 * 2048.0)).then_inc(ms, 1)
        e.wait_ge(ms, 3)
        e.activation(trow[:], pst[0:1, 0:2], AF.Identity).then_inc(ms, 1)
        e.wait_ge(ms, 5)
        e.chain(e.activation(thr_bc[:], pst[:, 0:2], AF.Identity))
        e.chain_inc(e.activation(nthr_bc[:], thr_bc[:], AF.Identity,
                                 scale=-1.0), ms)


def phase_ffn_h(p, xn_own, w1, b1, thr_bc, nthr_bc, consts, cring, h_out,
                thr_ready):
    NFC = 64
    NF = D // 128
    TH = TOK // 512
    ident = consts["ident"]
    ones_cf = consts["ones_col_f32"]

    rxn = p.ring(1)
    rw1 = p.ring(4)
    rb1 = p.ring(1)
    rsth = p.ring(4)
    qzd = p.sem()
    trd = p.sem()
    tre = p.sem()
    hmm = p.sem()
    hev = p.sem()
    b1m = p.sem()
    b1e = p.sem()

    xn_dm = p.sb("fh_xndm", [128, NF * TOK], BF16, freeable=True)
    w1buf = [p.sb("fh_w1", [128, 2048], F32) for _ in range(2)]
    tle = p.sb("fh_tle", [128, 2048], BF16)
    qnat = [p.sb("fh_qn", [128, 2048], BF16) for _ in range(2)]
    w1qT = [p.sb("fh_w1qT", [128, 2048], BF16) for _ in range(2)]
    b1row = p.sb("fh_b1r", [1, 8192], F32)
    b1c = p.sb("fh_b1c", [128, 64], F32)
    h_sb = [p.sb("fh_h", [128, TOK], BF16) for _ in range(2)]
    ps_t = p.ps("fh_pst", [128, 2048], F32, freeable=True)
    ps_h = [p.ps("fh_psh", [128, 512], F32, freeable=True) for _ in range(2)]
    ps_b1 = p.ps("fh_psb", [128, 64], F32, freeable=True)

    @p.on("sync")
    def _(e):
        rb1.push(e.dma_start(b1row[:], b1))
        for c in range(NF):
            rxn.push(e.dma_start(xn_dm[:, c * TOK:(c + 1) * TOK],
                                 xn_own[:, c * 128:(c + 1) * 128],
                                 transpose=True))
        for fc in range(NFC):
            if fc >= 2:
                e.wait_ge(qzd, fc - 1)
            rw1.push(e.dma_start(w1buf[fc % 2][:],
                                 w1[fc * 128:(fc + 1) * 128, :]))

    @p.on("vector")
    def _(e):
        if thr_ready is not None:
            e.wait_ge(thr_ready[0], thr_ready[1])
        for fc in range(NFC):
            rw1.wait(e, fc)
            if fc >= 2:
                e.wait_ge(trd, (fc - 1) * NF)
            e.chain(e.tensor_scalar(tle[:], w1buf[fc % 2][:],
                                    nthr_bc[:, 0:1], None, ALU.is_le,
                                    ALU.bypass))
            e.chain_inc(e.scalar_tensor_tensor(
                qnat[fc % 2][:], w1buf[fc % 2][:], thr_bc[:, 0:1], tle[:],
                ALU.is_ge, ALU.subtract), qzd)

    @p.on("tensor")
    def _(e):
        cring.wait_group(e)
        rb1.wait_group(e)
        for c in range(NFC):
            inst = e.matmul(ps_b1[:, c:c + 1],
                            b1row[0:1, c * 128:(c + 1) * 128],
                            ones_cf[0:1, 0:1], start=True, stop=True)
            if c == NFC - 1:
                inst.then_inc(b1m, 1)
        rxn.wait_group(e)
        for fc in range(NFC):
            e.wait_ge(qzd, fc + 1)
            e.wait_ge(tre, fc)
            for kc in range(NF):
                inst = e.matmul(ps_t[:, kc * 128:(kc + 1) * 128],
                                qnat[fc % 2][:, kc * 128:(kc + 1) * 128],
                                ident[:], start=True, stop=True)
                if kc == NF - 1:
                    inst.then_inc(trd, NF)
            e.wait_ge(tre, fc + 1)
            for th in range(TH):
                hh = TH * fc + th
                if hh >= 2:
                    e.wait_ge(hev, hh - 1)
                for kc in range(NF):
                    inst = e.matmul(
                        ps_h[hh % 2][:],
                        w1qT[fc % 2][:, kc * 128:(kc + 1) * 128],
                        xn_dm[:, kc * TOK + th * 512:kc * TOK + (th + 1) * 512],
                        start=(kc == 0), stop=(kc == NF - 1))
                    if kc == NF - 1:
                        inst.then_inc(hmm, 1)

    @p.on("scalar")
    def _(e):
        e.wait_ge(b1m, 1)
        e.activation(b1c[:], ps_b1[:], AF.Identity).then_inc(b1e, 1)
        for fc in range(NFC):
            e.wait_ge(trd, (fc + 1) * NF)
            if fc >= 2:
                e.wait_ge(hmm, TH * (fc - 1))
            e.activation(w1qT[fc % 2][:], ps_t[:], AF.Identity
                         ).then_inc(tre, 1)
            for th in range(TH):
                hh = TH * fc + th
                e.wait_ge(hmm, hh + 1)
                if fc >= 2 and th == 0:
                    rsth.wait(e, fc - 2)
                e.activation(h_sb[fc % 2][:, th * 512:(th + 1) * 512],
                             ps_h[hh % 2][:], AF.Gelu,
                             bias=b1c[:, fc:fc + 1]).then_inc(hev, 1)

    @p.on("gpsimd")
    def _(e):
        for fc in range(NFC):
            e.wait_ge(hev, TH * (fc + 1))
            rsth.push(e.dma_start(h_out[fc * 128:(fc + 1) * 128, :],
                                  h_sb[fc % 2][:]))
        rsth.wait_group(e)



def phase_ffn_w2q(p, w2, thr_bc, nthr_bc, consts, w2q_out):
    ident = consts["ident"]
    NEC = 16
    NSC = 4
    rw2 = p.ring(4)
    rst = p.ring(4)
    qzd = p.sem()
    trd = p.sem()
    tre = p.sem()

    w2buf = [p.sb("fw_w2", [128, 2048], F32) for _ in range(2)]
    tle = p.sb("fw_tle", [128, 2048], BF16)
    qnat = [p.sb("fw_qn", [128, 2048], BF16) for _ in range(2)]
    qT = [p.sb("fw_qT", [128, 2048], BF16) for _ in range(2)]
    ps_t = p.ps("fw_pst", [128, 2048], F32, freeable=True)

    NIT = NEC * NSC

    @p.on("sync")
    def _(e):
        for it in range(NIT):
            ec, sc = divmod(it, NSC)
            if it >= 2:
                e.wait_ge(qzd, it - 1)
            rw2.push(e.dma_start(
                w2buf[it % 2][:],
                w2[ec * 128:(ec + 1) * 128, sc * 2048:(sc + 1) * 2048]))

    @p.on("vector")
    def _(e):
        for it in range(NIT):
            rw2.wait(e, it)
            if it >= 2:
                e.wait_ge(trd, it - 1)
            e.chain(e.tensor_scalar(tle[:], w2buf[it % 2][:],
                                    nthr_bc[:, 1:2], None, ALU.is_le,
                                    ALU.bypass))
            e.chain_inc(e.scalar_tensor_tensor(
                qnat[it % 2][:], w2buf[it % 2][:], thr_bc[:, 1:2], tle[:],
                ALU.is_ge, ALU.subtract), qzd)

    @p.on("tensor")
    def _(e):
        for it in range(NIT):
            e.wait_ge(qzd, it + 1)
            e.wait_ge(tre, it)
            for kc in range(16):
                inst = e.matmul(ps_t[:, kc * 128:(kc + 1) * 128],
                                qnat[it % 2][:, kc * 128:(kc + 1) * 128],
                                ident[:], start=True, stop=True)
                if kc == 15:
                    inst.then_inc(trd, 1)

    @p.on("scalar")
    def _(e):
        for it in range(NIT):
            e.wait_ge(trd, it + 1)
            if it >= 2:
                rst.wait(e, it - 2)
            e.activation(qT[it % 2][:], ps_t[:], AF.Identity).then_inc(tre, 1)

    @p.on("gpsimd")
    def _(e):
        for it in range(NIT):
            ec, sc = divmod(it, NSC)
            e.wait_ge(tre, it + 1)
            dst = w2q_out[sc * 2048:(sc + 1) * 2048,
                          ec * 128:(ec + 1) * 128].rearrange(
                "(a p) c -> p a c", p=128)
            rst.push(e.dma_start(
                dst, qT[it % 2][:].rearrange("p (a c) -> p a c", a=16)))
        rst.wait_group(e)



def phase_ffn_out(p, h_in, w2q_in, b2, s_in, x_in, consts, x1_out):
    NG = 8
    GRP = 8
    NTT = TOK // 128
    ones_row = consts["ones_row_f32"]

    rb2 = p.ring(1)
    rw = p.ring(2, per_slot=GRP)
    rh = p.ring(2, per_slot=GRP)
    rsx = p.ring(4)
    rst = p.ring(4)
    ped = p.sem()
    eva = p.sem()
    bmm = p.sem()
    bev = p.sem()
    sild = p.sem()
    find = p.sem()

    wgb = p.sb("fo_w", [128, GRP * 2048], BF16)
    hgb = [p.sb("fo_h", [128, GRP * TOK], BF16) for _ in range(2)]
    b2row = p.sb("fo_b2r", [1, 2048], F32)
    bbc2 = p.sb("fo_bbc", [128, 2048], F32)
    o_sb = p.sb("fo_o", [128, NTT * 2048], F32)
    g_sb = p.sb("fo_g", [128, 2048], F32)
    sx = [p.sb("fo_sx", [128, 2048], F32) for _ in range(2)]
    xx = [p.sb("fo_xx", [128, 2048], F32) for _ in range(2)]
    x1t = [p.sb("fo_x1", [128, 2048], F32) for _ in range(2)]
    ps_o = [p.ps("fo_po", [128, 2048], F32) for _ in range(2)]

    @p.on("sync")
    def _(e):
        rb2.push(e.dma_start(b2row[:], b2))
        for go in range(NG):
            if go >= 1:
                e.wait_ge(ped, go * NTT)
            for ci in range(GRP):
                cf = go * GRP + ci
                rw.push_slot(e.dma_start(
                    wgb[:, ci * 2048:(ci + 1) * 2048],
                    w2q_in[cf * 128:(cf + 1) * 128, :]), go)
        for tt in range(NTT):
            rsx.push(e.dma_start(sx[tt % 2][:],
                                 s_in[tt * 128:(tt + 1) * 128, :]))
            rsx.push(e.dma_start(xx[tt % 2][:],
                                 x_in[tt * 128:(tt + 1) * 128, :]))
            e.wait_ge(find, tt + 1)

    @p.on("gpsimd")
    def _(e):
        for go in range(NG):
            if go >= 2:
                e.wait_ge(ped, (go - 1) * NTT)
            for ci in range(GRP):
                cf = go * GRP + ci
                rh.push_slot(e.dma_start(
                    hgb[go % 2][:, ci * TOK:(ci + 1) * TOK],
                    h_in[cf * 128:(cf + 1) * 128, :]), go)
        for tt in range(NTT):
            e.wait_ge(find, tt + 1)
            rst.push(e.dma_start(x1_out[tt * 128:(tt + 1) * 128, :],
                                 x1t[tt % 2][:]))
        rst.wait_group(e)

    @p.on("tensor")
    def _(e):
        rb2.wait_group(e)
        for i in range(4):
            e.matmul(ps_o[0][:, i * 512:(i + 1) * 512], ones_row[:],
                     b2row[:, i * 512:(i + 1) * 512],
                     start=True, stop=True).then_inc(bmm, 1)
        e.wait_ge(bev, 4)
        for go in range(NG):
            rw.wait_slot(e, go)
            rh.wait_slot(e, go)
            for tt in range(NTT):
                it = go * NTT + tt
                if it >= 2:
                    e.wait_ge(eva, it)
                for ci in range(GRP):
                    for c4 in range(4):
                        inst = e.matmul(
                            ps_o[it % 2][:, c4 * 512:(c4 + 1) * 512],
                            hgb[go % 2][:, ci * TOK + tt * 128:
                                         ci * TOK + (tt + 1) * 128],
                            wgb[:, ci * 2048 + c4 * 512:
                                ci * 2048 + (c4 + 1) * 512],
                            start=(ci == 0), stop=(ci == GRP - 1),
                            skip_group_check=True)
                        if ci == GRP - 1 and c4 == 3:
                            inst.then_inc(ped, 1)

    @p.on("scalar")
    def _(e):
        for i in range(4):
            e.wait_ge(bmm, i + 1)
            e.activation(bbc2[:, i * 512:(i + 1) * 512],
                         ps_o[0][:, i * 512:(i + 1) * 512],
                         AF.Identity).then_inc(bev, 1)
        for tt in range(NTT):
            e.wait_ge(eva, 1 + (NG - 1) * NTT + tt + 1)
            if tt >= 1:
                e.wait_ge(find, tt)
            e.activation(g_sb[:], o_sb[:, tt * 2048:(tt + 1) * 2048],
                         AF.Silu).then_inc(sild, 1)

    @p.on("vector")
    def _(e):
        e.wait_ge(bev, 4)
        for tt in range(NTT):
            e.chain(e.tensor_copy(o_sb[:, tt * 2048:(tt + 1) * 2048],
                                  bbc2[:]))
        e.nop().then_inc(eva, 1)
        for it in range(NG * NTT):
            tt = it % NTT
            e.wait_ge(ped, it + 1)
            for c4 in range(4):
                inst = e.tensor_add(
                    o_sb[:, tt * 2048 + c4 * 512:tt * 2048 + (c4 + 1) * 512],
                    o_sb[:, tt * 2048 + c4 * 512:tt * 2048 + (c4 + 1) * 512],
                    ps_o[it % 2][:, c4 * 512:(c4 + 1) * 512])
                if c4 == 3:
                    e.chain_inc(inst, eva)
                else:
                    e.chain(inst)
        for tt in range(NTT):
            e.wait_ge(sild, tt + 1)
            rsx.wait(e, 2 * tt)
            rsx.wait(e, 2 * tt + 1)
            if tt >= 2:
                rst.wait(e, tt - 2)
            e.chain(e.tensor_mul(x1t[tt % 2][:], g_sb[:], sx[tt % 2][:]))
            e.chain_inc(e.tensor_add(x1t[tt % 2][:], x1t[tt % 2][:],
                                     xx[tt % 2][:]), find)



def emit_cols_from_row(p, row_ap, ncols, ones_cf, name):
    colt = p.sb(name, [128, ncols], F32)
    psc = p.ps(name + "_ps", [128, ncols], F32)
    dn = p.sem()

    @p.on("tensor")
    def _(e):
        for c in range(ncols):
            inst = e.matmul(psc[:, c:c + 1], row_ap[0:1, c * 128:(c + 1) * 128],
                            ones_cf[0:1, 0:1], start=True, stop=True)
            if c == ncols - 1:
                inst.then_inc(dn, 1)

    @p.on("scalar")
    def _(e):
        e.wait_ge(dn, 1)
        e.activation(colt[:], psc[:], AF.Identity).then_inc(dn, 1)
    return colt, dn



def phase_qproj(p, tq, g_state, b_state, wq, consts, cring, xi0_out):
    ones_cf = consts["ones_col_f32"]

    rgb = p.ring(1)
    grow = p.sb("xp2_gr", [1, D], F32)
    brow = p.sb("xp2_br", [1, D], F32)

    @p.on("sync")
    def _(e):
        rgb.push(e.dma_start(grow[:], g_state))
        rgb.push(e.dma_start(brow[:], b_state))

    @p.on("tensor")
    def _(e):
        rgb.wait_group(e)
        cring.wait_group(e)
    gcol, gdn = emit_cols_from_row(p, grow, 16, ones_cf, "xp2_gc")
    bcol, bdn = emit_cols_from_row(p, brow, 16, ones_cf, "xp2_bc")

    wq_bf = p.dtmp("xp2_wqbf", [1024, D], BF16)
    rwc = p.ring(2)
    rwt = p.ring(1)
    rxt = p.ring(1)
    rst = p.ring(4)
    gbd = p.sem()
    qmd = p.sem()
    qev = p.sem()

    wq_fm = p.sb("xp2_wqfm", [128, 16 * 1024], BF16)
    xng = p.sb("xp2_xng", [128, 16 * TOK], BF16)
    xio = [p.sb("xp2_xio", [128, 512], BF16) for _ in range(2)]
    ps_q = [p.ps("xp2_psq", [128, 512], F32) for _ in range(2)]

    @p.on("gpsimd")
    def _(e):
        for i in range(2):
            rwc.push(e.dma_start(wq_bf[i * 512:(i + 1) * 512, :],
                                 wq[i * 512:(i + 1) * 512, :]))
        for oc in range(8):
            for th in range(TOK // 512):
                n = oc * (TOK // 512) + th
                e.wait_ge(qev, n + 1)
                rst.push(e.dma_start(
                    xi0_out[oc * 128:(oc + 1) * 128,
                            th * 512:(th + 1) * 512], xio[n % 2][:]))
        rst.wait_group(e)

    @p.on("sync")
    def _(e):
        rwc.wait_group(e, 1)
        for c in range(16):
            rwt.push(e.dma_start(wq_fm[:, c * 1024:(c + 1) * 1024],
                                 wq_bf[:, c * 128:(c + 1) * 128],
                                 transpose=True))
        for c in range(16):
            rxt.push(e.dma_start(xng[:, c * TOK:(c + 1) * TOK],
                                 tq[:, c * 128:(c + 1) * 128],
                                 transpose=True))

    @p.on("vector")
    def _(e):
        rxt.wait_group(e)
        e.wait_ge(gdn, 2)
        e.wait_ge(bdn, 2)
        for c in range(16):
            inst = e.tensor_scalar(xng[:, c * TOK:(c + 1) * TOK],
                                   xng[:, c * TOK:(c + 1) * TOK],
                                   gcol[:, c:c + 1], bcol[:, c:c + 1],
                                   ALU.mult, ALU.add)
            e.chain(inst)
        e.nop().then_inc(gbd, 1)

    @p.on("tensor")
    def _(e):
        rwt.wait_group(e)
        e.wait_ge(gbd, 1)
        for oc in range(8):
            for th in range(TOK // 512):
                n = oc * (TOK // 512) + th
                if n >= 2:
                    e.wait_ge(qev, n - 1)
                for kc in range(16):
                    inst = e.matmul(
                        ps_q[n % 2][:],
                        wq_fm[:, kc * 1024 + oc * 128:kc * 1024 + (oc + 1) * 128],
                        xng[:, kc * TOK + th * 512:kc * TOK + (th + 1) * 512],
                        start=(kc == 0), stop=(kc == 15))
                    if kc == 15:
                        inst.then_inc(qmd, 1)

    @p.on("scalar")
    def _(e):
        for n in range(16):
            e.wait_ge(qmd, n + 1)
            if n >= 2:
                rst.wait(e, n - 2)
            e.activation(xio[n % 2][:], ps_q[n % 2][:], AF.Identity
                         ).then_inc(qev, 1)


def phase_ln2_from_bf(p, xn_in, out, T=TOK):
    nt = T // 128
    rld = p.ring(4)
    rst = p.ring(4)
    s2d = p.sem()
    vep = p.sem()
    rsd = p.sem()
    nbd = p.sem()
    nrm = p.sem()

    xt = [p.sb("l2_x", [128, D], BF16) for _ in range(2)]
    xnt = [p.sb("l2_o", [128, D], BF16) for _ in range(2)]
    scr_v = p.sb("l2_scrv", [128, D], F32)
    scr_s = p.sb("l2_scrs", [128, D], F32)
    s1 = [p.sb("l2_s1", [128, 1], F32) for _ in range(2)]
    s2 = [p.sb("l2_s2", [128, 1], F32) for _ in range(2)]
    mean = p.sb("l2_mean", [128, 1], F32)
    ex2 = p.sb("l2_ex2", [128, 1], F32)
    varep = p.sb("l2_var", [128, 1], F32)
    rstd = p.sb("l2_rstd", [128, 1], F32)
    nbias = p.sb("l2_nb", [128, 1], F32)

    @p.on("sync")
    def _(e):
        for i in range(nt):
            if i >= 2:
                e.wait_ge(nrm, i - 1)
            rld.push(e.dma_start(xt[i % 2][:], xn_in[i * 128:(i + 1) * 128, :]))

    @p.on("vector")
    def _(e):
        for i in range(nt):
            b = i % 2
            rld.wait(e, i)
            e.chain(e.tensor_scalar(scr_v[:], xt[b][:], 0.0, None,
                                    ALU.add, ALU.add, accum_out=s1[b][:]))
            e.wait_ge(s2d, i + 1)
            e.chain(e.tensor_scalar(mean[:], s1[b][:], 1.0 / D, None,
                                    ALU.mult, ALU.bypass))
            e.chain(e.tensor_scalar(ex2[:], s2[b][:], 1.0 / D, None,
                                    ALU.mult, ALU.bypass))
            e.chain(e.scalar_tensor_tensor(varep[:], mean[:], -1.0, mean[:],
                                           ALU.mult, ALU.mult))
            e.chain(e.tensor_add(varep[:], varep[:], ex2[:]))
            e.chain_inc(e.tensor_scalar(varep[:], varep[:], LN_EPS, None,
                                        ALU.add, ALU.bypass), vep)
            e.wait_ge(rsd, i + 1)
            e.chain_inc(e.scalar_tensor_tensor(nbias[:], rstd[:], -1.0,
                                               mean[:], ALU.mult, ALU.mult), nbd)

    @p.on("scalar")
    def _(e):
        for i in range(nt):
            b = i % 2
            rld.wait(e, i)
            e.activation(scr_s[:], xt[b][:], AF.Square,
                         accum_out=s2[b][:]).then_inc(s2d, 1)
            e.wait_ge(vep, i + 1)
            e.chain(e.activation(rstd[:], varep[:], AF.Ln))
            e.chain_inc(e.activation(rstd[:], rstd[:], AF.Exp,
                                     scale=-0.5), rsd)
            e.wait_ge(nbd, i + 1)
            if i >= 2:
                rst.wait(e, i - 2)
            e.activation(xnt[b][:], xt[b][:], AF.Identity, bias=nbias[:],
                         scale=rstd[:]).then_inc(nrm, 1)

    @p.on("gpsimd")
    def _(e):
        for i in range(nt):
            e.wait_ge(nrm, i + 1)
            rst.push(e.dma_start(out[i * 128:(i + 1) * 128, :],
                                 xnt[i % 2][:]))
        rst.wait_group(e)



def phase_memkv(p, mem_s, g_mem, b_mem, wk, wv, log_beta, consts, cring,
                kb_out, v_out):
    Md = 1024
    NR = 256
    ones_cf = consts["ones_col_f32"]

    mn_d = p.dtmp("mk_mn", [NR, Md], BF16)
    m0 = p.scope()
    phase_lnmem(p, mem_s, mn_d, NR, Md)
    p.barrier()
    p.pop_scope(m0)

    rgb = p.ring(1)
    grow = p.sb("mk_gr", [1, Md], F32)
    brow = p.sb("mk_br", [1, Md], F32)
    lbrow = p.sb("mk_lb", [1, 8], F32)
    betar = p.sb("mk_be", [1, 8], F32)
    beta_bc = p.sb("mk_bbc", [128, 8], F32)

    @p.on("sync")
    def _(e):
        rgb.push(e.dma_start(grow[:], g_mem))
        rgb.push(e.dma_start(brow[:], b_mem))
        rgb.push(e.dma_start(lbrow[:], log_beta))

    @p.on("tensor")
    def _(e):
        rgb.wait_group(e)
        cring.wait_group(e)
    gcol, gdn = emit_cols_from_row(p, grow, 8, ones_cf, "mk_gc")
    bcol, bdn = emit_cols_from_row(p, brow, 8, ones_cf, "mk_bc")

    wkv_bf = p.dtmp("mk_wkv", [2 * Md, Md], BF16)
    rwc = p.ring(2)
    rwt = p.ring(1)
    rmt = p.ring(1)
    rst = p.ring(4)
    bbd = p.sem()
    gbd = p.sem()
    kmd = p.sem()
    kev = p.sem()
    vmd = p.sem()
    vev = p.sem()

    wk_fm = p.sb("mk_wkfm", [128, 8 * Md], BF16)
    wv_fm = p.sb("mk_wvfm", [128, 8 * Md], BF16)
    mn_dm = p.sb("mk_mndm", [128, 8 * NR], BF16)
    kbo = [p.sb("mk_kbo", [128, NR], BF16) for _ in range(2)]
    vo = [p.sb("mk_vo", [128, 512], BF16) for _ in range(2)]
    ps_k = [p.ps("mk_psk", [128, NR], F32) for _ in range(2)]
    ps_v = [p.ps("mk_psv", [128, 512], F32) for _ in range(2)]
    ps_b = p.ps("mk_psb", [128, 8], F32)

    @p.on("gpsimd")
    def _(e):
        rwc.push(e.dma_start(wkv_bf[0:Md, :], wk))
        rwc.push(e.dma_start(wkv_bf[Md:2 * Md, :], wv))
        for oc in range(8):
            e.wait_ge(kev, oc + 1)
            rst.push(e.dma_start(kb_out[oc * 128:(oc + 1) * 128, :],
                                 kbo[oc % 2][:]))
        for n in range(4):
            e.wait_ge(vev, n + 1)
            tc, vc = divmod(n, 2)
            rst.push(e.dma_start(
                v_out[tc * 128:(tc + 1) * 128, vc * 512:(vc + 1) * 512],
                vo[n % 2][:]))
        rst.wait_group(e)

    @p.on("sync")
    def _(e):
        rwc.wait_group(e, 1)
        for c in range(8):
            rwt.push(e.dma_start(wk_fm[:, c * Md:(c + 1) * Md],
                                 wkv_bf[0:Md, c * 128:(c + 1) * 128],
                                 transpose=True))
        for c in range(8):
            rwt.push(e.dma_start(wv_fm[:, c * Md:(c + 1) * Md],
                                 wkv_bf[Md:2 * Md, c * 128:(c + 1) * 128],
                                 transpose=True))
        for c in range(8):
            rmt.push(e.dma_start(mn_dm[:, c * NR:(c + 1) * NR],
                                 mn_d[:, c * 128:(c + 1) * 128],
                                 transpose=True))

    @p.on("vector")
    def _(e):
        rmt.wait_group(e)
        e.wait_ge(gdn, 2)
        e.wait_ge(bdn, 2)
        for c in range(8):
            e.chain(e.tensor_scalar(mn_dm[:, c * NR:(c + 1) * NR],
                                    mn_dm[:, c * NR:(c + 1) * NR],
                                    gcol[:, c:c + 1], bcol[:, c:c + 1],
                                    ALU.mult, ALU.add))
        e.nop().then_inc(gbd, 1)
        for oc in range(8):
            e.wait_ge(kmd, oc + 1)
            if oc >= 2:
                rst.wait(e, oc - 2)
            e.chain_inc(e.tensor_scalar(kbo[oc % 2][:], ps_k[oc % 2][:],
                                        beta_bc[:, oc:oc + 1], None,
                                        ALU.mult, ALU.bypass), kev)

    @p.on("scalar")
    def _(e):
        rgb.wait_group(e)
        e.activation(betar[:], lbrow[:], AF.Exp).then_inc(bbd, 1)
        e.wait_ge(bbd, 2)
        e.activation(beta_bc[:], ps_b[:], AF.Identity).then_inc(bbd, 1)
        for n in range(4):
            e.wait_ge(vmd, n + 1)
            if n >= 2:
                rst.wait(e, 8 + n - 2)
            e.activation(vo[n % 2][:], ps_v[n % 2][:], AF.Identity
                         ).then_inc(vev, 1)

    @p.on("tensor")
    def _(e):
        e.wait_ge(bbd, 1)
        e.matmul(ps_b[:], consts["ones_row_f32"][:], betar[:],
                 start=True, stop=True).then_inc(bbd, 1)
        rwt.wait_group(e)
        e.wait_ge(gbd, 1)
        e.wait_ge(bbd, 3)
        for oc in range(8):
            if oc >= 2:
                e.wait_ge(kev, oc - 1)
            for kc in range(8):
                inst = e.matmul(
                    ps_k[oc % 2][:],
                    wk_fm[:, kc * Md + oc * 128:kc * Md + (oc + 1) * 128],
                    mn_dm[:, kc * NR:(kc + 1) * NR],
                    start=(kc == 0), stop=(kc == 7))
                if kc == 7:
                    inst.then_inc(kmd, 1)
        for n in range(4):
            tc, vc = divmod(n, 2)
            if n >= 2:
                e.wait_ge(vev, n - 1)
            for kc in range(8):
                inst = e.matmul(
                    ps_v[n % 2][:],
                    mn_dm[:, kc * NR + tc * 128:kc * NR + (tc + 1) * 128],
                    wv_fm[:, kc * Md + vc * 512:kc * Md + (vc + 1) * 512],
                    start=(kc == 0), stop=(kc == 7))
                if kc == 7:
                    inst.then_inc(vmd, 1)


def phase_lnmem(p, x_in, out, T, W):
    nt = T // 128
    rld = p.ring(2)
    rst = p.ring(2)
    s2d = p.sem()
    vep = p.sem()
    rsd = p.sem()
    nbd = p.sem()
    nrm = p.sem()

    xt = [p.sb("lm_x", [128, W], F32) for _ in range(2)]
    xnt = [p.sb("lm_o", [128, W], BF16) for _ in range(2)]
    scr_v = p.sb("lm_scrv", [128, W], F32)
    scr_s = p.sb("lm_scrs", [128, W], F32)
    s1 = [p.sb("lm_s1", [128, 1], F32) for _ in range(2)]
    s2 = [p.sb("lm_s2", [128, 1], F32) for _ in range(2)]
    mean = p.sb("lm_mean", [128, 1], F32)
    ex2 = p.sb("lm_ex2", [128, 1], F32)
    varep = p.sb("lm_var", [128, 1], F32)
    rstd = p.sb("lm_rstd", [128, 1], F32)
    nbias = p.sb("lm_nb", [128, 1], F32)

    @p.on("sync")
    def _(e):
        for i in range(nt):
            if i >= 2:
                e.wait_ge(nrm, i - 1)
            rld.push(e.dma_start(xt[i % 2][:], x_in[i * 128:(i + 1) * 128, :]))

    @p.on("vector")
    def _(e):
        for i in range(nt):
            b = i % 2
            rld.wait(e, i)
            e.chain(e.tensor_scalar(scr_v[:], xt[b][:], 0.0, None,
                                    ALU.add, ALU.add, accum_out=s1[b][:]))
            e.wait_ge(s2d, i + 1)
            e.chain(e.tensor_scalar(mean[:], s1[b][:], 1.0 / W, None,
                                    ALU.mult, ALU.bypass))
            e.chain(e.tensor_scalar(ex2[:], s2[b][:], 1.0 / W, None,
                                    ALU.mult, ALU.bypass))
            e.chain(e.scalar_tensor_tensor(varep[:], mean[:], -1.0, mean[:],
                                           ALU.mult, ALU.mult))
            e.chain(e.tensor_add(varep[:], varep[:], ex2[:]))
            e.chain_inc(e.tensor_scalar(varep[:], varep[:], LN_EPS, None,
                                        ALU.add, ALU.bypass), vep)
            e.wait_ge(rsd, i + 1)
            e.chain_inc(e.scalar_tensor_tensor(nbias[:], rstd[:], -1.0,
                                               mean[:], ALU.mult, ALU.mult), nbd)

    @p.on("scalar")
    def _(e):
        for i in range(nt):
            b = i % 2
            rld.wait(e, i)
            e.activation(scr_s[:], xt[b][:], AF.Square,
                         accum_out=s2[b][:]).then_inc(s2d, 1)
            e.wait_ge(vep, i + 1)
            e.chain(e.activation(rstd[:], varep[:], AF.Ln))
            e.chain_inc(e.activation(rstd[:], rstd[:], AF.Exp,
                                     scale=-0.5), rsd)
            e.wait_ge(nbd, i + 1)
            if i >= 2:
                rst.wait(e, i - 2)
            e.activation(xnt[b][:], xt[b][:], AF.Identity, bias=nbias[:],
                         scale=rstd[:]).then_inc(nrm, 1)

    @p.on("gpsimd")
    def _(e):
        for i in range(nt):
            e.wait_ge(nrm, i + 1)
            rst.push(e.dma_start(out[i * 128:(i + 1) * 128, :],
                                 xnt[i % 2][:]))
        rst.wait_group(e)



def phase_entmax(p, xi0_in, kb_in, v_in, consts, cring, xi3_out,
                 n_mich=2, n_bf=2, n_fp=2, NSTEP=3, NH=8):
    NTC = TOK // 128
    NIT = n_mich + n_bf + n_fp
    NBF = n_mich + n_bf
    NJ = NIT + 1
    N = 2048
    ident = consts["ident"]

    rkb = p.ring(1)
    rv = p.ring(1)
    rxi = p.ring(1)
    rst = p.ring(4)
    smd = p.sem()
    sev = p.sem()
    rel = p.sem()
    sst = p.sem()
    ngp = p.sem()
    rowd = p.sem()
    dcd = p.sem()
    sqd = p.sem()
    ttd = p.sem()
    tte = p.sem()
    xmd = p.sem()
    xev = p.sem()
    sgc = p.sem()

    kb_sb = p.sb("em_kb", [128, NH * N], BF16)
    v_sb = p.sb("em_v", [128, 16 * 1024], BF16)
    xi = [p.sb("em_xi", [128, NH * TOK], BF16) for _ in range(2)]
    xp = p.sb("em_xp", [128, NTC * N], F32)
    m = [p.sb("em_m", [128, N], F32) for _ in range(2)]
    m_bf = [p.sb("em_mbf", [128, N], BF16) for _ in range(2)]

    w_bf = [p.sb("em_w", [128, N], BF16) for _ in range(2)]
    wT = [p.sb("em_wT", [128, N], BF16) for _ in range(2)]
    mx2 = p.sb("em_mx2", [128, NTC], F32)
    tau = p.sb("em_tau", [128, NTC], F32)
    a_r = p.sb("em_a", [128, NTC], F32)
    na_r = p.sb("em_na", [128, NTC], F32)
    s1c = p.sb("em_s1", [128, NTC], F32)
    s2c = p.sb("em_s2", [128, NTC], F32)
    ncc = p.sb("em_nc", [128, NTC], F32)
    ssg = p.sb("em_ssg", [128, NTC], F32)
    t1 = p.sb("em_t1", [128, NTC], F32)
    t2 = p.sb("em_t2", [128, NTC], F32)
    t3 = p.sb("em_t3", [128, NTC], F32)
    disc = p.sb("em_dc", [128, NTC], F32)
    sqr = p.sb("em_sq", [128, NTC], F32)
    gj = p.sb("em_gj", [128, N], BF16)
    msq = w_bf[0]
    ps_s = p.ps("em_pss", [128, N], F32)
    ps_t = p.ps("em_pst", [128, N], F32)
    psA = [ps_s, ps_t]
    ps_xi = ps_s[:, 0:128]

    @p.on("sync")
    def _(e):
        for c in range(NH):
            rkb.push(e.dma_start(kb_sb[:, c * N:(c + 1) * N],
                                 kb_in[c * 128:(c + 1) * 128, :]))
        for c in range(16):
            rv.push(e.dma_start(v_sb[:, c * 1024:(c + 1) * 1024],
                                v_in[c * 128:(c + 1) * 128, :]))
        for c in range(NH):
            rxi.push(e.dma_start(xi[0][:, c * TOK:(c + 1) * TOK],
                                 xi0_in[c * 128:(c + 1) * 128, :]))

    @p.on("tensor")
    def _(e):
        cring.wait_group(e)
        rkb.wait_group(e)
        rv.wait_group(e)
        rxi.wait_group(e)
        for s in range(NSTEP):
            src = xi[s % 2]
            for h in range(NH):
                B = s * NH + h
                if s >= 1:
                    e.wait_ge(xev, s * NH * NTC)
                if B >= 1:
                    e.wait_ge(xev, (B - 1) * NTC + NTC)
                    e.wait_ge(tte, (B - 1) * NTC + NTC)
                for tc in range(NTC):
                    g = B * NTC + tc
                    if tc >= 2:
                        e.wait_ge(sev, g - 1)
                    for c4 in range(4):
                        inst = e.matmul(
                            psA[tc % 2][:, c4 * 512:(c4 + 1) * 512],
                            src[:, h * TOK + tc * 128:h * TOK + (tc + 1) * 128],
                            kb_sb[:, h * N + c4 * 512:h * N + (c4 + 1) * 512],
                            start=True, stop=True, skip_group_check=True)
                        if c4 == 3:
                            inst.then_inc(smd, 1)
                e.wait_ge(sev, B * NTC + NTC)
                for tc in range(NTC + 1):
                    g = B * NTC + tc
                    if tc < NTC:
                        e.wait_ge(sst, B * NJ * NTC + NIT * NTC + tc + 1)
                        if tc >= 1:
                            e.wait_ge(tte, g)
                        for kc in range(16):
                            inst = e.matmul(
                                ps_t[:, kc * 128:(kc + 1) * 128],
                                w_bf[tc % 2][:, kc * 128:(kc + 1) * 128],
                                ident[:], start=True, stop=True)
                            if kc == 15:
                                inst.then_inc(ttd, 1)
                    if tc >= 1:
                        tp_ = tc - 1
                        gp = B * NTC + tp_
                        e.wait_ge(tte, gp + 1)
                        if tp_ >= 1:
                            e.wait_ge(xev, gp)
                        for kc in range(16):
                            inst = e.matmul(
                                ps_xi,
                                v_sb[:, kc * 1024 + h * 128:
                                     kc * 1024 + (h + 1) * 128],
                                wT[tp_ % 2][:, kc * 128:(kc + 1) * 128],
                                start=(kc == 0), stop=(kc == 15),
                                skip_group_check=True)
                            if kc == 15:
                                inst.then_inc(xmd, 1)

    @p.on("scalar")
    def _(e):
        for s in range(NSTEP):
            for h in range(NH):
                B = s * NH + h
                for tc in range(NTC):
                    g = B * NTC + tc
                    e.wait_ge(smd, g + 1)
                    e.activation(xp[:, tc * N:(tc + 1) * N], psA[tc % 2][:],
                                 AF.Identity, scale=0.5).then_inc(sev, 1)
                for j in range(NIT):
                    e.wait_ge(rowd, B * (NJ + 1) + j + 1)
                    mt = m_bf if j < NBF else m
                    for tc in range(NTC):
                        gr = B * NJ * NTC + j * NTC + tc
                        if gr >= 2:
                            e.wait_ge(sst, gr - 1)
                        e.activation(mt[gr % 2][:],
                                     xp[:, tc * N:(tc + 1) * N],
                                     AF.Relu, bias=na_r[:, tc:tc + 1],
                                     accum_out=s1c[:, tc:tc + 1]
                                     ).then_inc(rel, 1)
                    if j < n_mich:
                        for tc in range(NTC):
                            gs = (B * n_mich + j) * NTC + tc
                            if gs >= 1:
                                e.wait_ge(sgc, gs)
                            e.activation(gj[:],
                                         xp[:, tc * N:(tc + 1) * N],
                                         AF.Sign, bias=na_r[:, tc:tc + 1],
                                         accum_out=ssg[:, tc:tc + 1]
                                         ).then_inc(sgc, 1)
                        e.wait_ge(dcd, B * n_mich + j + 1)
                        e.activation(sqr[:], disc[:], AF.Sqrt
                                     ).then_inc(sqd, 1)
                e.wait_ge(rowd, B * (NJ + 1) + NIT + 1)
                gr0 = B * NJ * NTC + NIT * NTC
                e.activation(m[gr0 % 2][:], xp[:, 0:N], AF.Relu,
                             bias=na_r[:, 0:1],
                             accum_out=s1c[:, 0:1]).then_inc(rel, 1)
                for tc in range(NTC):
                    g = B * NTC + tc
                    if tc + 1 < NTC:
                        gr = gr0 + tc + 1
                        if gr >= 2:
                            e.wait_ge(sst, gr - 1)
                        e.activation(m[gr % 2][:],
                                     xp[:, (tc + 1) * N:(tc + 2) * N],
                                     AF.Relu, bias=na_r[:, tc + 1:tc + 2],
                                     accum_out=s1c[:, tc + 1:tc + 2]
                                     ).then_inc(rel, 1)
                    if g >= 1:
                        e.wait_ge(xmd, g)
                    e.wait_ge(ttd, g + 1)
                    e.activation(wT[tc % 2][:], ps_t[:], AF.Identity
                                 ).then_inc(tte, 1)
                    e.wait_ge(xmd, g + 1)
                    e.activation(xi[(s + 1) % 2][:, h * TOK + tc * 128:
                                                 h * TOK + (tc + 1) * 128],
                                 ps_xi, AF.Identity).then_inc(xev, 1)

    @p.on("vector")
    def _(e):
        for s in range(NSTEP):
            for h in range(NH):
                B = s * NH + h
                for tc in range(NTC):
                    e.wait_ge(sev, B * NTC + tc + 1)
                    e.chain(e.reduce_max(mx2[:, tc:tc + 1],
                                         xp[:, tc * N:(tc + 1) * N], axis=AX.X))
                e.chain(e.tensor_scalar(a_r[:], mx2[:], -1.0, None,
                                        ALU.add, ALU.bypass))
                e.chain(e.tensor_scalar(tau[:], a_r[:], 0.0, None,
                                        ALU.mult, ALU.bypass))
                e.chain(e.tensor_scalar(tau[:], tau[:], -1.0, None,
                                        ALU.add, ALU.bypass))
                e.chain_inc(e.tensor_scalar(na_r[:], a_r[:], -1.0, None,
                                            ALU.mult, ALU.bypass),
                            rowd)
                for j in range(NIT):
                    mt = m_bf if j < NBF else m
                    for tc in range(NTC):
                        gr = B * NJ * NTC + j * NTC + tc
                        e.wait_ge(rel, gr + 1)
                        e.chain_inc(e.scalar_tensor_tensor(
                            msq[:], mt[gr % 2][:], 0.0, mt[gr % 2][:],
                            ALU.add, ALU.mult,
                            accum_out=s2c[:, tc:tc + 1]), sst)
                    if j < n_mich:
                        e.wait_ge(sgc, (B * n_mich + j + 1) * NTC)
                        e.chain(e.tensor_scalar(ncc[:], ssg[:], float(N),
                                                0.5, ALU.add, ALU.mult))
                        e.chain(e.tensor_mul(t1[:], ncc[:], tau[:]))
                        e.chain(e.tensor_add(t1[:], t1[:], s1c[:]))
                        e.chain(e.tensor_add(t2[:], s1c[:], t1[:]))
                        e.chain(e.tensor_mul(t2[:], tau[:], t2[:]))
                        e.chain(e.scalar_tensor_tensor(t2[:], t2[:], -1.0,
                                                       s2c[:], ALU.add,
                                                       ALU.add))
                        e.chain(e.tensor_mul(t2[:], ncc[:], t2[:]))
                        e.chain(e.tensor_mul(disc[:], t1[:], t1[:]))
                        e.chain(e.tensor_sub(disc[:], disc[:], t2[:]))
                        e.chain_inc(e.tensor_scalar(disc[:], disc[:], 0.0,
                                                    None, ALU.max, ALU.bypass),
                                    dcd)
                        e.wait_ge(sqd, B * n_mich + j + 1)
                        e.chain(e.tensor_sub(t1[:], t1[:], sqr[:]))
                        e.chain(e.reciprocal(t2[:], ncc[:]))
                        e.chain(e.tensor_mul(tau[:], t1[:], t2[:]))
                    else:
                        e.chain(e.tensor_scalar(t1[:], s2c[:], -1.0, None,
                                                ALU.add, ALU.bypass))
                        e.chain(e.tensor_scalar(t2[:], s1c[:], 2.0, None,
                                                ALU.mult, ALU.bypass))
                        e.chain(e.reciprocal(t3[:], t2[:]))
                        e.chain(e.tensor_mul(t1[:], t1[:], t3[:]))
                        e.chain(e.tensor_add(tau[:], tau[:], t1[:]))
                    e.chain(e.tensor_add(a_r[:], mx2[:], tau[:]))
                    e.chain_inc(e.tensor_scalar(na_r[:], a_r[:], -1.0, None,
                                                ALU.mult, ALU.bypass), rowd)
                for tc in range(NTC):
                    gr = B * NJ * NTC + NIT * NTC + tc
                    e.wait_ge(rel, gr + 1)
                    if B * NTC + tc >= 2:
                        e.wait_ge(ttd, B * NTC + tc - 1)
                    e.chain_inc(e.scalar_tensor_tensor(
                        w_bf[tc % 2][:], m[gr % 2][:], 0.0, m[gr % 2][:],
                        ALU.add, ALU.mult), sst)
                e.nop().then_inc(rowd, 1)

    @p.on("gpsimd")
    def _(e):
        e.wait_ge(xev, NSTEP * NH * NTC)
        for c in range(NH):
            rst.push(e.dma_start(xi3_out[c * 128:(c + 1) * 128, :],
                                 xi[NSTEP % 2][:, c * TOK:(c + 1) * TOK]))
        rst.wait_group(e)



def phase_wo(p, xi3_in, wo, consts, cring, s2_out):
    NTT = TOK // 128
    wo_bf = p.dtmp("wo_bf", [D, 1024], BF16)
    rwc = p.ring(1)
    rwt = p.ring(1)
    rxi = p.ring(1)
    rst = p.ring(4)
    pmd = p.sem()
    pev = p.sem()

    wo_fm = p.sb("wo2_fm", [128, 8 * D], BF16)
    xi_sb = p.sb("wo2_xi", [128, 8 * TOK], BF16)
    so = [p.sb("wo2_so", [128, D], F32) for _ in range(2)]
    ps_o = [p.ps("wo2_po", [128, D], F32) for _ in range(2)]

    @p.on("gpsimd")
    def _(e):
        for i in range(2):
            rwc.push(e.dma_start(wo_bf[i * 1024:(i + 1) * 1024, :],
                                 wo[i * 1024:(i + 1) * 1024, :]))
        for tt in range(NTT):
            e.wait_ge(pev, tt + 1)
            rst.push(e.dma_start(s2_out[tt * 128:(tt + 1) * 128, :],
                                 so[tt % 2][:]))
        rst.wait_group(e)

    @p.on("sync")
    def _(e):
        rwc.wait_group(e, 1)
        for c in range(8):
            rwt.push(e.dma_start(wo_fm[:, c * D:(c + 1) * D],
                                 wo_bf[:, c * 128:(c + 1) * 128],
                                 transpose=True))
        for c in range(8):
            rxi.push(e.dma_start(xi_sb[:, c * TOK:(c + 1) * TOK],
                                 xi3_in[c * 128:(c + 1) * 128, :]))

    @p.on("tensor")
    def _(e):
        cring.wait_group(e)
        rwt.wait_group(e, 7)
        rxi.wait_group(e, 7)
        for tt in range(NTT):
            if tt >= 2:
                e.wait_ge(pev, tt - 1)
            for kc in range(8):
                for c4 in range(4):
                    inst = e.matmul(
                        ps_o[tt % 2][:, c4 * 512:(c4 + 1) * 512],
                        xi_sb[:, kc * TOK + tt * 128:kc * TOK + (tt + 1) * 128],
                        wo_fm[:, kc * D + c4 * 512:kc * D + (c4 + 1) * 512],
                        start=(kc == 0), stop=(kc == 7),
                        skip_group_check=True)
                    if kc == 7 and c4 == 3:
                        inst.then_inc(pmd, 1)

    @p.on("scalar")
    def _(e):
        for tt in range(NTT):
            e.wait_ge(pmd, tt + 1)
            if tt >= 2:
                rst.wait(e, tt - 2)
            e.activation(so[tt % 2][:], ps_o[tt % 2][:], AF.Identity
                         ).then_inc(pev, 1)



def phase_moe(p, xn2_in, gate_w, w1, w3, w2, s2_in, x1_in, consts, cring,
              x2_out):
    E = 16
    I = 1408
    NI = I // 128
    NTT = TOK // 128
    NF = D // 128
    TH = TOK // 512

    rxt = p.ring(1)
    rgw = p.ring(1)
    rw13 = p.ring(16)
    rsx = p.ring(4)
    rst = p.ring(4)
    lmd = p.sem()
    lev = p.sem()
    sgm = p.sem()
    cmb = p.sem()
    gmd = p.sem()
    gev = p.sem()
    gud = p.sem()
    ymd = p.sem()
    yac = p.sem()
    sil = p.sem()
    fin = p.sem()

    xn2_dm = p.sb("mo_xn", [128, NF * TOK], BF16)
    gate_sb = p.sb("mo_gw", [128, NF * 16], BF16)
    logit = p.sb("mo_lg", [128, NTT * 16], F32)
    m1 = p.sb("mo_m1", [128, 1], F32)
    m2 = p.sb("mo_m2", [128, 1], F32)
    d12 = p.sb("mo_d12", [128, 1], F32)
    c1 = p.sb("mo_c1", [128, 1], F32)
    c2 = p.sb("mo_c2", [128, 1], F32)
    msk1 = p.sb("mo_mk1", [128, 16], F32)
    tmpl = p.sb("mo_tmp", [128, 16], F32)
    comb = p.sb("mo_cmb", [128, NTT * 16], F32)
    we_sb = p.sb("mo_we", [128, NF * I], BF16)
    sg = p.sb("mo_sg", [128, NI * TOK], BF16)
    up = p.sb("mo_up", [128, NI * TOK], BF16)
    moe = p.sb("mo_moe", [128, NTT * D], F32)
    s2t = p.sb("mo_s2", [128, D], F32)
    x2t = p.sb("mo_x2", [128, D], F32)
    ps_l = p.ps("mo_pl", [128, 16], F32)
    ps_g = [p.ps("mo_pg", [128, 512], F32) for _ in range(2)]
    ps_y = [p.ps("mo_py", [128, D // 2], F32) for _ in range(2)]

    W13_PE = 43

    @p.on("sync")
    def _(e):
        for c in range(NF):
            rxt.push(e.dma_start(xn2_dm[:, c * TOK:(c + 1) * TOK],
                                 xn2_in[:, c * 128:(c + 1) * 128],
                                 transpose=True))
        for tt in range(NTT):
            if tt >= 1:
                e.wait_ge(fin, tt)
            rsx.push(e.dma_start(s2t[:], s2_in[tt * 128:(tt + 1) * 128, :]))
            e.wait_ge(sil, tt + 1)
            rsx.push(e.dma_start(moe[:, tt * D:(tt + 1) * D],
                                 x1_in[tt * 128:(tt + 1) * 128, :]))

    @p.on("gpsimd")
    def _(e):
        rgw.push(e.dma_start(
            gate_sb[:].rearrange("p (a c) -> p a c", a=NF),
            gate_w.rearrange("(a p) c -> p a c", p=128)))
        for ex in range(E):
            if ex >= 1:
                e.wait_ge(ymd, ex * NTT)
            for kc in range(NF):
                rw13.push(e.dma_start(we_sb[:, kc * I:(kc + 1) * I],
                                      w1[ex, kc * 128:(kc + 1) * 128, :]))
            e.wait_ge(gmd, (2 * ex + 1) * NI * TH)
            for kc in range(NF):
                rw13.push(e.dma_start(we_sb[:, kc * I:(kc + 1) * I],
                                      w3[ex, kc * 128:(kc + 1) * 128, :]))
            e.wait_ge(gmd, (2 * ex + 2) * NI * TH)
            w2v = we_sb[:].rearrange("p (a c) -> p a c", a=NI, c=2048)
            for ic in range(NI):
                rw13.push(e.dma_start(w2v[:, ic, :],
                                      w2[ex, ic * 128:(ic + 1) * 128, :]))
        for tt in range(NTT):
            e.wait_ge(fin, tt + 1)
            rst.push(e.dma_start(x2_out[tt * 128:(tt + 1) * 128, :],
                                 x2t[:]))
        rst.wait_group(e)

    @p.on("tensor")
    def _(e):
        cring.wait_group(e)
        rxt.wait_group(e, NF - 1)
        rgw.wait_group(e, 0)
        for tt in range(NTT):
            if tt >= 1:
                e.wait_ge(lev, tt)
            for kc in range(NF):
                inst = e.matmul(ps_l[:],
                                xn2_dm[:, kc * TOK + tt * 128:
                                       kc * TOK + (tt + 1) * 128],
                                gate_sb[:, kc * 16:(kc + 1) * 16],
                                start=(kc == 0), stop=(kc == NF - 1))
                if kc == NF - 1:
                    inst.then_inc(lmd, 1)
        for ex in range(E):
            for half in range(2):
                for ic in range(NI):
                    for th in range(TH):
                        n = (2 * ex + half) * NI * TH + ic * TH + th
                        if n >= 2:
                            e.wait_ge(gev, n - 1)
                        for kc in range(NF):
                            if ic == 0 and th == 0:
                                rw13.wait(e, ex * W13_PE + half * NF + kc)
                            inst = e.matmul(
                                ps_g[n % 2][:],
                                we_sb[:, kc * I + ic * 128:
                                      kc * I + (ic + 1) * 128],
                                xn2_dm[:, kc * TOK + th * 512:
                                       kc * TOK + (th + 1) * 512],
                                start=(kc == 0), stop=(kc == NF - 1))
                            if kc == NF - 1:
                                inst.then_inc(gmd, 1)
            e.wait_ge(gud, ex * NI + NI)
            for tt in range(NTT):
                for hf in range(2):
                    yg = 2 * (ex * NTT + tt) + hf
                    e.wait_ge(yac, yg + 1)
                    for ic in range(NI):
                        if tt == 0 and hf == 0:
                            rw13.wait(e, ex * W13_PE + 2 * NF + ic)
                        for c4 in range(2):
                            inst = e.matmul(
                                ps_y[hf][:, c4 * 512:(c4 + 1) * 512],
                                up[:, ic * TOK + tt * 128:
                                   ic * TOK + (tt + 1) * 128],
                                we_sb[:, ic * 2048 + hf * 1024 + c4 * 512:
                                      ic * 2048 + hf * 1024 + (c4 + 1) * 512],
                                start=(ic == 0), stop=(ic == NI - 1),
                                skip_group_check=True)
                            if ic == NI - 1 and c4 == 1:
                                inst.then_inc(ymd, 1)

    @p.on("scalar")
    def _(e):
        for tt in range(NTT):
            e.wait_ge(lmd, tt + 1)
            e.activation(logit[:, tt * 16:(tt + 1) * 16], ps_l[:],
                         AF.Identity).then_inc(lev, 1)
        for tt in range(NTT):
            e.wait_ge(sgm, 3 * tt + 1)
            e.chain_inc(e.activation(c1[:], d12[:], AF.Sigmoid), sgm)
            e.chain_inc(e.activation(c2[:], d12[:], AF.Sigmoid,
                                     scale=-1.0), sgm)
        for ex in range(E):
            for half in range(2):
                for ic in range(NI):
                    for th in range(TH):
                        n = (2 * ex + half) * NI * TH + ic * TH + th
                        e.wait_ge(gmd, n + 1)
                        if half == 0:
                            e.wait_ge(gud, max(0, (ex - 1) * NI + ic + 1))
                            e.activation(sg[:, ic * TOK + th * 512:
                                            ic * TOK + (th + 1) * 512],
                                         ps_g[n % 2][:], AF.Silu
                                         ).then_inc(gev, 1)
                        else:
                            e.wait_ge(ymd, ex * NTT)
                            e.activation(up[:, ic * TOK + th * 512:
                                            ic * TOK + (th + 1) * 512],
                                         ps_g[n % 2][:], AF.Identity
                                         ).then_inc(gev, 1)
        for tt in range(NTT):
            e.wait_ge(yac, 1 + 2 * ((E - 1) * NTT + tt + 1))
            e.wait_ge(fin, tt)
            if tt >= 1:
                rst.wait(e, tt - 1)
            e.activation(x2t[:], moe[:, tt * D:(tt + 1) * D], AF.Silu
                         ).then_inc(sil, 1)

    @p.on("vector")
    def _(e):
        for tt in range(NTT):
            e.wait_ge(lev, tt + 1)
            lg = logit[:, tt * 16:(tt + 1) * 16]
            e.chain(e.reduce_max(m1[:], lg, axis=AX.X))
            e.chain(e.tensor_scalar(msk1[:], lg, m1[:], None,
                                    ALU.is_equal, ALU.bypass))
            e.chain(e.scalar_tensor_tensor(tmpl[:], msk1[:], -1e30, lg,
                                           ALU.mult, ALU.add))
            e.chain(e.reduce_max(m2[:], tmpl[:], axis=AX.X))
            e.chain_inc(e.tensor_sub(d12[:], m1[:], m2[:]), sgm)
            e.wait_ge(sgm, 3 * tt + 3)
            e.chain(e.tensor_scalar(comb[:, tt * 16:(tt + 1) * 16],
                                    msk1[:], c1[:], None, ALU.mult,
                                    ALU.bypass))
            e.chain(e.tensor_scalar(tmpl[:], tmpl[:], m2[:], None,
                                    ALU.is_equal, ALU.bypass))
            e.chain_inc(e.scalar_tensor_tensor(
                comb[:, tt * 16:(tt + 1) * 16], tmpl[:], c2[:],
                comb[:, tt * 16:(tt + 1) * 16], ALU.mult, ALU.add), cmb)
        e.chain(e.memset(moe[:], 0.0))
        e.nop().then_inc(yac, 1)
        for ex in range(E):
            for ic in range(NI):
                for th in range(TH):
                    ng = (2 * ex + 1) * NI * TH + ic * TH + th
                    e.wait_ge(gev, ng + 1)
                e.chain_inc(e.tensor_mul(up[:, ic * TOK:(ic + 1) * TOK],
                                         sg[:, ic * TOK:(ic + 1) * TOK],
                                         up[:, ic * TOK:(ic + 1) * TOK]),
                            gud)
            for tt in range(NTT):
                for hf in range(2):
                    yg = 2 * (ex * NTT + tt) + hf
                    e.wait_ge(ymd, yg + 1)
                    e.wait_ge(cmb, tt + 1)
                    for c4 in range(2):
                        o0 = tt * D + hf * 1024 + c4 * 512
                        inst = e.scalar_tensor_tensor(
                            moe[:, o0:o0 + 512],
                            ps_y[hf][:, c4 * 512:(c4 + 1) * 512],
                            comb[:, tt * 16 + ex:tt * 16 + ex + 1],
                            moe[:, o0:o0 + 512],
                            ALU.mult, ALU.add)
                        if c4 == 1:
                            e.chain_inc(inst, yac)
                        else:
                            e.chain(inst)
        for tt in range(NTT):
            e.wait_ge(sil, tt + 1)
            rsx.wait(e, 2 * tt)
            rsx.wait(e, 2 * tt + 1)
            e.chain(e.tensor_mul(x2t[:], x2t[:], s2t[:]))
            e.chain_inc(e.tensor_add(x2t[:], x2t[:],
                                     moe[:, tt * D:(tt + 1) * D]), fin)



def phase_kv_full(p, xn_full, w_attn, tabk, consts, cring, k_out, v_out):
    NF = D // 128
    NB = S // TOK
    TH = TOK // 512
    perm = consts["permsign"]

    rxn = p.ring(2, per_slot=D // 128)
    rtab = p.ring(2, per_slot=2)
    rwk = p.ring(1)
    rwv = p.ring(1)
    rstk = p.ring(4)
    rstv = p.ring(4)
    ped = p.sem()
    rotd = p.sem()
    qrd = p.sem()
    rpd = p.sem()
    pvd = p.sem()
    vevd = p.sem()
    xnc = p.sem()

    xn_dm = [p.sb("kvf_xn", [128, NF * TOK], BF16) for _ in range(2)]
    wk_sb = p.sb("kvf_wk", [128, NF * 512], BF16)
    wv_sb = p.sb("kvf_wv", [128, NF * 2048], BF16)
    ck = [p.sb("kvf_ck", [128, TOK], F32) for _ in range(2)]
    sk = [p.sb("kvf_sk", [128, TOK], F32) for _ in range(2)]
    qraw = [p.sb("kvf_qraw", [128, 512], F32) for _ in range(2)]
    t1 = p.sb("kvf_t1", [128, 512], F32)
    t2 = p.sb("kvf_t2", [128, 512], F32)
    qo = [p.sb("kvf_qo", [128, 512], BF16) for _ in range(2)]
    vo = [p.sb("kvf_vo", [128, 512], BF16) for _ in range(2)]
    pqk = [p.ps("kvf_pqk", [128, 512], F32) for _ in range(2)]
    prot = [p.ps("kvf_prot", [128, 512], F32) for _ in range(2)]
    pv = [p.ps("kvf_pv", [128, 512], F32) for _ in range(2)]

    KC = 4
    NV = (TOK // 128) * 4

    @p.on("sync")
    def _(e):
        for tb in range(NB):
            if tb >= 2:
                e.wait_ge(xnc, tb - 1)
            for c in range(NF):
                rxn.push_slot(e.dma_start(
                    xn_dm[tb % 2][:, c * TOK:(c + 1) * TOK],
                    xn_full[tb * TOK:(tb + 1) * TOK, c * 128:(c + 1) * 128],
                    transpose=True), tb)
            rtab.push_slot(e.dma_start(
                ck[tb % 2][:], tabk["cosk"][:, tb * TOK:(tb + 1) * TOK]), tb)
            rtab.push_slot(e.dma_start(
                sk[tb % 2][:], tabk["sink"][:, tb * TOK:(tb + 1) * TOK]), tb)
            for c in range(KC):
                for h in range(TH):
                    n = (tb * KC + c) * TH + h
                    e.wait_ge(rpd, n + 1)
                    rstk.push(e.dma_start(
                        k_out[c * 128:(c + 1) * 128,
                              tb * TOK + h * 512:tb * TOK + (h + 1) * 512],
                        qo[n % 2][:]))
            for tc in range(TOK // 128):
                for vc in range(4):
                    n = tb * NV + 4 * tc + vc
                    e.wait_ge(vevd, n + 1)
                    rstv.push(e.dma_start(
                        v_out[tb * TOK + tc * 128:tb * TOK + (tc + 1) * 128,
                              vc * 512:(vc + 1) * 512],
                        vo[n % 2][:]))
        rstk.wait_group(e)
        rstv.wait_group(e)

    @p.on("gpsimd")
    def _(e):
        rwk.push(e.dma_start(
            wk_sb[:].rearrange("p (a c) -> p a c", a=NF),
            w_attn[:, 2048:2560].rearrange("(a p) c -> p a c", p=128)))
        rwv.push(e.dma_start(
            wv_sb[:].rearrange("p (a c) -> p a c", a=NF),
            w_attn[:, 2560:4608].rearrange("(a p) c -> p a c", p=128)))

    @p.on("tensor")
    def _(e):
        cring.wait_group(e)
        rwk.wait_group(e, 0)
        rwv.wait_group(e, 0)
        for tb in range(NB):
            rxn.wait_slot(e, tb)
            rtab.wait_slot(e, tb)
            for c in range(KC):
                for h in range(TH):
                    n = (tb * KC + c) * TH + h
                    if n >= 2:
                        e.wait_ge(qrd, n - 1)
                    for kc in range(NF):
                        inst = e.matmul(
                            pqk[n % 2][:],
                            wk_sb[:, kc * 512 + c * 128:kc * 512 + (c + 1) * 128],
                            xn_dm[tb % 2][:, kc * TOK + h * 512:
                                          kc * TOK + h * 512 + 512],
                            start=(kc == 0), stop=(kc == NF - 1))
                        if kc == NF - 1:
                            inst.then_inc(ped, 1)
                    e.wait_ge(qrd, n + 1)
                    if n >= 2:
                        e.wait_ge(rpd, n - 1)
                    e.matmul(prot[n % 2][:], perm[:], qraw[n % 2][:],
                             start=True, stop=True).then_inc(rotd, 1)
            for tc in range(TOK // 128):
                for vc in range(4):
                    n = tb * NV + 4 * tc + vc
                    if n >= 2:
                        e.wait_ge(vevd, n - 1)
                    for kc in range(NF):
                        inst = e.matmul(
                            pv[n % 2][:],
                            xn_dm[tb % 2][:, kc * TOK + tc * 128:
                                          kc * TOK + (tc + 1) * 128],
                            wv_sb[:, kc * 2048 + vc * 512:
                                  kc * 2048 + (vc + 1) * 512],
                            start=(kc == 0), stop=(kc == NF - 1))
                        if kc == NF - 1:
                            inst.then_inc(pvd, 1)
            e.nop().then_inc(xnc, 1)

    @p.on("scalar")
    def _(e):
        for tb in range(NB):
            for c in range(KC):
                for h in range(TH):
                    n = (tb * KC + c) * TH + h
                    e.wait_ge(ped, n + 1)
                    if n >= 2:
                        e.wait_ge(rotd, n - 1)
                        e.wait_ge(rpd, n - 1)
                    e.activation(qraw[n % 2][:], pqk[n % 2][:], AF.Identity
                                 ).then_inc(qrd, 1)
            for tc in range(TOK // 128):
                for vc in range(4):
                    n = tb * NV + 4 * tc + vc
                    e.wait_ge(pvd, n + 1)
                    if n >= 2:
                        rstv.wait(e, n - 2)
                    e.activation(vo[n % 2][:], pv[n % 2][:], AF.Identity
                                 ).then_inc(vevd, 1)

    @p.on("vector")
    def _(e):
        for tb in range(NB):
            for c in range(KC):
                for h in range(TH):
                    n = (tb * KC + c) * TH + h
                    e.wait_ge(qrd, n + 1)
                    e.chain(e.tensor_mul(
                        t1[:], qraw[n % 2][:],
                        ck[tb % 2][:, h * 512:(h + 1) * 512]))
                    e.wait_ge(rotd, n + 1)
                    e.chain(e.tensor_mul(
                        t2[:], prot[n % 2][:],
                        sk[tb % 2][:, h * 512:(h + 1) * 512]))
                    if n >= 2:
                        rstk.wait(e, n - 2)
                    e.chain_inc(e.tensor_add(qo[n % 2][:], t1[:], t2[:]), rpd)


def phase_absum_full(p, w1, w2, out):
    r1 = p.ring(4)
    red = p.sem()
    fin = p.sem()
    rst = p.ring(1)
    ro = p.ring(1)

    wbuf = [p.sb("abf_w", [128, 8192], F32) for _ in range(2)]
    rcol = p.sb("abf_r", [128, 1], F32)
    acc = p.sb("abf_acc", [128, 2], F32)
    onec = p.sb("abf_one", [128, 1], F32)
    osb = p.sb("abf_o", [2, 1], F32)
    pso = p.ps("abf_ps", [2, 1], F32)
    ones_in = p.din("ones_col_f32_ab", [128, 1], F32)

    chunks = [(w1[i * 128:(i + 1) * 128, :], 2048, 0) for i in range(64)] + \
             [(w2[i * 128:(i + 1) * 128, :], 8192, 1) for i in range(16)]

    @p.on("sync")
    def _(e):
        ro.push(e.dma_start(onec[:], ones_in))
        for i, (src, w, col) in enumerate(chunks):
            if i >= 2:
                e.wait_ge(red, i - 1)
            r1.push(e.dma_start(wbuf[i % 2][:, 0:w], src))

    @p.on("vector")
    def _(e):
        e.chain(e.memset(acc[:], 0.0))
        for i, (src, w, col) in enumerate(chunks):
            r1.wait(e, i)
            e.chain(e.reduce_sum(rcol[:], wbuf[i % 2][:, 0:w], axis=AX.X,
                                 apply_absolute_value=True))
            e.chain_inc(e.tensor_add(acc[:, col:col + 1], acc[:, col:col + 1],
                                     rcol[:]), red)
        e.nop().then_inc(red, 1)

    @p.on("tensor")
    def _(e):
        e.wait_ge(red, len(chunks) + 1)
        ro.wait_group(e)
        e.matmul(pso[:], acc[:], onec[:], start=True, stop=True
                 ).then_inc(fin, 1)

    @p.on("scalar")
    def _(e):
        e.wait_ge(fin, 1)
        e.activation(osb[:], pso[:], AF.Identity).then_inc(fin, 1)

    @p.on("gpsimd")
    def _(e):
        e.wait_ge(fin, 2)
        rst.push(e.dma_start(out, osb[:]))
        rst.wait_group(e)


def phase_ffn_thr2(p, absums, consts, cring, thr_bc, nthr_bc):
    ident = consts["ident_f32"]
    ones_row = consts["ones_row_f32"]
    ra = p.ring(1)
    ms = p.sem()

    tsb = p.sb("t2_t", [2, 1], F32)
    asb = p.sb("t2_a", [2, 1], F32)
    trow = p.sb("t2_r", [1, 2], F32)
    pst = p.ps("t2_p", [128, 2], F32)

    @p.on("sync")
    def _(e):
        ra.push(e.dma_start(asb[:], absums))

    @p.on("tensor")
    def _(e):
        cring.wait_group(e)
        e.wait_ge(ms, 1)
        e.matmul(pst[0:1, 0:2], tsb[:], ident[0:2, 0:2],
                 start=True, stop=True).then_inc(ms, 1)
        e.wait_ge(ms, 3)
        e.matmul(pst[:, 0:2], ones_row[:], trow[:],
                 start=True, stop=True).then_inc(ms, 1)

    @p.on("scalar")
    def _(e):
        ra.wait_group(e)
        e.activation(tsb[:], asb[:], AF.Identity,
                     scale=0.5 / (8192.0 * 2048.0)).then_inc(ms, 1)
        e.wait_ge(ms, 2)
        e.activation(trow[:], pst[0:1, 0:2], AF.Identity).then_inc(ms, 1)
        e.wait_ge(ms, 4)
        e.chain(e.activation(thr_bc[:], pst[:, 0:2], AF.Identity))
        e.chain_inc(e.activation(nthr_bc[:], thr_bc[:], AF.Identity,
                                 scale=-1.0), ms)


def phase_memkv_full(p, mem, g_mem, b_mem, wk, wv, log_beta, consts, cring,
                     kb_out, v_out):
    Md = 1024
    NR = 2048
    ones_cf = consts["ones_col_f32"]

    mn_d = p.dtmp("mkf_mn", [NR, Md], BF16)
    m0 = p.scope()
    phase_lnmem(p, mem, mn_d, NR, Md)
    p.barrier()
    p.pop_scope(m0)

    rgb = p.ring(1)
    grow = p.sb("mkf_gr", [1, Md], F32)
    brow = p.sb("mkf_br", [1, Md], F32)
    lbrow = p.sb("mkf_lb", [1, 8], F32)
    betar = p.sb("mkf_be", [1, 8], F32)
    beta_bc = p.sb("mkf_bbc", [128, 8], F32)

    @p.on("sync")
    def _(e):
        rgb.push(e.dma_start(grow[:], g_mem))
        rgb.push(e.dma_start(brow[:], b_mem))
        rgb.push(e.dma_start(lbrow[:], log_beta))

    @p.on("tensor")
    def _(e):
        rgb.wait_group(e, 2)
        cring.wait_group(e)
    gcol, gdn = emit_cols_from_row(p, grow, 8, ones_cf, "mkf_gc")
    bcol, bdn = emit_cols_from_row(p, brow, 8, ones_cf, "mkf_bc")

    wkv_bf = p.dtmp("mkf_wkv", [2 * Md, Md], BF16)
    rwc = p.ring(2)
    rwt = p.ring(1)
    rmt = p.ring(1)
    rstk = p.ring(4)
    rstv = p.ring(4)
    bbd = p.sem()
    gbd = p.sem()
    kmd = p.sem()
    kev = p.sem()
    vmd = p.sem()
    vev = p.sem()

    wk_fm = p.sb("mkf_wkfm", [128, 8 * Md], BF16)
    wv_fm = p.sb("mkf_wvfm", [128, 8 * Md], BF16)
    mn_dm = p.sb("mkf_mndm", [128, 8 * NR], BF16)
    kbo = [p.sb("mkf_kbo", [128, 512], BF16) for _ in range(2)]
    vo = [p.sb("mkf_vo", [128, 512], BF16) for _ in range(2)]
    ps_k = [p.ps("mkf_psk", [128, 512], F32) for _ in range(2)]
    ps_v = [p.ps("mkf_psv", [128, 512], F32) for _ in range(2)]
    ps_b = p.ps("mkf_psb", [128, 8], F32)

    NKG = 8 * 4
    NVG = 16 * 2

    @p.on("gpsimd")
    def _(e):
        rwc.push(e.dma_start(wkv_bf[0:Md, :], wk))
        rwc.push(e.dma_start(wkv_bf[Md:2 * Md, :], wv))
        for g in range(NKG):
            oc, nb = divmod(g, 4)
            e.wait_ge(kev, g + 1)
            rstk.push(e.dma_start(
                kb_out[oc * 128:(oc + 1) * 128, nb * 512:(nb + 1) * 512],
                kbo[g % 2][:]))
        for g in range(NVG):
            nc_, th = divmod(g, 2)
            e.wait_ge(vev, g + 1)
            rstv.push(e.dma_start(
                v_out[nc_ * 128:(nc_ + 1) * 128, th * 512:(th + 1) * 512],
                vo[g % 2][:]))
        rstk.wait_group(e)
        rstv.wait_group(e)

    @p.on("sync")
    def _(e):
        rwc.wait_group(e, 1)
        for c in range(8):
            rwt.push(e.dma_start(wk_fm[:, c * Md:(c + 1) * Md],
                                 wkv_bf[0:Md, c * 128:(c + 1) * 128],
                                 transpose=True))
        for c in range(8):
            rwt.push(e.dma_start(wv_fm[:, c * Md:(c + 1) * Md],
                                 wkv_bf[Md:2 * Md, c * 128:(c + 1) * 128],
                                 transpose=True))
        for c in range(8):
            rmt.push(e.dma_start(mn_dm[:, c * NR:(c + 1) * NR],
                                 mn_d[:, c * 128:(c + 1) * 128],
                                 transpose=True))

    @p.on("vector")
    def _(e):
        rmt.wait_group(e, 7)
        e.wait_ge(gdn, 2)
        e.wait_ge(bdn, 2)
        for c in range(8):
            e.chain(e.tensor_scalar(mn_dm[:, c * NR:(c + 1) * NR],
                                    mn_dm[:, c * NR:(c + 1) * NR],
                                    gcol[:, c:c + 1], bcol[:, c:c + 1],
                                    ALU.mult, ALU.add))
        e.nop().then_inc(gbd, 1)
        for g in range(NKG):
            oc = g // 4
            e.wait_ge(kmd, g + 1)
            if g >= 2:
                rstk.wait(e, g - 2)
            e.chain_inc(e.tensor_scalar(kbo[g % 2][:], ps_k[g % 2][:],
                                        beta_bc[:, oc:oc + 1], None,
                                        ALU.mult, ALU.bypass), kev)

    @p.on("scalar")
    def _(e):
        rgb.wait_group(e, 2)
        e.activation(betar[:], lbrow[:], AF.Exp).then_inc(bbd, 1)
        e.wait_ge(bbd, 2)
        e.activation(beta_bc[:], ps_b[:], AF.Identity).then_inc(bbd, 1)
        for g in range(NVG):
            e.wait_ge(vmd, g + 1)
            if g >= 2:
                rstv.wait(e, g - 2)
            e.activation(vo[g % 2][:], ps_v[g % 2][:], AF.Identity
                         ).then_inc(vev, 1)

    @p.on("tensor")
    def _(e):
        e.wait_ge(bbd, 1)
        e.matmul(ps_b[:], consts["ones_row_f32"][:], betar[:],
                 start=True, stop=True).then_inc(bbd, 1)
        rwt.wait_group(e, 15)
        e.wait_ge(gbd, 1)
        e.wait_ge(bbd, 3)
        for g in range(NKG):
            oc, nb = divmod(g, 4)
            if g >= 2:
                e.wait_ge(kev, g - 1)
            for kc in range(8):
                inst = e.matmul(
                    ps_k[g % 2][:],
                    wk_fm[:, kc * Md + oc * 128:kc * Md + (oc + 1) * 128],
                    mn_dm[:, kc * NR + nb * 512:kc * NR + (nb + 1) * 512],
                    start=(kc == 0), stop=(kc == 7))
                if kc == 7:
                    inst.then_inc(kmd, 1)
        for g in range(NVG):
            nc_, th = divmod(g, 2)
            if g >= 2:
                e.wait_ge(vev, g - 1)
            for kc in range(8):
                inst = e.matmul(
                    ps_v[g % 2][:],
                    mn_dm[:, kc * NR + nc_ * 128:kc * NR + (nc_ + 1) * 128],
                    wv_fm[:, kc * Md + th * 512:kc * Md + (th + 1) * 512],
                    start=(kc == 0), stop=(kc == 7))
                if kc == 7:
                    inst.then_inc(vmd, 1)



def build_pre():
    p = Prog()
    x_in = p.din("x", [TOK, D], F32)
    w_attn = p.din("w_attn", [D, 4608], F32)
    tabs = {t: p.din(t, [128, TOK], F32)
            for t in ("cosq", "sinq", "cosk", "sink")}
    xn_out = p.dout("xn", [TOK, D], BF16)
    q_out = p.dout("q", [2048, TOK], BF16)
    k_out = p.dout("k", [512, TOK], BF16)
    v_out = p.dout("v", [TOK, 2048], BF16)
    consts, cring = load_consts(p, ["permsign"])
    m = p.scope()
    phase_ln(p, x_in, xn_out)
    p.barrier()
    p.pop_scope(m)
    phase_qkv(p, xn_out, w_attn, tabs, consts, cring, q_out, k_out, v_out)
    return p.finalize()


def build_attn():
    p = Prog()
    q_in = p.din("q", [2048, TOK], BF16)
    k_in = p.din("k", [512, S], BF16)
    v_in = p.din("v", [S, 2048], BF16)
    ctx_out = p.dout("ctx", [8192, TOK], BF16)
    consts, cring = load_consts(p, ["ones_col_bf", "ones_row_f32"])
    phase_attn(p, q_in, k_in, v_in, consts, cring, ctx_out)
    return p.finalize()


def build_wout():
    p = Prog()
    ctx_in = p.din("ctx", [8192, TOK], BF16)
    w_out = p.din("w_out", [8192, D], F32)
    b_out = p.din("b_out", [1, D], F32)
    s_out = p.dout("s", [TOK, D], F32)
    consts, cring = load_consts(p, ["ones_row_f32"])
    phase_wout(p, ctx_in, w_out, b_out, consts, cring, s_out)
    return p.finalize()


def build_xpre2():
    p = Prog()
    x1_in = p.din("x1", [TOK, D], F32)
    g_state = p.din("g_state", [1, D], F32)
    b_state = p.din("b_state", [1, D], F32)
    wq = p.din("wq", [1024, D], F32)
    xn2_out = p.dout("xn2", [TOK, D], BF16)
    xi0_out = p.dout("xi0", [1024, TOK], BF16)
    tq = p.dtmp("tq", [TOK, D], BF16)
    consts, cring = load_consts(p, ["ones_col_f32", "ones_row_f32"])
    m = p.scope()
    phase_ln(p, x1_in, xn2_out)
    p.barrier()
    p.pop_scope(m)
    m = p.scope()
    phase_ln2_from_bf(p, xn2_out, tq)
    p.barrier()
    p.pop_scope(m)
    phase_qproj(p, tq, g_state, b_state, wq, consts, cring, xi0_out)
    return p.finalize()


def build_memkv():
    p = Prog()
    mem_s = p.din("mem", [2048 // NCORE, 1024], F32)
    g_mem = p.din("g_mem", [1, 1024], F32)
    b_mem = p.din("b_mem", [1, 1024], F32)
    wk = p.din("wk", [1024, 1024], F32)
    wv = p.din("wv", [1024, 1024], F32)
    log_beta = p.din("log_beta", [1, 8], F32)
    kb_out = p.dout("kb", [1024, 2048 // NCORE], BF16)
    v_out = p.dout("v", [2048 // NCORE, 1024], BF16)
    consts, cring = load_consts(p, ["ones_col_f32", "ones_row_f32"])
    phase_memkv(p, mem_s, g_mem, b_mem, wk, wv, log_beta, consts, cring,
                kb_out, v_out)
    return p.finalize()


def build_entmax(NSTEP=3, NH=8):
    p = Prog()
    xi0 = p.din("xi0", [1024, TOK], BF16)
    kb = p.din("kb", [1024, 2048], BF16)
    v = p.din("v", [2048, 1024], BF16)
    xi3 = p.dout("xi3", [1024, TOK], BF16)
    consts, cring = load_consts(p, ["ident"])
    phase_entmax(p, xi0, kb, v, consts, cring, xi3, NSTEP=NSTEP, NH=NH)
    return p.finalize()


def build_womoe():
    p = Prog()
    xi3 = p.din("xi3", [1024, TOK], BF16)
    wo = p.din("wo", [D, 1024], F32)
    xn2 = p.din("xn2", [TOK, D], BF16)
    gate_w = p.din("gate_w", [D, 16], F32)
    w1 = p.din("w1", [16, D, 1408], F32)
    w3 = p.din("w3", [16, D, 1408], F32)
    w2 = p.din("w2", [16, 1408, D], F32)
    x1 = p.din("x1", [TOK, D], F32)
    x2 = p.dout("x2", [TOK, D], F32)
    s2_d = p.dtmp("s2", [TOK, D], F32)
    consts, cring = load_consts(p, [])
    m = p.scope()
    phase_wo(p, xi3, wo, consts, cring, s2_d)
    p.barrier()
    p.pop_scope(m)
    phase_moe(p, xn2, gate_w, w1, w3, w2, s2_d, x1, consts, cring, x2)
    return p.finalize()


def build_absum():
    p = Prog()
    w1s = p.din("w1s", [8192 // NCORE, D], F32)
    w2s = p.din("w2s", [D // NCORE, 8192], F32)
    out = p.dout("absum", [1, 2], F32)
    phase_absum(p, w1s, w2s, out)
    return p.finalize()


def build_ffn():
    p = Prog()
    xn = p.din("xn", [TOK, D], BF16)
    w1 = p.din("w1", [8192, D], F32)
    b1 = p.din("b1", [1, 8192], F32)
    w2 = p.din("w2", [D, 8192], F32)
    b2 = p.din("b2", [1, D], F32)
    absums = p.din("absums", [8, 2], F32)
    s_in = p.din("s", [TOK, D], F32)
    x_in = p.din("x", [TOK, D], F32)
    x1_out = p.dout("x1", [TOK, D], F32)
    h_dram = p.dtmp("h", [8192, TOK], BF16)
    w2q_dram = p.dtmp("w2q", [8192, D], BF16)
    consts, cring = load_consts(
        p, ["ident", "ident_f32", "ones_row_f32", "ones_col_f32"])
    thr_bc = p.sb("thr", [128, 2], F32)
    nthr_bc = p.sb("nthr", [128, 2], F32)
    m = p.scope()
    phase_ffn_thr(p, absums, consts, cring, thr_bc, nthr_bc)
    p.barrier()
    p.pop_scope(m)
    m = p.scope()
    phase_ffn_h(p, xn, w1, b1, thr_bc, nthr_bc, consts, cring, h_dram, None)
    p.barrier()
    p.pop_scope(m)
    m = p.scope()
    phase_ffn_w2q(p, w2, thr_bc, nthr_bc, consts, w2q_dram)
    p.barrier()
    p.pop_scope(m)
    phase_ffn_out(p, h_dram, w2q_dram, b2, s_in, x_in, consts, x1_out)
    return p.finalize()


def build_fused(bases=None):
    p = Prog(bases=bases)
    x_full = p.din("x_full", [S, D], F32)
    x_own = p.din("x_own", [TOK, D], F32)
    w_attn = p.din("w_attn", [D, 4608], F32)
    tabs = {t: p.din(t, [128, TOK], F32)
            for t in ("cosq", "sinq", "cosk", "sink")}
    tabkf = {t: p.din(t + "_full", [128, S], F32) for t in ("cosk", "sink")}
    w_out = p.din("w_out", [8192, D], F32)
    b_out = p.din("b_out", [1, D], F32)
    f_w1 = p.din("f_w1", [8192, D], F32)
    f_b1 = p.din("f_b1", [1, 8192], F32)
    f_w2 = p.din("f_w2", [D, 8192], F32)
    f_b2 = p.din("f_b2", [1, D], F32)
    wq = p.din("wq", [1024, D], F32)
    g_state = p.din("g_state", [1, D], F32)
    b_state = p.din("b_state", [1, D], F32)
    mem = p.din("mem", [2048, 1024], F32)
    g_mem = p.din("g_mem", [1, 1024], F32)
    b_mem = p.din("b_mem", [1, 1024], F32)
    wk = p.din("wk", [1024, 1024], F32)
    wv = p.din("wv", [1024, 1024], F32)
    wo = p.din("wo", [D, 1024], F32)
    log_beta = p.din("log_beta", [1, 8], F32)
    gate_w = p.din("gate_w", [D, 16], F32)
    m_w1 = p.din("m_w1", [16, D, 1408], F32)
    m_w3 = p.din("m_w3", [16, D, 1408], F32)
    m_w2 = p.din("m_w2", [16, 1408, D], F32)
    x2_out = p.dout("x2", [TOK, D], F32)

    xn_own = p.dtmp("f_xnown", [TOK, D], BF16)
    xn_full = p.dtmp("f_xnfull", [S, D], BF16)
    q_d = p.dtmp("f_q", [2048, TOK], BF16)
    k_d = p.dtmp("f_k", [512, S], BF16)
    v_d = p.dtmp("f_v", [S, 2048], BF16)
    ctx_d = p.dtmp("f_ctx", [8192, TOK], BF16)
    s_d = p.dtmp("f_s", [TOK, D], F32)
    ab_d = p.dtmp("f_ab", [2, 1], F32)
    h_d = p.dtmp("f_h", [8192, TOK], BF16)
    w2q_d = p.dtmp("f_w2q", [8192, D], BF16)
    x1_d = p.dtmp("f_x1", [TOK, D], F32)
    xn2_d = p.dtmp("f_xn2", [TOK, D], BF16)
    tq_d = p.dtmp("f_tq", [TOK, D], BF16)
    xi0_d = p.dtmp("f_xi0", [1024, TOK], BF16)
    kb_d = p.dtmp("f_kb", [1024, 2048], BF16)
    v2_d = p.dtmp("f_v2", [2048, 1024], BF16)
    xi3_d = p.dtmp("f_xi3", [1024, TOK], BF16)
    s2_d = p.dtmp("f_s2", [TOK, D], F32)

    consts, cring = load_consts(
        p, ["permsign", "ident", "ident_f32", "ones_col_bf",
            "ones_row_f32", "ones_col_f32"])
    thr_bc = p.sb("f_thr", [128, 2], F32)
    nthr_bc = p.sb("f_nthr", [128, 2], F32)

    def ph(f):
        m = p.scope()
        f()
        p.barrier(reset=True)
        p.pop_scope(m)

    ph(lambda: phase_ln(p, x_own, xn_own))
    ph(lambda: phase_ln(p, x_full, xn_full, T=S))
    ph(lambda: phase_qkv(p, xn_own, w_attn, tabs, consts, cring,
                         q_d, None, None, qonly=True))
    ph(lambda: phase_kv_full(p, xn_full, w_attn, tabkf, consts, cring,
                             k_d, v_d))
    ph(lambda: phase_attn(p, q_d, k_d, v_d, consts, cring, ctx_d))
    ph(lambda: phase_wout(p, ctx_d, w_out, b_out, consts, cring, s_d))
    ph(lambda: phase_absum_full(p, f_w1, f_w2, ab_d))
    ph(lambda: phase_ffn_thr2(p, ab_d, consts, cring, thr_bc, nthr_bc))
    ph(lambda: phase_ffn_h(p, xn_own, f_w1, f_b1, thr_bc, nthr_bc, consts,
                           cring, h_d, None))
    ph(lambda: phase_ffn_w2q(p, f_w2, thr_bc, nthr_bc, consts, w2q_d))
    ph(lambda: phase_ffn_out(p, h_d, w2q_d, f_b2, s_d, x_own, consts, x1_d))
    ph(lambda: phase_ln(p, x1_d, xn2_d))
    ph(lambda: phase_ln2_from_bf(p, xn2_d, tq_d))
    ph(lambda: phase_qproj(p, tq_d, g_state, b_state, wq, consts, cring,
                           xi0_d))
    ph(lambda: phase_memkv_full(p, mem, g_mem, b_mem, wk, wv, log_beta,
                                consts, cring, kb_d, v2_d))
    ph(lambda: phase_entmax(p, xi0_d, kb_d, v2_d, consts, cring, xi3_d))
    ph(lambda: phase_wo(p, xi3_d, wo, consts, cring, s2_d))
    m = p.scope()
    phase_moe(p, xn2_d, gate_w, m_w1, m_w3, m_w2, s2_d, x1_d, consts, cring,
              x2_out)
    nc = p.finalize()
    return nc, p.counts


def build_fused_2pass():
    _, counts = build_fused()
    names = {nm for (_, nm) in counts}
    maxep = max((ep for (ep, _) in counts), default=0)
    bases = {}
    for nm in names:
        run = 0
        for ep in range(maxep + 2):
            bases[(ep, nm)] = run
            run += counts.get((ep, nm), 0)
    nc, _ = build_fused(bases)
    return nc


def kernel_fused(**inputs):
    CORES = list(range(NCORE))
    consts = host_constants()
    x = np.ascontiguousarray(inputs["L0_attn_x"].reshape(S, D))
    nc = _get("fused", build_fused_2pass)
    im = []
    for c in CORES:
        d = {
            "x_full": x,
            "x_own": x[c * TOK:(c + 1) * TOK],
            "w_attn": np.ascontiguousarray(inputs["L0_attn_W_attn"]),
            "w_out": np.ascontiguousarray(inputs["L0_attn_W_out"]),
            "b_out": np.ascontiguousarray(
                inputs["L0_attn_b_out"].reshape(1, D)),
            "f_w1": np.ascontiguousarray(inputs["L0_ffn_w1"]),
            "f_b1": np.ascontiguousarray(inputs["L0_ffn_b1"].reshape(1, 8192)),
            "f_w2": np.ascontiguousarray(inputs["L0_ffn_w2"]),
            "f_b2": np.ascontiguousarray(inputs["L0_ffn_b2"].reshape(1, D)),
            "wq": np.ascontiguousarray(inputs["L1_attn_Wq"]),
            "g_state": np.ascontiguousarray(
                inputs["L1_attn_g_state"].reshape(1, D)),
            "b_state": np.ascontiguousarray(
                inputs["L1_attn_b_state"].reshape(1, D)),
            "mem": np.ascontiguousarray(inputs["L1_attn_memory"]),
            "g_mem": np.ascontiguousarray(
                inputs["L1_attn_g_mem"].reshape(1, 1024)),
            "b_mem": np.ascontiguousarray(
                inputs["L1_attn_b_mem"].reshape(1, 1024)),
            "wk": np.ascontiguousarray(inputs["L1_attn_Wk"]),
            "wv": np.ascontiguousarray(inputs["L1_attn_Wv"]),
            "wo": np.ascontiguousarray(inputs["L1_attn_Wo"]),
            "log_beta": np.ascontiguousarray(
                inputs["L1_attn_log_beta"].reshape(1, 8)),
            "gate_w": np.ascontiguousarray(inputs["L1_ffn_gate_w"]),
            "m_w1": np.ascontiguousarray(inputs["L1_ffn_w1"]),
            "m_w3": np.ascontiguousarray(inputs["L1_ffn_w3"]),
            "m_w2": np.ascontiguousarray(inputs["L1_ffn_w2"]),
            "ones_col_f32_ab": consts["ones_col_f32"],
            "cosk_full": consts["cosk"],
            "sink_full": consts["sink"],
        }
        for t in ("cosq", "sinq", "cosk", "sink"):
            d[t] = np.ascontiguousarray(
                consts[t][:, c * TOK:(c + 1) * TOK])
        for w in ("permsign", "ident", "ident_f32", "ones_col_bf",
                  "ones_row_f32", "ones_col_f32"):
            d[w] = consts[w]
        im.append(d)
    res = run_bass_kernel_spmd(nc, im, core_ids=CORES).results
    x2 = np.concatenate([r["x2"] for r in res], 0)
    return x2.reshape(1, S, D)



_BUILT = {}


def _get(name, fn):
    if name not in _BUILT:
        _BUILT[name] = fn()
    return _BUILT[name]


def kernel(**inputs):
    return kernel_fused(**inputs)


def kernel_unfused(**inputs):
    CORES = list(range(NCORE))
    consts = host_constants()

    x = np.ascontiguousarray(inputs["L0_attn_x"].reshape(S, D))
    w_attn = np.ascontiguousarray(inputs["L0_attn_W_attn"])
    w_out = np.ascontiguousarray(inputs["L0_attn_W_out"])
    b_out = np.ascontiguousarray(inputs["L0_attn_b_out"].reshape(1, D))
    f_w1 = np.ascontiguousarray(inputs["L0_ffn_w1"])
    f_b1 = np.ascontiguousarray(inputs["L0_ffn_b1"].reshape(1, 8192))
    f_w2 = np.ascontiguousarray(inputs["L0_ffn_w2"])
    f_b2 = np.ascontiguousarray(inputs["L0_ffn_b2"].reshape(1, D))
    mem = np.ascontiguousarray(inputs["L1_attn_memory"])
    wq = np.ascontiguousarray(inputs["L1_attn_Wq"])
    wk = np.ascontiguousarray(inputs["L1_attn_Wk"])
    wv = np.ascontiguousarray(inputs["L1_attn_Wv"])
    wo = np.ascontiguousarray(inputs["L1_attn_Wo"])
    log_beta = np.ascontiguousarray(inputs["L1_attn_log_beta"].reshape(1, 8))
    g_state = np.ascontiguousarray(inputs["L1_attn_g_state"].reshape(1, D))
    b_state = np.ascontiguousarray(inputs["L1_attn_b_state"].reshape(1, D))
    g_mem = np.ascontiguousarray(inputs["L1_attn_g_mem"].reshape(1, 1024))
    b_mem = np.ascontiguousarray(inputs["L1_attn_b_mem"].reshape(1, 1024))
    gate_w = np.ascontiguousarray(inputs["L1_ffn_gate_w"])
    m_w1 = np.ascontiguousarray(inputs["L1_ffn_w1"])
    m_w3 = np.ascontiguousarray(inputs["L1_ffn_w3"])
    m_w2 = np.ascontiguousarray(inputs["L1_ffn_w2"])

    def sl(c):
        return slice(c * TOK, (c + 1) * TOK)

    nc = _get("pre", build_pre)
    im = []
    for c in CORES:
        im.append({"x": x[sl(c)], "w_attn": w_attn,
                   "permsign": consts["permsign"],
                   **{t: np.ascontiguousarray(consts[t][:, sl(c)])
                      for t in ("cosq", "sinq", "cosk", "sink")}})
    res = run_bass_kernel_spmd(nc, im, core_ids=CORES).results
    xn_all = [r["xn"] for r in res]
    q_all = [r["q"] for r in res]
    k_full = np.ascontiguousarray(np.concatenate([r["k"] for r in res], 1))
    v_full = np.ascontiguousarray(np.concatenate([r["v"] for r in res], 0))

    nc = _get("attn", build_attn)
    im = [{"q": q_all[c], "k": k_full, "v": v_full,
           "ones_col_bf": consts["ones_col_bf"],
           "ones_row_f32": consts["ones_row_f32"]} for c in CORES]
    res = run_bass_kernel_spmd(nc, im, core_ids=CORES).results
    ctx_all = [r["ctx"] for r in res]

    nc = _get("wout", build_wout)
    im = [{"ctx": ctx_all[c], "w_out": w_out, "b_out": b_out,
           "ones_row_f32": consts["ones_row_f32"]} for c in CORES]
    res = run_bass_kernel_spmd(nc, im, core_ids=CORES).results
    s_all = [r["s"] for r in res]

    nc = _get("absum", build_absum)
    im = [{"w1s": f_w1[c * 1024:(c + 1) * 1024],
           "w2s": f_w2[c * 256:(c + 1) * 256],
           "ones_col_f32_ab": consts["ones_col_f32"]} for c in CORES]
    res = run_bass_kernel_spmd(nc, im, core_ids=CORES).results
    absums = np.ascontiguousarray(
        np.concatenate([r["absum"] for r in res], 0))

    nc = _get("ffn", build_ffn)
    im = [{"xn": xn_all[c], "w1": f_w1, "b1": f_b1, "w2": f_w2, "b2": f_b2,
           "absums": absums, "s": s_all[c], "x": x[sl(c)],
           "ident": consts["ident"], "ident_f32": consts["ident_f32"],
           "ones_row_f32": consts["ones_row_f32"],
           "ones_col_f32": consts["ones_col_f32"]} for c in CORES]
    res = run_bass_kernel_spmd(nc, im, core_ids=CORES).results
    x1_all = [r["x1"] for r in res]

    nc = _get("xpre2", build_xpre2)
    im = [{"x1": x1_all[c], "g_state": g_state, "b_state": b_state,
           "wq": wq, "ones_col_f32": consts["ones_col_f32"],
           "ones_row_f32": consts["ones_row_f32"]} for c in CORES]
    res = run_bass_kernel_spmd(nc, im, core_ids=CORES).results
    xn2_all = [r["xn2"] for r in res]
    xi0_all = [r["xi0"] for r in res]

    nc = _get("memkv", build_memkv)
    NR = 2048 // NCORE
    im = [{"mem": mem[c * NR:(c + 1) * NR], "g_mem": g_mem, "b_mem": b_mem,
           "wk": wk, "wv": wv, "log_beta": log_beta,
           "ones_col_f32": consts["ones_col_f32"],
           "ones_row_f32": consts["ones_row_f32"]} for c in CORES]
    res = run_bass_kernel_spmd(nc, im, core_ids=CORES).results
    kb_full = np.ascontiguousarray(np.concatenate([r["kb"] for r in res], 1))
    v2_full = np.ascontiguousarray(np.concatenate([r["v"] for r in res], 0))

    nc = _get("entmax", build_entmax)
    im = [{"xi0": xi0_all[c], "kb": kb_full, "v": v2_full,
           "ident": consts["ident"]} for c in CORES]
    res = run_bass_kernel_spmd(nc, im, core_ids=CORES).results
    xi3_all = [r["xi3"] for r in res]

    nc = _get("womoe", build_womoe)
    im = [{"xi3": xi3_all[c], "wo": wo, "xn2": xn2_all[c],
           "gate_w": gate_w, "w1": m_w1, "w3": m_w3, "w2": m_w2,
           "x1": x1_all[c]} for c in CORES]
    res = run_bass_kernel_spmd(nc, im, core_ids=CORES).results
    x2 = np.concatenate([r["x2"] for r in res], 0)
    return x2.reshape(1, S, D)
